# version 10
# baseline (speedup 1.0000x reference)
import numpy as np
from contextlib import ExitStack

import concourse.bass as bass
import concourse.mybir as mybir
from concourse.bass_utils import run_bass_kernel_spmd

F32 = mybir.dt.float32
BF16 = mybir.dt.bfloat16
AF = mybir.ActivationFunctionType
ALU = mybir.AluOpType
AX = mybir.AxisListType

ENGINES = ("sync", "scalar", "vector", "tensor", "gpsimd")
DEBUG_TAPS = False
STOP_AT = None
NOPREF = False

C_ZERO, C_LNEPS, C_CLIPEPS, C_LR2, C_N2LR, C_NLR, C_INV1024, C_QUART = range(8)
C_ONE, C_FHMASK, C_XTRAMASK, C_TWO = 8, 9, 10, 11


class Sem:
    def __init__(self, handle, name):
        self.h, self.name, self.n = handle, name, 0


class Sched:
    def __init__(self):
        self.q = {e: [] for e in ENGINES}
        self.trace = {e: [] for e in ENGINES}

    def op(self, eng, fn, inc=None, dv=1):
        if inc is None:
            self.q[eng].append(fn)
            return None
        def run(e, fn=fn, inc=inc, dv=dv):
            fn(e).then_inc(inc.h, dv)
        self.q[eng].append(run)
        inc.n += dv
        self.trace[eng].append(("i", inc, dv))
        return (inc, inc.n)

    def wait(self, eng, edge):
        if edge is None:
            return
        sem, v = edge
        self.q[eng].append(lambda e, sem=sem, v=v: e.wait_ge(sem.h, v))
        self.trace[eng].append(("w", sem, v))

    def check(self):
        pos = {e: 0 for e in ENGINES}
        vals = {}
        progressed = True
        while progressed:
            progressed = False
            for e in ENGINES:
                while pos[e] < len(self.trace[e]):
                    kind, sem, v = self.trace[e][pos[e]]
                    if kind == "i":
                        vals[sem.name] = vals.get(sem.name, 0) + v
                        pos[e] += 1
                        progressed = True
                    elif vals.get(sem.name, 0) >= v:
                        pos[e] += 1
                        progressed = True
                    else:
                        break
        stuck = {e: self.trace[e][pos[e]] for e in ENGINES if pos[e] < len(self.trace[e])}
        if stuck:
            raise RuntimeError("deadlock: " + str({e: (k, s.name, v) for e, (k, s, v) in stuck.items()}))


class Banks:
    def __init__(self, n=8):
        self.pool = [[i, None] for i in range(n)]

    def get(self):
        i, e = self.pool.pop(0)
        return i, e

    def put(self, i, e):
        assert i not in [x[0] for x in self.pool], f"double put of bank {i}"
        self.pool.append([i, e])


def build_program():
    nc = bass.Bass("TRN2", target_bir_lowering=False, detect_race_conditions=False)

    dt = {}
    def din(name, shape):
        dt[name] = nc.dram_tensor(name, list(shape), F32, kind="ExternalInput").ap()
    din("xo", (1024, 2048)); din("xxt", (1024, 2)); din("xf", (1024, 2048))
    din("wbT", (1024, 1024)); din("wpT", (1024, 1024)); din("wmod", (1024, 6144))
    din("cfm", (128, 8)); din("gam", (128, 8)); din("bet", (128, 8))
    din("initBT", (1024, 16)); din("initA", (1024, 16)); din("initAT", (16, 1024))
    din("initB", (16, 1024)); din("wr", (1024, 8))
    dt["sel"] = nc.dram_tensor("sel", [8, 1024], BF16, kind="ExternalInput").ap()
    din("ones", (128, 128)); din("eye", (128, 128)); din("consts", (128, 16))
    din("wex1", (9, 1024, 512)); din("wex2", (9, 512, 1024))
    dt["onesb"] = nc.dram_tensor("onesb", [128, 128], BF16, kind="ExternalInput").ap()
    out_t = nc.dram_tensor("out", [1024, 2048], F32, kind="ExternalOutput").ap()
    dbg_t = nc.dram_tensor("dbg", [1024, 2048], F32, kind="ExternalOutput").ap() if DEBUG_TAPS else None
    dbg2_t = nc.dram_tensor("dbg2", [8, 8, 256], F32, kind="ExternalOutput").ap() if DEBUG_TAPS else None
    dbg3_t = nc.dram_tensor("dbg3", [8, 128, 4], F32, kind="ExternalOutput").ap() if DEBUG_TAPS else None
    dbg4_t = nc.dram_tensor("dbg4", [128, 96], F32, kind="ExternalOutput").ap() if DEBUG_TAPS else None
    x2sp = nc.dram_tensor("x2sp", [1024, 2048], F32,
                          kind="ExternalOutput" if DEBUG_TAPS else "Internal").ap()
    w1s = nc.dram_tensor("w1s", [9, 2, 1024, 512], BF16).ap()
    w2s = nc.dram_tensor("w2s", [9, 2, 512, 1024], BF16).ap()
    wbs = nc.dram_tensor("wbs", [2, 2, 1024, 512], BF16).ap()

    fm = lambda n: dt[n].rearrange("(kc p) t -> p kc t", p=128)

    st = ExitStack()
    with st:
        def sb(name, shape):
            return st.enter_context(nc.sbuf_tensor(name, shape, F32))

        wslab = sb("wslab", [128, 16384])
        xsl = sb("xsl", [128, 2, 8, 256])
        xcol = sb("xcol", [128, 8])
        xnp_w = sb("xnp", [128, 2, 2048])
        hp_w = sb("hp", [128, 2, 2048])
        btp_w = sb("btp", [128, 128])
        ibtp_w = sb("ibtp", [128, 128])
        hsl = sb("hsl", [128, 2048])
        vsl = sb("vsl", [128, 2048])
        vctm = sb("vctm", [128, 2, 1024])
        ztm = sb("ztm", [128, 2, 1024])
        x2sl = sb("x2sl", [128, 2, 8, 256])
        sq = sb("sq", [128, 2, 256])
        stats = sb("stats", [128, 4, 256])
        modv = sb("modv", [128, 48])
        dmod = modv
        sfm = sb("sfm", [128, 8]); cfmv = sb("cfmv", [128, 8])
        gamv = sb("gamv", [128, 8]); betv = sb("betv", [128, 8])
        conv = sb("conv", [128, 16])
        onesv = sb("onesv", [128, 128]); eyev = sb("eyev", [128, 128])
        initAv = sb("initAv", [128, 8, 16])
        selv = st.enter_context(nc.sbuf_tensor("selv", [8, 1024], BF16))
        accA = st.enter_context(nc.sbuf_tensor("accA", [16, 1024], F32))
        accB = st.enter_context(nc.sbuf_tensor("accB", [16, 1024], F32))
        iA2T = st.enter_context(nc.sbuf_tensor("iA2T", [16, 1024], F32))
        iBv = st.enter_context(nc.sbuf_tensor("iBv", [16, 1024], F32))
        a2ef = st.enter_context(nc.sbuf_tensor("a2ef", [16, 1024], F32))
        beff = st.enter_context(nc.sbuf_tensor("beff", [16, 1024], F32))
        prji = st.enter_context(nc.sbuf_tensor("prji", [16, 256], F32))
        prje = st.enter_context(nc.sbuf_tensor("prje", [16, 256], F32))
        ptmi = sb("ptmi", [128, 2, 16])
        ptmep_w = sb("ptmep", [128, 32])
        ptmef = sb("ptmef", [128, 32])
        btf_w = sb("btf", [128, 128])
        ptmep = lambda hl: ptmep_w[:, hl * 16:(hl + 1) * 16].bitcast(BF16)
        eyeb = st.enter_context(nc.sbuf_tensor("eyeb", [128, 128], BF16))
        mids = st.enter_context(nc.sbuf_tensor("mids", [16, 256], F32))
        c16 = st.enter_context(nc.sbuf_tensor("c16", [16, 8], F32))
        lgfm = st.enter_context(nc.sbuf_tensor("lgfm", [8, 256], F32))
        prfm = st.enter_context(nc.sbuf_tensor("prfm", [8, 256], BF16))
        lgtm = sb("lgtm", [128, 2, 8])
        rtr = sb("rtr", [128, 2, 10])
        m1c = sb("m1c", [128, 2]); m2c = sb("m2c", [128, 2]); nm1c = sb("nm1c", [128, 2])
        dvc = sb("dvc", [128, 2]); g1c = sb("g1c", [128, 2])
        eqc = sb("eqc", [128, 2, 8]); prtm = sb("prtm", [128, 2, 8])
        maskv = sb("maskv", [128, 2, 256])
        maskb = st.enter_context(nc.sbuf_tensor("maskb", [128, 2, 256], BF16))
        onebv = st.enter_context(nc.sbuf_tensor("onebv", [128, 128], BF16))
        wrp = st.enter_context(nc.sbuf_tensor("wrp", [128, 2, 8, 8], BF16))
        ps = [st.enter_context(nc.psum_tensor(f"ps{i}", [128, 512], F32)) for i in range(8)]

        sems = {n: Sem(st.enter_context(nc.semaphore(n)), n)
                for n in ("dW", "dX", "dG", "dP", "dS", "dT", "dE", "pe", "dv", "ac", "gp")}
        dW, dX, dG = sems["dW"], sems["dX"], sems["dG"]
        dP, dS, dT, dE = sems["dP"], sems["dS"], sems["dT"], sems["dE"]
        spe, sdv, sac, sgp = sems["pe"], sems["dv"], sems["ac"], sems["gp"]

        S = Sched()
        BK = Banks()

        def finish():
            S.check()
            blk = st.enter_context(nc.Block())
            for ename in ENGINES:
                section = getattr(blk, ename)
                def emit(eng, _q=tuple(S.q[ename])):
                    for item in _q:
                        item(eng)
                section(emit)
            return nc

        def dma(qeng, sem, out_ap, in_ap, waits=(), slow=False):
            for ed in waits:
                S.wait(qeng, ed)
            def fn(e, o=out_ap, i=in_ap, slow=slow):
                return e.dma_start(out=o, in_=i, allow_slow_non_contiguous=True) if slow \
                    else e.dma_start(out=o, in_=i)
            return S.op(qeng, fn, inc=sem, dv=16)

        def vop(fn, eng="vector", inc=None):
            return S.op(eng, fn, inc=(inc or {"vector": sdv, "scalar": sac,
                                             "gpsimd": sgp}[eng]))

        def vc(fn, eng="vector"):
            ed = vop(fn, eng)
            S.wait(eng, ed)
            return ed

        wpf = wslab[:, 0:8192]
        wppf = lambda hl: wslab[:, 8192 + hl * 4096:8192 + (hl + 1) * 4096].bitcast(BF16)
        wpp = lambda hl: wslab[:, 8192 + hl * 4096:8192 + (hl + 1) * 4096]             .bitcast(BF16).rearrange("p (k f) -> p k f", k=8)
        wbp = lambda hl: wslab[:, hl * 4096:(hl + 1) * 4096]             .bitcast(BF16).rearrange("p (k f) -> p k f", k=8)
        xnp = lambda sl, hl: xnp_w[:, sl, hl * 1024:(hl + 1) * 1024]             .bitcast(BF16).rearrange("p (k t) -> p k t", k=8)
        hpb = lambda sl, hl: hp_w[:, sl, hl * 1024:(hl + 1) * 1024].bitcast(BF16)
        hpv = lambda sl, hl: hp_w[:, sl, hl * 1024:(hl + 1) * 1024]             .bitcast(BF16).rearrange("p (k t) -> p k t", k=8)
        btpf = lambda hl: btp_w[:, hl * 64:(hl + 1) * 64].bitcast(BF16)
        btp = lambda hl: btp_w[:, hl * 64:(hl + 1) * 64]             .bitcast(BF16).rearrange("p (k r) -> p k r", k=8)
        ibtpf = lambda hl: ibtp_w[:, hl * 64:(hl + 1) * 64].bitcast(BF16)
        ibtp = lambda hl: ibtp_w[:, hl * 64:(hl + 1) * 64]             .bitcast(BF16).rearrange("p (k r) -> p k r", k=8)
        w1v = lambda b: wslab[:, b * 8192:b * 8192 + 4096].rearrange("p (k f) -> p k f", k=8)
        w2v = lambda b: wslab[:, b * 8192 + 4096:(b + 1) * 8192].rearrange("p (f h) -> p f h", f=4)
        w1p = lambda b, pl: wslab[:, b * 8192 + pl * 2048: b * 8192 + (pl + 1) * 2048]             .bitcast(BF16).rearrange("p (k f) -> p k f", k=8)
        w2p = lambda b, pl: wslab[:, b * 8192 + 4096 + pl * 2048: b * 8192 + 4096 + (pl + 1) * 2048]             .bitcast(BF16).rearrange("p (f h) -> p f h", f=4)
        h2p = lambda sl, pl: (vctm if pl == 0 else ztm)[:, sl, :]             .bitcast(BF16).rearrange("p (k t) -> p k t", k=8)
        hep = lambda pl: vsl[:, 1024:2048].bitcast(BF16)             .rearrange("p (f two t) -> p f two t", f=4, two=2)[:, :, pl, :]
        wstg = x2sl.ap().rearrange("p a b c -> p (a b c)")
        ppair = lambda half, pl: wslab[:, half * 4096 + pl * 2048:(half * 4096) + (pl + 1) * 2048]             .bitcast(BF16)
        wmv = lambda b: (vctm if b == 0 else ztm).ap().rearrange(
            "p a b -> p (a b)")[:, 0:2048].rearrange("p (k t) -> p k t", k=8)
        hcv = lambda sl: hsl.ap().rearrange("p (k t) -> p k t", k=8)
        vv = vsl.ap().rearrange("p (k t) -> p k t", k=8)
        xc3 = xcol.ap().rearrange("p (a b) -> p a b", b=1)
        ztm_p = lambda ct, hl: ztm[:, ct, hl * 512:(hl + 1) * 512].bitcast(BF16)
        s16 = vsl[:, 0:128].rearrange("p (k r) -> p k r", k=8)
        wrv = vsl[:, 144:208].rearrange("p (a b) -> p a b", a=8)
        initBTv = x2sl.ap().rearrange("p a b c -> p (a b c)")[:, 0:128]             .rearrange("p (k r) -> p k r", k=8)
        hem = lambda half: vsl[:, half * 1024:half * 1024 + 1024].rearrange("p (f t) -> p f t", f=4)
        mflat = maskv.ap().rearrange("p a b -> p (a b)")
        hx3 = mflat[:, 0:16].rearrange("p (k t) -> p k t", k=8)
        xnx3 = mflat[:, 16:32].rearrange("p (k t) -> p k t", k=8)
        xx3 = mflat[:, 32:48].rearrange("p (k t) -> p k t", k=8)

        e_sm = {}
        for name, dst in (("consts", conv), ("ones", onesv), ("eye", eyev),
                          ("cfm", cfmv), ("gam", gamv), ("bet", betv)):
            e_sm[name] = dma("sync", dW, dst[:], dt[name])
        e_ibt = dma("sync", dW, initBTv[:], dt["initBT"].rearrange("(kc p) r -> p kc r", p=128))
        e_ia = dma("sync", dW, initAv[:], dt["initA"].rearrange("(kc p) r -> p kc r", p=128))
        e_iat = dma("sync", dW, iA2T[:], dt["initAT"])
        e_ib = dma("sync", dW, iBv[:], dt["initB"])
        e_wr = dma("sync", dW, wrv[:], dt["wr"].rearrange("(kc p) e -> p kc e", p=128))
        e_onb = dma("sync", dW, onebv[:], dt["onesb"])
        e_sel = dma("sync", dW, selv[:], dt["sel"])
        e_wm = {}
        def load_wm(jg, waits=()):
            e_wm[jg] = dma("sync", dW, wmv(jg % 2)[:],
                           fm("wmod")[:, :, jg * 256:(jg + 1) * 256], waits=waits)
        load_wm(0)
        load_wm(1)

        x_edge = {}
        def load_x(idx, waits=()):
            src = fm("xf") if idx < 8 else fm("xo")
            c = idx % 8
            x_edge[idx] = dma("scalar", dX, xsl[:, idx % 2, :, :],
                              src[:, :, c * 256:(c + 1) * 256], waits=waits)
        load_x(0)
        load_x(1)
        e_wpt0 = dma("scalar", dP, wpf.rearrange("p (k h) -> p k h", k=8), fm("wpT"))
        e_xxt = dma("scalar", dT, xx3[:],
                    dt["xxt"].rearrange("(kc p) t -> p kc t", p=128), slow=True)

        S.wait("scalar", e_sm["cfm"])
        e_silu = vop(lambda e: e.activation(out=sfm[:], in_=cfmv[:], func=AF.Silu), "scalar")
        S.wait("vector", e_silu)
        for k in range(8):
            e_s16 = vc(lambda e, k=k: e.tensor_copy(s16[:, k, :],
                                                    bass.AP(sfm, k, [[8, 128], [0, 16]])))

        S.wait("vector", e_sm["eye"])
        e_eyeb = vc(lambda e: e.tensor_copy(eyeb[:], eyev[:]))
        S.wait("vector", e_wpt0)
        vc(lambda e: e.tensor_copy(wppf(0), wpf))
        e_wpt = vc(lambda e: e.tensor_tensor(wppf(1), wpf, wppf(0), ALU.subtract))
        S.wait("vector", e_ibt)
        ibtf = x2sl.ap().rearrange("p a b c -> p (a b c)")[:, 0:128]
        vc(lambda e: e.tensor_copy(ibtpf(0), ibtf))
        e_ibp = vc(lambda e: e.tensor_tensor(ibtpf(1), ibtf, ibtpf(0), ALU.subtract))

        ps_state = {"w1_store": None, "w2_store": None, "split_done": None,
                    "last_store": None, "last_load": None}
        PS_ITEMS = [((dt["wex1"][ei], w1s[ei], 0), (dt["wex2"][ei], w2s[ei], 1))
                    for ei in range(9)]
        PS_ITEMS.append(((dt["wbT"][:, 0:512], wbs[0], 0),
                         (dt["wbT"][:, 512:1024], wbs[1], 0)))
        PAIR_SRC = (wslab[:, 0:4096].bitcast(BF16).rearrange("p (pl k f) -> p pl k f", pl=2, k=8),
                    wslab[:, 4096:8192].bitcast(BF16).rearrange("p (pl f h) -> p pl f h", pl=2, f=4))

        def presplit_half(srcd, dstd, mode, qeng):
            stg3 = x2sl.ap().rearrange("p a b c -> p (a b c)").rearrange(
                "p (k f) -> p k f", k=(8 if mode == 0 else 4))
            e_ld = dma(qeng, dS, stg3,
                       srcd.rearrange("(kc p) f -> p kc f", p=128),
                       waits=[w for w in (ps_state["split_done"],) if w])
            ps_state["last_load"] = e_ld
            S.wait("vector", e_ld)
            S.wait("vector", ps_state["w1_store" if mode == 0 else "w2_store"])
            vc(lambda e, mode=mode: e.tensor_copy(ppair(mode, 0), wstg))
            e_sp = vc(lambda e, mode=mode: e.tensor_tensor(ppair(mode, 1), wstg,
                                                           ppair(mode, 0), ALU.subtract))
            ps_state["split_done"] = e_sp
            e_st = dma("gpsimd", dE, dstd.rearrange("pl (kc p) f -> p pl kc f", p=128),
                       PAIR_SRC[mode], waits=(e_sp,))
            ps_state["w1_store" if mode == 0 else "w2_store"] = e_st
            ps_state["last_store"] = e_st

        def presplit_item(item, qeng):
            for srcd, dstd, mode in item:
                presplit_half(srcd, dstd, mode, qeng)

        ps_state["split_done"] = e_ibp
        ps_state["w1_store"] = e_wpt
        ps_state["w2_store"] = e_wpt
        P0H = [half for ei in (5, 6, 7, 8) for half in PS_ITEMS[ei]]

        S.wait("tensor", e_s16)
        S.wait("tensor", e_sm["eye"])
        S.wait("tensor", e_sm["ones"])
        wm_pe = {}
        for jg in range(24):
            S.wait("tensor", e_wm[jg])
            bk, fe = BK.get()
            S.wait("tensor", fe)
            for k in range(8):
                fn = (lambda e, k=k, jg=jg, bk=bk: e.matmul(
                    ps[bk][0:16, 0:256], s16[:, k, :], wmv(jg % 2)[:, k, :],
                    start=(k == 0), stop=(k == 7)))
                e_mm = S.op("tensor", fn, inc=spe) if k == 7 else S.op("tensor", fn)
            wm_pe[jg] = e_mm
            if jg + 2 < 24:
                load_wm(jg + 2, waits=(e_mm,))
            S.wait("vector", e_mm)
            e_ev = vop(lambda e, bk=bk: e.tensor_copy(prji[:], ps[bk][0:16, 0:256]))
            BK.put(bk, e_ev)
            bk2, fe2 = BK.get()
            S.wait("tensor", fe2)
            S.wait("tensor", e_ev)
            for t2 in range(2):
                fn = (lambda e, bk2=bk2, t2=t2: e.transpose(
                    ps[bk2][:, t2 * 16:(t2 + 1) * 16], prji[:, t2 * 128:(t2 + 1) * 128],
                    eyev[0:16, 0:16]))
                e_tr = S.op("tensor", fn, inc=spe) if t2 == 1 else S.op("tensor", fn)
            S.wait("vector", e_tr)
            e_mod = vop(lambda e, bk2=bk2, jg=jg: e.tensor_copy(
                modv[:, jg * 2:(jg + 1) * 2], bass.AP(ps[bk2], 0, [[512, 128], [16, 2]])))
            BK.put(bk2, e_mod)

        e_wmod_pe = (spe, spe.n)
        e_wbt = None

        S.wait("vector", e_mod)
        vc(lambda e: e.tensor_scalar_add(dmod[:, 8:16], modv[:, 8:16], 1.0))
        vc(lambda e: e.tensor_scalar_add(dmod[:, 32:40], modv[:, 32:40], 1.0))
        for src_c in (16, 40):
            vc(lambda e, s=src_c: e.tensor_single_scalar(rtr[:, 0, 0:8], modv[:, s:s + 8], 0.0, ALU.is_ge))
            vc(lambda e: e.tensor_scalar(out=rtr[:, 0, 0:8], in0=rtr[:, 0, 0:8], scalar1=2.0,
                                         scalar2=-1.0, op0=ALU.mult, op1=ALU.add))
            vc(lambda e, s=src_c: e.tensor_tensor(rtr[:, 1, 0:8], modv[:, s:s + 8], rtr[:, 0, 0:8], ALU.mult))
            vc(lambda e: e.tensor_scalar_max(rtr[:, 1, 0:8], rtr[:, 1, 0:8], 0.25))
            e_dmod = vc(lambda e, s=src_c: e.tensor_tensor(dmod[:, s:s + 8], rtr[:, 1, 0:8],
                                                           rtr[:, 0, 0:8], ALU.mult))
        if DEBUG_TAPS:
            dma("gpsimd", dG, dbg4_t[:, 0:48], modv[:], waits=(e_dmod,))
            dma("gpsimd", dG, dbg4_t[:, 48:96], dmod[:], waits=(e_dmod,))
        S.wait("vector", e_wr)
        vc(lambda e: e.tensor_copy(wrp[:, 0, :, :].rearrange("p a b -> p (a b)"),
                                   wrv[:].rearrange("p a b -> p (a b)")))
        e_wrp = vc(lambda e: e.tensor_tensor(wrp[:, 1, :, :].rearrange("p a b -> p (a b)"),
                                             wrv[:].rearrange("p a b -> p (a b)"),
                                             wrp[:, 0, :, :].rearrange("p a b -> p (a b)"), ALU.subtract))
        S.wait("vector", e_iat)
        e_ia2 = vc(lambda e: e.tensor_scalar_mul(iA2T[:], iA2T[:], 2.0))
        S.wait("vector", e_ib)
        vop(lambda e: e.memset(accA[:], 0.0))
        e_zacc = vop(lambda e: e.memset(accB[:], 0.0))

        state = {"sq_user": e_zacc}

        def ln_stats(src_view, W, deps=(), pair=None):
            bk1, f1 = BK.get()
            bk2, f2 = BK.get()
            for ed in deps:
                S.wait("tensor", ed)
                S.wait("scalar", ed)
                S.wait("vector", ed)
            S.wait("tensor", f1)
            S.wait("tensor", f2)
            if pair is None:
                for k in range(8):
                    fn = (lambda e, k=k, bk1=bk1: e.matmul(ps[bk1][:, 0:W], onesv[:], src_view(k),
                                                           start=(k == 0), stop=(k == 7)))
                    e_s1 = S.op("tensor", fn, inc=spe) if k == 7 else S.op("tensor", fn)
            else:
                S.wait("tensor", e_onb)
                for k in range(8):
                    for hl in range(2):
                        fn = (lambda e, k=k, hl=hl, bk1=bk1: e.matmul(
                            ps[bk1][:, 0:W], onebv[:], pair(k, hl),
                            start=(k == 0 and hl == 0), stop=(k == 7 and hl == 1)))
                        e_s1 = S.op("tensor", fn, inc=spe) if (k == 7 and hl == 1) \
                            else S.op("tensor", fn)
            S.wait("scalar", state["sq_user"])
            sq_pe = []
            for k in range(8):
                if k >= 2:
                    S.wait("scalar", sq_pe[k - 2])
                e_sq = vop(lambda e, k=k: e.activation(out=sq[:, k % 2, 0:W], in_=src_view(k),
                                                       func=AF.Square), "scalar")
                S.wait("tensor", e_sq)
                fn = (lambda e, k=k, bk2=bk2: e.matmul(ps[bk2][:, 0:W], onesv[:], sq[:, k % 2, 0:W],
                                                       start=(k == 0), stop=(k == 7)))
                ed = S.op("tensor", fn, inc=spe)
                sq_pe.append(ed)
            e_s2 = sq_pe[-1]
            S.wait("vector", e_s1)
            S.wait("vector", e_s2)
            vc(lambda e, bk1=bk1: e.tensor_scalar(out=stats[:, 0, 0:W], in0=ps[bk1][:, 0:W],
                                                  scalar1=conv[:, C_INV1024:C_INV1024 + 1],
                                                  scalar2=None, op0=ALU.mult))
            e_e1 = vc(lambda e, bk2=bk2: e.tensor_scalar(out=stats[:, 1, 0:W], in0=ps[bk2][:, 0:W],
                                                         scalar1=conv[:, C_INV1024:C_INV1024 + 1],
                                                         scalar2=None, op0=ALU.mult))
            BK.put(bk1, e_e1)
            BK.put(bk2, e_e1)
            vc(lambda e: e.tensor_tensor(stats[:, 2, 0:W], stats[:, 0, 0:W], stats[:, 0, 0:W], ALU.mult))
            e_var = vc(lambda e: e.tensor_tensor(stats[:, 1, 0:W], stats[:, 1, 0:W],
                                                 stats[:, 2, 0:W], ALU.subtract))
            S.wait("scalar", e_var)
            e_std = vc(lambda e: e.activation(out=stats[:, 2, 0:W], in_=stats[:, 1, 0:W], func=AF.Sqrt,
                                              bias=conv[:, C_LNEPS:C_LNEPS + 1], scale=1.0), "scalar")
            S.wait("vector", e_std)
            e_r = vc(lambda e: e.reciprocal(stats[:, 2, 0:W], stats[:, 2, 0:W]))
            e_nm = vc(lambda e: e.tensor_tensor(stats[:, 3, 0:W], stats[:, 0, 0:W],
                                                stats[:, 2, 0:W], ALU.mult))
            return e_nm, e_s2

        def apply_ln(dst_view, src_view, W, sc, sh, e_stats, e_pe, pre_waits=(),
                     u_buf=None, src3=None):
            S.wait("vector", e_stats)
            S.wait("vector", e_pe)
            for ed in pre_waits:
                S.wait("vector", ed)
            sc_t, sc_c = sc
            sh_t, sh_c = sh
            if src3 is not None:
                u3, uk = u_buf
                rb3 = bass.AP(stats, 2 * 256, [[1024, 128], [0, 8], [1, W]])
                nb3 = bass.AP(stats, 3 * 256, [[1024, 128], [0, 8], [1, W]])
                vc(lambda e: e.tensor_tensor(u3, src3, rb3, ALU.mult))
                vc(lambda e: e.tensor_tensor(u3, u3, nb3, ALU.subtract))
                last = None
                for k in range(8):
                    fn = (lambda e, k=k: e.tensor_scalar(
                        out=dst_view(k), in0=uk(k),
                        scalar1=sc_t[:, sc_c + k:sc_c + k + 1],
                        scalar2=sh_t[:, sh_c + k:sh_c + k + 1], op0=ALU.mult, op1=ALU.add))
                    last = vc(fn) if k == 7 else vop(fn)
                return last
            last = None
            for k in range(8):
                rb = bass.AP(stats, 2 * 256, [[1024, 128], [1, W]])
                nb = bass.AP(stats, 3 * 256, [[1024, 128], [1, W]])
                vc(lambda e, k=k, rb=rb: e.tensor_tensor(sq[:, k % 2, 0:W], src_view(k), rb, ALU.mult))
                vc(lambda e, k=k, nb=nb: e.tensor_tensor(sq[:, k % 2, 0:W], sq[:, k % 2, 0:W], nb, ALU.subtract))
                last = vc(lambda e, k=k: e.tensor_scalar(
                    out=dst_view(k), in0=sq[:, k % 2, 0:W],
                    scalar1=sc_t[:, sc_c + k:sc_c + k + 1],
                    scalar2=sh_t[:, sh_c + k:sh_c + k + 1], op0=ALU.mult, op1=ALU.add))
            state["sq_user"] = last
            return last

        if STOP_AT == "p0":
            return finish()
        S.wait("vector", e_xxt)
        S.wait("vector", e_dmod)
        xxv = lambda k: xx3[:, k, :]
        e_st, e_pe0 = ln_stats(xxv, 2, deps=(e_xxt, e_sm["consts"]))
        e_hx = apply_ln(lambda k: hx3[:, k, :], xxv, 2, (dmod, 8), (dmod, 0), e_st, e_pe0)
        hxv = lambda k: hx3[:, k, :]
        e_st2, e_pe1 = ln_stats(hxv, 2, deps=(e_hx, e_sm["gam"], e_sm["bet"]))
        e_xnx = apply_ln(lambda k: xnx3[:, k, :], hxv, 2, (gamv, 0), (betv, 0), e_st2, e_pe1)
        e_xnx = vc(lambda e: e.tensor_scalar(out=xnx3[:, :, 0:1], in0=xnx3[:, :, 0:1],
                                             scalar1=conv[:, C_XTRAMASK:C_XTRAMASK + 1],
                                             scalar2=None, op0=ALU.mult))

        if STOP_AT == "p1":
            return finish()
        h_free = {-2: e_wmod_pe, -1: e_wmod_pe}
        xn_free = {-2: None, -1: None}
        P = {}

        def part1a(idx):
            me = {}
            xvw = lambda k: xsl[:, idx % 2, k, :]
            x3 = xsl[:, idx % 2, :, :]
            e_st, e_pe_a = ln_stats(xvw, 256, deps=(x_edge[idx], e_dmod))
            e_ap = apply_ln(lambda k: hcv(idx % 2)[:, k, :], xvw, 256, (dmod, 8), (dmod, 0),
                            e_st, e_pe_a, pre_waits=(h_free[idx - 2],),
                            u_buf=(vv[:, :, :], lambda k: vv[:, k, :]), src3=x3)
            h3f = hsl[:, :]
            S.wait("vector", e_ap)
            S.wait("vector", h_free[idx - 2])
            e_hi = vc(lambda e, idx=idx: e.tensor_copy(hpb(idx % 2, 0), h3f))
            me["h"] = vc(lambda e, idx=idx: e.tensor_tensor(hpb(idx % 2, 1), h3f,
                                                            hpb(idx % 2, 0), ALU.subtract))
            return me

        def part1b(idx, e_h):
            me = {}
            hvw = lambda k: hcv(idx % 2)[:, k, :]
            h3 = hcv(idx % 2)[:, :, :]
            e_st2, e_pe_b = ln_stats(hvw, 256, deps=(e_h, e_sm["gam"], e_sm["bet"]),
                                     pair=lambda k, hl, idx=idx: hpv(idx % 2, hl)[:, k, :])
            if idx not in (0, 8):
                S.wait("vector", e_st2)
                rb0 = bass.AP(stats, 2 * 256, [[1024, 128], [0, 8], [1, 1]])
                nb0 = bass.AP(stats, 3 * 256, [[1024, 128], [0, 8], [1, 1]])
                vc(lambda e: e.tensor_tensor(sq[:, 0, 0:8].rearrange("p (a b) -> p a b", b=1),
                                             hcv(idx % 2)[:, :, 0:1], rb0, ALU.mult))
                vc(lambda e: e.tensor_tensor(sq[:, 0, 0:8].rearrange("p (a b) -> p a b", b=1),
                                             sq[:, 0, 0:8].rearrange("p (a b) -> p a b", b=1),
                                             nb0, ALU.subtract))
                e_pk = None
                for k in range(8):
                    fn = (lambda e, k=k: e.tensor_scalar(
                        out=xcol[:, k:k + 1], in0=sq[:, 0, k:k + 1],
                        scalar1=gamv[:, k:k + 1], scalar2=betv[:, k:k + 1],
                        op0=ALU.mult, op1=ALU.add))
                    e_pk = vc(fn) if k == 7 else vop(fn)
                state["sq_user"] = e_pk
                psl = (idx - 1) % 2
                vc(lambda e, psl=psl: e.tensor_copy(xnp(psl, 0)[:, :, 255:256], xc3))
                me["patch"] = vc(lambda e, psl=psl: e.tensor_tensor(
                    xnp(psl, 1)[:, :, 255:256], xc3,
                    xnp(psl, 0)[:, :, 255:256], ALU.subtract))
            e_apx = apply_ln(lambda k: hcv(0)[:, k, :], hvw, 256, (gamv, 0), (betv, 0),
                             e_st2, e_pe_b,
                             u_buf=(vv[:, :, :], lambda k: vv[:, k, :]), src3=h3)
            S.wait("vector", e_apx)
            S.wait("vector", xn_free[idx - 2])
            e_hix = vc(lambda e, idx=idx: e.tensor_copy(xnp(idx % 2, 0)[:, :, 0:255],
                                                        hcv(0)[:, :, 1:256]))
            me["xn"] = vc(lambda e, idx=idx: e.tensor_tensor(
                xnp(idx % 2, 1)[:, :, 0:255], hcv(0)[:, :, 1:256],
                xnp(idx % 2, 0)[:, :, 0:255], ALU.subtract))
            if "patch" not in me:
                me["patch"] = me["xn"]
            return me

        def patch_last(idx):
            col = 1 if idx == 7 else 0
            S.wait("vector", e_xnx)
            vc(lambda e, col=col: e.tensor_copy(xc3, xnx3[:, :, col:col + 1]))
            vc(lambda e, idx=idx: e.tensor_copy(xnp(idx % 2, 0)[:, :, 255:256], xc3))
            return vc(lambda e, idx=idx: e.tensor_tensor(
                xnp(idx % 2, 1)[:, :, 255:256], xc3,
                xnp(idx % 2, 0)[:, :, 255:256], ALU.subtract))

        def partZ(idx, e_h):
            me = {}
            S.wait("tensor", e_h)
            S.wait("tensor", e_eyeb)
            for ct in range(2):
                for fg in range(2):
                    for hl in range(2):
                        bkt, fet = BK.get()
                        S.wait("tensor", fet)
                        for ff in range(4):
                            f = fg * 4 + ff
                            fn = (lambda e, bkt=bkt, ct=ct, f=f, ff=ff, hl=hl, idx=idx:
                                  e.matmul(ps[bkt][:, ff * 128:(ff + 1) * 128],
                                           hpv(idx % 2, hl)[:, f, ct * 128:ct * 128 + 128],
                                           eyeb[:], start=True, stop=True))
                            e_t = S.op("tensor", fn, inc=spe) if ff == 3 else S.op("tensor", fn)
                        S.wait("vector", e_t)
                        ed = vop(lambda e, bkt=bkt, ct=ct, fg=fg, hl=hl: e.tensor_copy(
                            ztm_p(ct, hl)[:, fg * 512:(fg + 1) * 512], ps[bkt][:, 0:512]))
                        BK.put(bkt, ed)
            bkp, fep = BK.get()
            S.wait("tensor", fep)
            S.wait("tensor", e_ibp)
            for k in range(8):
                for pi, (pa, pb) in enumerate(((0, 0), (0, 1), (1, 0))):
                    fn = (lambda e, k=k, bkp=bkp, pa=pa, pb=pb, pi=pi, idx=idx: e.matmul(
                        ps[bkp][0:16, 0:256], ibtp(pa)[:, k, :], hpv(idx % 2, pb)[:, k, :],
                        start=(k == 0 and pi == 0), stop=(k == 7 and pi == 2)))
                    e_pi = S.op("tensor", fn, inc=spe) if (k == 7 and pi == 2) \
                        else S.op("tensor", fn)
            me["hpe"] = e_pi
            S.wait("vector", e_pi)
            e_piv = vop(lambda e, bkp=bkp: e.tensor_copy(prji[:], ps[bkp][0:16, 0:256]))
            BK.put(bkp, e_piv)
            bkq, feq = BK.get()
            S.wait("tensor", feq)
            S.wait("tensor", e_piv)
            for ct in range(2):
                fn = (lambda e, bkq=bkq, ct=ct: e.transpose(
                    ps[bkq][:, ct * 16:(ct + 1) * 16], prji[:, ct * 128:(ct + 1) * 128],
                    eyev[0:16, 0:16]))
                e_pt = S.op("tensor", fn, inc=spe) if ct == 1 else S.op("tensor", fn)
            S.wait("vector", e_pt)
            e_ptm = vop(lambda e, bkq=bkq: e.tensor_copy(ptmi[:].rearrange("p a b -> p (a b)"),
                                                         ps[bkq][:, 0:32]))
            BK.put(bkq, e_ptm)
            me["ptmi"] = e_ptm
            return me

        def part2(idx, e_xn_ready):
            me = {}
            fh = idx < 8
            for f in range(8):
                if f % 2 == 0:
                    bkv, fev = BK.get()
                    S.wait("tensor", fev)
                    if f == 0:
                        S.wait("tensor", e_xn_ready)
                        S.wait("tensor", e_wpt)
                for k in range(8):
                    for pi, (pa, pb) in enumerate(((0, 0), (0, 1), (1, 0))):
                        last = (k == 7 and f % 2 == 1 and pi == 2)
                        fn = (lambda e, f=f, k=k, bkv=bkv, pa=pa, pb=pb, pi=pi, idx=idx,
                              last=last: e.matmul(
                            ps[bkv][:, (f % 2) * 256:(f % 2) * 256 + 256],
                            wpp(pa)[:, k, f * 128:(f + 1) * 128],
                            xnp(idx % 2, pb)[:, k, :],
                            start=(k == 0 and f % 2 == 0 and pi == 0), stop=last))
                        if last:
                            e_v = S.op("tensor", fn, inc=spe)
                        else:
                            S.op("tensor", fn)
                if f % 2 == 1:
                    S.wait("vector", e_v)
                    e_vev = vop(lambda e, bkv=bkv, f=f: e.tensor_copy(
                        vsl[:, (f - 1) * 256:(f + 1) * 256], ps[bkv][:, 0:512]))
                    BK.put(bkv, e_vev)
            me["xnpe"] = e_v
            bkp, fep = BK.get()
            S.wait("tensor", fep)
            S.wait("tensor", e_vev)
            S.wait("tensor", e_ia)
            for k in range(8):
                fn = (lambda e, k=k, bkp=bkp: e.matmul(
                    ps[bkp][0:16, 0:256], initAv[:, k, :], vv[:, k, :],
                    start=(k == 0), stop=(k == 7)))
                e_pe2 = S.op("tensor", fn, inc=spe) if k == 7 else S.op("tensor", fn)
            S.wait("vector", e_pe2)
            e_pev = vop(lambda e, bkp=bkp: e.tensor_copy(prje[:], ps[bkp][0:16, 0:256]))
            BK.put(bkp, e_pev)
            bkq, feq = BK.get()
            S.wait("tensor", feq)
            S.wait("tensor", e_pev)
            for ct in range(2):
                fn = (lambda e, bkq=bkq, ct=ct: e.transpose(
                    ps[bkq][:, ct * 16:(ct + 1) * 16], prje[:, ct * 128:(ct + 1) * 128],
                    eyev[0:16, 0:16]))
                e_pt = S.op("tensor", fn, inc=spe) if ct == 1 else S.op("tensor", fn)
            S.wait("vector", e_pt)
            e_pf = vc(lambda e, bkq=bkq: e.tensor_copy(ptmef[:], ps[bkq][:, 0:32]))
            BK.put(bkq, e_pf)
            vc(lambda e: e.tensor_copy(ptmep(0), ptmef[:]))
            e_ptme = vc(lambda e: e.tensor_tensor(ptmep(1), ptmef[:],
                                                  ptmep(0), ALU.subtract))
            for ct in range(2):
                for fg in range(2):
                    bkt, fet = BK.get()
                    S.wait("tensor", fet)
                    for ff in range(4):
                        f = fg * 4 + ff
                        fn = (lambda e, bkt=bkt, ct=ct, f=f, ff=ff: e.transpose(
                            ps[bkt][:, ff * 128:(ff + 1) * 128], vv[:, f, ct * 128:ct * 128 + 128],
                            eyev[:]))
                        e_t = S.op("tensor", fn, inc=spe) if ff == 3 else S.op("tensor", fn)
                    S.wait("vector", e_t)
                    e_vc = vop(lambda e, bkt=bkt, ct=ct, fg=fg: e.tensor_copy(
                        vctm[:, ct, fg * 512:(fg + 1) * 512], ps[bkt][:, 0:512]))
                    BK.put(bkt, e_vc)
            acc_last = None

            def acc_upd(bkd, hg, acc_t):
                if fh:
                    return vop(lambda e, bkd=bkd, hg=hg, acc_t=acc_t: e.scalar_tensor_tensor(
                        out=acc_t[:, hg * 512:(hg + 1) * 512], in0=ps[bkd][0:16, 0:512],
                        scalar=conv[0:16, C_FHMASK:C_FHMASK + 1],
                        in1=acc_t[:, hg * 512:(hg + 1) * 512], op0=ALU.mult, op1=ALU.add))
                return vop(lambda e, bkd=bkd, hg=hg, acc_t=acc_t: e.tensor_tensor(
                    acc_t[:, hg * 512:(hg + 1) * 512], acc_t[:, hg * 512:(hg + 1) * 512],
                    ps[bkd][0:16, 0:512], ALU.add))

            for hg in range(2):
                bkd, fed = BK.get()
                S.wait("tensor", fed)
                S.wait("tensor", e_vc)
                for ct in range(2):
                    fn = (lambda e, bkd=bkd, ct=ct, hg=hg: e.matmul(
                        ps[bkd][0:16, 0:512], ptmi[:, ct, :],
                        vctm[:, ct, hg * 512:(hg + 1) * 512],
                        start=(ct == 0), stop=(ct == 1)))
                    e_d = S.op("tensor", fn, inc=spe) if ct == 1 else S.op("tensor", fn)
                S.wait("vector", e_d)
                acc_last = acc_upd(bkd, hg, accA)
                BK.put(bkd, acc_last)
            for hg in range(2):
                bkd, fed = BK.get()
                S.wait("tensor", fed)
                for ct in range(2):
                    for pi, (pa, pb) in enumerate(((0, 0), (0, 1), (1, 0))):
                        fn = (lambda e, bkd=bkd, ct=ct, hg=hg, pa=pa, pb=pb, pi=pi: e.matmul(
                            ps[bkd][0:16, 0:512], ptmep(pa)[:, ct * 16:(ct + 1) * 16],
                            ztm_p(ct, pb)[:, hg * 512:(hg + 1) * 512],
                            start=(ct == 0 and pi == 0), stop=(ct == 1 and pi == 2)))
                        e_d = S.op("tensor", fn, inc=spe) if (ct == 1 and pi == 2) \
                            else S.op("tensor", fn)
                S.wait("vector", e_d)
                acc_last = acc_upd(bkd, hg, accB)
                BK.put(bkd, acc_last)
            me["acc"] = acc_last
            return me

        def part3(idx, e_h_ready, e_acc_prev, e_spill_prev):
            s = idx - 8
            me = {}
            S.wait("vector", e_acc_prev)
            S.wait("vector", e_zacc)
            vc(lambda e: e.tensor_tensor(a2ef[:], accA[:], accA[:], ALU.mult))
            vc(lambda e: e.tensor_reduce(c16[:, 0:1], a2ef[:], AX.X, ALU.add))
            vc(lambda e: e.tensor_tensor(beff[:], accB[:], accB[:], ALU.mult))
            vc(lambda e: e.tensor_reduce(c16[:, 1:2], beff[:], AX.X, ALU.add))
            e_rs = vc(lambda e: e.tensor_tensor(c16[:, 2:3], c16[:, 0:1], c16[:, 1:2], ALU.add))
            bkc, fec = BK.get()
            S.wait("tensor", fec)
            S.wait("tensor", e_rs)
            e_cs = S.op("tensor", lambda e, bkc=bkc: e.matmul(
                ps[bkc][0:16, 0:1], onesv[0:16, 0:16], c16[:, 2:3], start=True, stop=True), inc=spe)
            S.wait("scalar", e_cs)
            e_nrm = vop(lambda e, bkc=bkc: e.activation(out=c16[:, 3:4], in_=ps[bkc][0:16, 0:1],
                                                        func=AF.Sqrt,
                                                        bias=conv[0:16, C_ZERO:C_ZERO + 1],
                                                        scale=conv[0:16, C_LR2:C_LR2 + 1]), "scalar")
            BK.put(bkc, e_nrm)
            S.wait("vector", e_nrm)
            vc(lambda e: e.tensor_scalar(out=c16[:, 3:4], in0=c16[:, 3:4],
                                         scalar1=conv[0:16, C_CLIPEPS:C_CLIPEPS + 1],
                                         scalar2=None, op0=ALU.add))
            vc(lambda e: e.reciprocal(c16[:, 4:5], c16[:, 3:4]))
            vc(lambda e: e.tensor_scalar_min(c16[:, 4:5], c16[:, 4:5], 1.0))
            vc(lambda e: e.tensor_scalar(out=c16[:, 5:6], in0=c16[:, 4:5],
                                         scalar1=conv[0:16, C_N2LR:C_N2LR + 1], scalar2=None, op0=ALU.mult))
            vc(lambda e: e.tensor_scalar(out=c16[:, 6:7], in0=c16[:, 4:5],
                                         scalar1=conv[0:16, C_NLR:C_NLR + 1], scalar2=None, op0=ALU.mult))
            S.wait("vector", e_ia2)
            vc(lambda e: e.scalar_tensor_tensor(out=a2ef[:], in0=accA[:], scalar=c16[:, 5:6],
                                                in1=iA2T[:], op0=ALU.mult, op1=ALU.add))
            e_ab = vc(lambda e: e.scalar_tensor_tensor(out=beff[:], in0=accB[:], scalar=c16[:, 6:7],
                                                       in1=iBv[:], op0=ALU.mult, op1=ALU.add))
            bkb, feb = BK.get()
            S.wait("tensor", feb)
            S.wait("tensor", e_ab)
            for k in range(8):
                fn = (lambda e, bkb=bkb, k=k: e.transpose(
                    ps[bkb][:, k * 16:(k + 1) * 16], beff[:, k * 128:(k + 1) * 128], eyev[0:16, 0:16]))
                e_bt = S.op("tensor", fn, inc=spe) if k == 7 else S.op("tensor", fn)
            S.wait("vector", e_bt)
            e_bf = vc(lambda e, bkb=bkb: e.tensor_copy(btf_w[:], ps[bkb][:, 0:128]))
            BK.put(bkb, e_bf)
            vc(lambda e: e.tensor_copy(btpf(0), btf_w[:]))
            e_btp = vc(lambda e: e.tensor_tensor(btpf(1), btf_w[:],
                                                 btpf(0), ALU.subtract))
            base_banks = []
            for bg in range(4):
                bkx, fex = BK.get()
                S.wait("tensor", fex)
                base_banks.append(bkx)
                if bg == 0:
                    S.wait("tensor", e_h_ready)
                    S.wait("tensor", e_wbt)
                for half in range(2):
                    hh = bg * 2 + half
                    for k in range(8):
                        for pi, (pa, pb) in enumerate(((0, 0), (0, 1), (1, 0))):
                            S.op("tensor", (lambda e, bkx=bkx, hh=hh, half=half, k=k,
                                            pa=pa, pb=pb, pi=pi, idx=idx: e.matmul(
                                ps[bkx][:, half * 256:half * 256 + 256],
                                wbp(pa)[:, k, hh * 128:(hh + 1) * 128],
                                hpv(idx % 2, pb)[:, k, :],
                                start=(k == 0 and half == 0 and pi == 0), stop=False)))
            bkm, fem = BK.get()
            S.wait("tensor", fem)
            S.wait("tensor", e_btp)
            for k in range(8):
                for pi, (pa, pb) in enumerate(((0, 0), (0, 1), (1, 0))):
                    fn = (lambda e, bkm=bkm, k=k, pa=pa, pb=pb, pi=pi, idx=idx: e.matmul(
                        ps[bkm][0:16, 0:256], btp(pa)[:, k, :], hpv(idx % 2, pb)[:, k, :],
                        start=(k == 0 and pi == 0), stop=(k == 7 and pi == 2)))
                    e_md = S.op("tensor", fn, inc=spe) if (k == 7 and pi == 2) \
                        else S.op("tensor", fn)
            me["hpe"] = e_md
            S.wait("vector", e_md)
            e_mv = vop(lambda e, bkm=bkm: e.tensor_copy(mids[:], ps[bkm][0:16, 0:256]))
            BK.put(bkm, e_mv)
            S.wait("tensor", e_mv)
            e_l = {}
            for bg in range(4):
                for half in range(2):
                    hh = bg * 2 + half
                    fn = (lambda e, bg=bg, half=half, hh=hh: e.matmul(
                        ps[base_banks[bg]][:, half * 256:half * 256 + 256],
                        a2ef[:, hh * 128:(hh + 1) * 128], mids[:], start=False, stop=(half == 1)))
                    e_l[bg] = S.op("tensor", fn, inc=spe) if half == 1 else S.op("tensor", fn)
            x2e = []
            for bg in range(4):
                S.wait("vector", e_l[bg])
                if bg == 0:
                    S.wait("vector", e_spill_prev)
                    S.wait("vector", ps_state["last_load"] if s == 0 else None)
                for half in range(2):
                    hh = bg * 2 + half
                    ed = vop(lambda e, bg=bg, half=half, hh=hh, s=s: e.scalar_tensor_tensor(
                        out=x2sl[:, s % 2, hh, :],
                        in0=ps[base_banks[bg]][:, half * 256:half * 256 + 256],
                        scalar=dmod[:, 16 + hh:16 + hh + 1],
                        in1=xsl[:, (s + 8) % 2, hh, :], op0=ALU.mult, op1=ALU.add))
                    x2e.append(ed)
                BK.put(base_banks[bg], x2e[-1])
            me["x2"] = x2e[-1]
            me["spill"] = dma("gpsimd", dG,
                              x2sp.rearrange("(kc p) t -> p kc t", p=128)[:, :, s * 256:(s + 1) * 256],
                              x2sl[:, s % 2, :, :], waits=(me["x2"],))
            return me

        P[0] = part1a(0)
        P[0].update(part1b(0, P[0]["h"]))
        Z = {0: partZ(0, P[0]["h"])}
        h_free[0] = Z[0]["hpe"]
        PRESPLIT = {0: (9,), 1: (0,), 2: (1, 2), 3: (3, 4), 4: (5, 6), 5: (7,), 6: (8,)}
        for s in range(8):
            nxt = s + 1
            P[nxt] = part1a(nxt)
            P[nxt].update(part1b(nxt, P[nxt]["h"]))
            if s < 7:
                e_xn_ready = P[nxt]["patch"]
                load_x(s + 2, waits=(P[s]["h"],))
            else:
                load_x(9, waits=(P[7]["h"],))
                e_xn_ready = patch_last(7)
            m2 = part2(s, e_xn_ready)
            P[s].update(m2)
            xn_free[s] = m2["xnpe"]
            Z[nxt] = partZ(nxt, P[nxt]["h"])
            h_free[nxt] = Z[nxt]["hpe"]
            for ei in PRESPLIT.get(s, ()):
                presplit_item(PS_ITEMS[ei], "sync" if ei in (9, 1, 3) else "scalar")
        for half in range(2):
            for pl in range(2):
                e_wbt = dma("sync", dW,
                            wbp(pl)[:, :, half * 512:(half + 1) * 512],
                            wbs[half, pl].rearrange("(kc p) f -> p kc f", p=128),
                            waits=(ps_state["last_store"],))
        if STOP_AT == "fh":
            return finish()
        prev_acc = P[7]["acc"]
        spill_hist = {}
        for s in range(8):
            idx = 8 + s
            if s < 7:
                P[idx + 1] = part1a(idx + 1)
            m3 = part3(idx, P[idx]["h"], prev_acc, spill_hist.get(idx - 2))
            P[idx].update(m3)
            h_free[idx] = m3["hpe"]
            spill_hist[idx] = m3["spill"]
            if s < 6:
                load_x(idx + 2, waits=(m3["x2"],))
            if s < 7:
                P[idx + 1].update(part1b(idx + 1, P[idx + 1]["h"]))
                e_xn_ready = P[idx + 1]["patch"]
            else:
                e_xn_ready = patch_last(15)
            m2 = part2(idx, e_xn_ready)
            P[idx].update(m2)
            xn_free[idx] = m2["xnpe"]
            prev_acc = m2["acc"]
            if s < 7:
                Z[idx + 1] = partZ(idx + 1, P[idx + 1]["h"])

        e_attn_pe = (spe, spe.n)
        if STOP_AT == "attn":
            return finish()

        dbg_backedge = {}
        x2_edge = {}
        def load_x2(g, waits):
            x2_edge[g] = dma("scalar", dX, x2sl[:, g % 2, :, :],
                             x2sp.rearrange("(kc p) t -> p kc t", p=128)[:, :, g * 256:(g + 1) * 256],
                             waits=waits)
        load_x2(0, (P[14]["spill"], P[8]["spill"]))
        load_x2(1, (P[15]["spill"], P[9]["spill"]))

        wex_edge = {}
        first_wex = [True]
        def load_wex(g, ei, buf, waits=()):
            deps = list(waits)
            if first_wex[0]:
                deps.append(e_attn_pe)
                first_wex[0] = False
            e1 = dma("sync", dW,
                     wslab[:, buf * 8192:buf * 8192 + 4096].bitcast(BF16)
                     .rearrange("p (pl k f) -> p pl k f", pl=2, k=8),
                     w1s[ei].rearrange("pl (kc p) f -> p pl kc f", p=128), waits=deps)
            wex_edge[(g, ei)] = dma("sync", dW,
                                    wslab[:, buf * 8192 + 4096:(buf + 1) * 8192].bitcast(BF16)
                                    .rearrange("p (pl f h) -> p pl f h", pl=2, f=4),
                                    w2s[ei].rearrange("pl (fc p) h -> p pl fc h", p=128))

        load_wex(0, 0, 0)
        load_wex(0, 1, 1)

        h2_free = {-2: None, -1: None}
        mask_gp = {0: None, 1: None}
        wex_pe = {0: None, 1: None}
        out_dma = []
        for g in range(8):
            xv2 = lambda k, g=g: x2sl[:, g % 2, k, :]
            e_st, e_pe_g = ln_stats(xv2, 256, deps=(x2_edge[g],))
            e_h2 = apply_ln(lambda k, g=g: hcv(g % 2)[:, k, :], xv2, 256, (dmod, 32), (dmod, 24),
                            e_st, e_pe_g,
                            pre_waits=(dbg_backedge.get("h2") if DEBUG_TAPS else None,))
            S.wait("vector", h2_free[g - 2])
            if g == 0:
                S.wait("vector", e_attn_pe)
            vc(lambda e, g=g: e.tensor_copy(h2p(g % 2, 0).rearrange("p a b -> p (a b)"),
                                            hcv(g % 2)[:, :, :].rearrange("p a b -> p (a b)")))
            e_h2s = vc(lambda e, g=g: e.tensor_tensor(
                h2p(g % 2, 1).rearrange("p a b -> p (a b)"),
                hcv(g % 2)[:, :, :].rearrange("p a b -> p (a b)"),
                h2p(g % 2, 0).rearrange("p a b -> p (a b)"), ALU.subtract))
            if DEBUG_TAPS:
                dbg_backedge["h2"] = dma(
                    "gpsimd", dG, dbg_t.rearrange("(kc p) t -> p kc t", p=128)[:, :, g * 256:(g + 1) * 256],
                    hcv(g % 2)[:, :, :], waits=(e_h2,))
            bkr, fer = BK.get()
            S.wait("tensor", fer)
            S.wait("tensor", e_h2s)
            S.wait("tensor", e_wrp)
            for k in range(8):
                for pw, ph in ((0, 0), (0, 1), (1, 0)):
                    fn = (lambda e, bkr=bkr, k=k, g=g, pw=pw, ph=ph: e.matmul(
                        ps[bkr][0:8, 0:256], wrp[:, pw, k, :], h2p(g % 2, ph)[:, k, :],
                        start=(k == 0 and pw == 0 and ph == 0), stop=(k == 7 and pw == 1)))
                    e_lg = S.op("tensor", fn, inc=spe) if (k == 7 and pw == 1) else S.op("tensor", fn)
            S.wait("vector", e_lg)
            e_lgv = vop(lambda e, bkr=bkr: e.tensor_copy(lgfm[:], ps[bkr][0:8, 0:256]))
            BK.put(bkr, e_lgv)
            bkr2, fer2 = BK.get()
            S.wait("tensor", fer2)
            S.wait("tensor", e_lgv)
            for ct in range(2):
                fn = (lambda e, bkr2=bkr2, ct=ct: e.transpose(
                    ps[bkr2][:, ct * 8:(ct + 1) * 8], lgfm[:, ct * 128:(ct + 1) * 128],
                    eyev[0:8, 0:8]))
                e_lt = S.op("tensor", fn, inc=spe) if ct == 1 else S.op("tensor", fn)
            S.wait("vector", e_lt)
            e_ltv = vop(lambda e, bkr2=bkr2: e.tensor_copy(lgtm[:].rearrange("p a b -> p (a b)"),
                                                           ps[bkr2][:, 0:16]))
            BK.put(bkr2, e_ltv)
            S.wait("vector", e_ltv)
            if DEBUG_TAPS:
                S.wait("vector", dbg_backedge.get("prfm"))
                S.wait("vector", dbg_backedge.get("m1c"))
            for ct in range(2):
                vc(lambda e, ct=ct: e.max(out=rtr[:, ct, 0:8], in_=lgtm[:, ct, :]))
            for ct in range(2):
                vc(lambda e, ct=ct: e.tensor_copy(m1c[:, ct:ct + 1], rtr[:, ct, 0:1]))
                vc(lambda e, ct=ct: e.tensor_copy(m2c[:, ct:ct + 1], rtr[:, ct, 1:2]))
            e_nm1 = vc(lambda e: e.tensor_scalar_mul(nm1c[:], m1c[:], -1.0))
            S.wait("scalar", e_nm1)
            for ct in range(2):
                e_exp = vc(lambda e, ct=ct: e.activation(
                    out=dvc[:, ct:ct + 1], in_=m2c[:, ct:ct + 1], func=AF.Exp,
                    bias=nm1c[:, ct:ct + 1], scale=1.0), "scalar")
            S.wait("vector", e_exp)
            vc(lambda e: e.tensor_scalar_add(dvc[:], dvc[:], 1.0))
            vc(lambda e: e.reciprocal(dvc[:], dvc[:]))
            vc(lambda e: e.tensor_single_scalar(g1c[:], dvc[:], 1.0, ALU.is_ge))
            for ct in range(2):
                vc(lambda e, ct=ct: e.tensor_single_scalar(eqc[:, ct, :], lgtm[:, ct, :],
                                                           m1c[:, ct:ct + 1], ALU.is_equal))
                e_prt = vc(lambda e, ct=ct: e.tensor_scalar(out=prtm[:, ct, :], in0=eqc[:, ct, :],
                                                            scalar1=g1c[:, ct:ct + 1],
                                                            scalar2=None, op0=ALU.mult))
            bkr3, fer3 = BK.get()
            S.wait("tensor", fer3)
            S.wait("tensor", e_prt)
            for ct in range(2):
                fn = (lambda e, bkr3=bkr3, ct=ct: e.transpose(
                    ps[bkr3][0:8, ct * 128:(ct + 1) * 128], prtm[:, ct, :], eyev[:]))
                e_pf = S.op("tensor", fn, inc=spe) if ct == 1 else S.op("tensor", fn)
            S.wait("vector", e_pf)
            e_pfm = vop(lambda e, bkr3=bkr3: e.tensor_copy(prfm[:], ps[bkr3][0:8, 0:256]))
            BK.put(bkr3, e_pfm)
            if DEBUG_TAPS:
                e_tap2 = dma("gpsimd", dG, dbg2_t[g], prfm[:], waits=(e_pfm,))
                dma("gpsimd", dG, dbg3_t[g][:, 0:2], m1c[:], waits=(e_pfm,), slow=True)
                e_tap3 = dma("gpsimd", dG, dbg3_t[g][:, 2:4], m2c[:], waits=(e_pfm,), slow=True)
                dbg_backedge["prfm"] = e_tap2
                dbg_backedge["m1c"] = e_tap3

            if STOP_AT == f"ffn_r{g}":
                return finish()
            ye_banks = []
            for bg in range(4):
                bky, fey = BK.get()
                S.wait("tensor", fey)
                ye_banks.append(bky)
            ye_last = {}
            e_hpe_last = None
            e_ye_prev = None
            prev_hem1 = None
            for ei in range(9):
                buf = (g * 9 + ei) % 2
                nxt = (g, ei + 1) if ei < 8 else (g + 1, 0)
                if nxt[0] <= 7 and nxt not in wex_edge and not (NOPREF and nxt[0] > 0):
                    load_wex(nxt[0], nxt[1], (g * 9 + ei + 1) % 2,
                             waits=() if wex_pe[(buf + 1) % 2] is None else (wex_pe[(buf + 1) % 2],))
                if ei < 8:
                    bkm2, fem2 = BK.get()
                    S.wait("tensor", fem2)
                    S.wait("tensor", e_pfm)
                    S.wait("tensor", e_sel)
                    e_mk = S.op("tensor", lambda e, bkm2=bkm2, ei=ei: e.matmul(
                        ps[bkm2][:, 0:256], selv[:, ei * 128:(ei + 1) * 128], prfm[:],
                        start=True, stop=True), inc=spe)
                    S.wait("scalar", e_mk)
                    S.wait("scalar", mask_gp[ei % 2])
                    e_mke = vop(lambda e, bkm2=bkm2, ei=ei: e.copy(maskb[:, ei % 2, :],
                                                                   ps[bkm2][:, 0:256]), "scalar")
                    BK.put(bkm2, e_mke)
                S.wait("tensor", wex_edge[(g, ei)])
                e_hm = None
                for fg in range(2):
                    bkh, feh = BK.get()
                    S.wait("tensor", feh)
                    for half in range(2):
                        f = fg * 2 + half
                        for k in range(8):
                            for pw, ph in ((0, 0), (0, 1), (1, 0)):
                                fn = (lambda e, bkh=bkh, f=f, half=half, k=k, buf=buf, g=g,
                                      pw=pw, ph=ph: e.matmul(
                                    ps[bkh][:, half * 256:half * 256 + 256],
                                    w1p(buf, pw)[:, k, f * 128:(f + 1) * 128],
                                    h2p(g % 2, ph)[:, k, :],
                                    start=(k == 0 and half == 0 and pw == 0 and ph == 0),
                                    stop=(k == 7 and half == 1 and pw == 1)))
                                if k == 7 and half == 1 and pw == 1:
                                    e_he = S.op("tensor", fn, inc=spe)
                                else:
                                    S.op("tensor", fn)
                    e_hpe_last = e_he
                    S.wait("scalar", e_he)
                    if fg == 0:
                        S.wait("scalar", prev_hem1)
                        S.wait("vector", e_ye_prev)
                    e_t1 = {}
                    for half in range(2):
                        f = fg * 2 + half
                        e_t1[half] = vop(lambda e, bkh=bkh, half=half, f=f: e.activation(
                            out=hem(1)[:, f, :], in_=ps[bkh][:, half * 256:half * 256 + 256],
                            func=AF.Relu), "scalar")
                    for half in range(2):
                        f = fg * 2 + half
                        if half == 0:
                            S.wait("vector", e_t1[1])
                            if ei < 8:
                                S.wait("vector", e_mke)
                        if ei < 8:
                            e_msk = vc(lambda e, f=f, ei=ei: e.tensor_tensor(
                                hem(1)[:, f, :], hem(1)[:, f, :], maskb[:, ei % 2, :], ALU.mult))
                            if half == 1:
                                mask_gp[ei % 2] = e_msk
                        e_hm = vc(lambda e, bkh=bkh, half=half, f=f: e.tensor_tensor(
                            hem(0)[:, f, :], hem(1)[:, f, :],
                            ps[bkh][:, half * 256:half * 256 + 256], ALU.mult))
                        vc(lambda e, f=f: e.tensor_copy(hep(0)[:, f, :], hem(0)[:, f, :]))
                        e_hm = vc(lambda e, f=f: e.tensor_tensor(hep(1)[:, f, :], hem(0)[:, f, :],
                                                                 hep(0)[:, f, :], ALU.subtract))
                    BK.put(bkh, e_hm)
                if STOP_AT == f"ffn_he{ei}" and g == 0:
                    return finish()
                S.wait("tensor", e_hm)
                for bg in range(4):
                    for half in range(2):
                        hh = bg * 2 + half
                        for f in range(4):
                            for pw, ph in ((0, 0), (0, 1), (1, 0)):
                                last_pass = (pw == 1)
                                last_all = (half == 1 and f == 3 and bg == 3 and last_pass)
                                fn = (lambda e, bank=ye_banks[bg], half=half, hh=hh, f=f, buf=buf,
                                      ei=ei, pw=pw, ph=ph: e.matmul(
                                    ps[bank][:, half * 256:half * 256 + 256],
                                    w2p(buf, pw)[:, f, hh * 128:(hh + 1) * 128], hep(ph)[:, f, :],
                                    start=(ei == 0 and half == 0 and f == 0 and pw == 0 and ph == 0),
                                    stop=(ei == 8 and half == 1 and f == 3 and pw == 1)))
                                if ei == 8 and f == 3 and half == 1 and last_pass:
                                    ye_last[bg] = S.op("tensor", fn, inc=spe)
                                elif last_all:
                                    wex_pe[buf] = S.op("tensor", fn, inc=spe)
                                else:
                                    S.op("tensor", fn)
                if ei == 8:
                    wex_pe[buf] = ye_last[3]
                e_ye_prev = wex_pe[buf] if ei < 8 else ye_last[3]
                prev_hem1 = e_hm
                if STOP_AT == f"ffn_e{ei}" and g == 0:
                    return finish()
                if STOP_AT is not None and STOP_AT.startswith(f"ffn_e{ei}mim") and g == 0:
                    S.wait("tensor", wex_edge[(0, 8)])
                    for fg in range(2):
                        bkh, feh = BK.get()
                        S.wait("tensor", feh)
                        for half in range(2):
                            f = fg * 2 + half
                            for k in range(8):
                                fn = (lambda e, bkh=bkh, f=f, half=half, k=k, g=g: e.matmul(
                                    ps[bkh][:, half * 256:half * 256 + 256],
                                    w1v(0)[:, k, f * 128:(f + 1) * 128], hcv(g % 2)[:, k, :],
                                    start=(k == 0 and half == 0), stop=(k == 7 and half == 1)))
                                if k == 7 and half == 1:
                                    e_heM = S.op("tensor", fn, inc=spe)
                                else:
                                    S.op("tensor", fn)
                        if STOP_AT == f"ffn_e{ei}mimA":
                            S.wait("vector", e_heM)
                            ed = vop(lambda e, bkh=bkh: e.tensor_copy(sq[:, 0, :], ps[bkh][:, 0:256]))
                            BK.put(bkh, ed)
                            continue
                        S.wait("scalar", e_heM)
                        if fg == 0:
                            S.wait("scalar", prev_hem1)
                            S.wait("vector", e_ye_prev)
                        for half in range(2):
                            f = fg * 2 + half
                            e_t1M = vop(lambda e, bkh=bkh, half=half, f=f: e.activation(
                                out=hem(1)[:, f, :], in_=ps[bkh][:, half * 256:half * 256 + 256],
                                func=AF.Relu), "scalar")
                            if STOP_AT == f"ffn_e{ei}mimB":
                                S.wait("vector", e_t1M)
                                e_hmM = vop(lambda e, bkh=bkh, half=half, f=f: e.tensor_copy(
                                    hem(0)[:, f, :], hem(1)[:, f, :]))
                            else:
                                S.wait("vector", e_t1M)
                                e_hmM = vop(lambda e, bkh=bkh, half=half, f=f: e.tensor_tensor(
                                    hem(0)[:, f, :], hem(1)[:, f, :],
                                    ps[bkh][:, half * 256:half * 256 + 256], ALU.mult))
                        BK.put(bkh, e_hmM)
                    return finish()
                if STOP_AT == f"ffn_e{ei}pad" and g == 0:
                    for fg in range(2):
                        bkh, feh = BK.get()
                        S.wait("tensor", feh)
                        for half in range(2):
                            f = fg * 2 + half
                            for k in range(8):
                                fn = (lambda e, bkh=bkh, f=f, half=half, k=k, g=g: e.matmul(
                                    ps[bkh][:, half * 256:half * 256 + 256],
                                    w1v(0)[:, k, f * 128:(f + 1) * 128], hcv(g % 2)[:, k, :],
                                    start=(k == 0 and half == 0), stop=(k == 7 and half == 1)))
                                if k == 7 and half == 1:
                                    e_dummy = S.op("tensor", fn, inc=spe)
                                else:
                                    S.op("tensor", fn)
                        S.wait("vector", e_dummy)
                        ed = vop(lambda e, bkh=bkh: e.tensor_copy(sq[:, 0, :], ps[bkh][:, 0:256]))
                        BK.put(bkh, ed)
                    return finish()
            h2_free[g] = e_hpe_last
            oe = []
            for bg in range(4):
                S.wait("vector", ye_last[bg])
                for half in range(2):
                    hh = bg * 2 + half
                    ed = vop(lambda e, bank=ye_banks[bg], half=half, hh=hh, g=g: e.scalar_tensor_tensor(
                        out=x2sl[:, g % 2, hh, :],
                        in0=ps[bank][:, half * 256:half * 256 + 256],
                        scalar=dmod[:, 40 + hh:40 + hh + 1],
                        in1=x2sl[:, g % 2, hh, :], op0=ALU.mult, op1=ALU.add))
                    oe.append(ed)
                BK.put(ye_banks[bg], oe[-1])
            e_od = dma("gpsimd", dG,
                       out_t.rearrange("(kc p) t -> p kc t", p=128)[:, :, g * 256:(g + 1) * 256],
                       x2sl[:, g % 2, :, :], waits=(oe[-1],))
            out_dma.append(e_od)
            if g + 2 <= 7:
                load_x2(g + 2, (e_od, P[8 + g + 2]["spill"]))
            if STOP_AT == f"ffn_g{g}":
                return finish()

        S.wait("gpsimd", out_dma[-1])
        S.wait("sync", out_dma[-1])

        S.check()
        blk = st.enter_context(nc.Block())
        for ename in ENGINES:
            section = getattr(blk, ename)
            def emit(eng, _q=tuple(S.q[ename])):
                for item in _q:
                    item(eng)
            section(emit)

    return nc


def kernel(L0_attn_x, L0_attn_Wb, L0_attn_init_A, L0_attn_init_B, L0_attn_ln_gamma,
           L0_attn_ln_beta, L0_attn_Wp, L0_ffn_Ws1, L0_ffn_Ws2, L0_ffn_W1, L0_ffn_W2,
           L0_ffn_Wr, L0_c, L0_wmod):
    x = np.asarray(L0_attn_x, dtype=np.float32)
    B, S_, D = x.shape
    fmT = lambda a: np.ascontiguousarray(np.asarray(a, np.float32).T)

    common = dict(
        wbT=fmT(L0_attn_Wb), wpT=fmT(L0_attn_Wp),
        initA=np.ascontiguousarray(np.asarray(L0_attn_init_A, np.float32)),
        initAT=fmT(L0_attn_init_A),
        initB=np.ascontiguousarray(np.asarray(L0_attn_init_B, np.float32)),
        initBT=fmT(L0_attn_init_B),
        gam=np.ascontiguousarray(np.asarray(L0_attn_ln_gamma, np.float32).reshape(8, 128).T),
        bet=np.ascontiguousarray(np.asarray(L0_attn_ln_beta, np.float32).reshape(8, 128).T),
        wr=np.ascontiguousarray(np.asarray(L0_ffn_Wr, np.float32)),
        wmod=np.ascontiguousarray(np.asarray(L0_wmod, np.float32)),
        wex1=np.ascontiguousarray(np.concatenate(
            [np.asarray(L0_ffn_W1, np.float32), np.asarray(L0_ffn_Ws1, np.float32)[None]], 0)),
        wex2=np.ascontiguousarray(np.concatenate(
            [np.asarray(L0_ffn_W2, np.float32), np.asarray(L0_ffn_Ws2, np.float32)[None]], 0)),
        ones=np.ones((128, 128), dtype=np.float32),
        eye=np.eye(128, dtype=np.float32),
    )
    import ml_dtypes
    sel = np.zeros((8, 1024), dtype=ml_dtypes.bfloat16)
    for e in range(8):
        sel[e, e * 128:(e + 1) * 128] = 1.0
    common["sel"] = sel
    common["onesb"] = np.ones((128, 128), dtype=ml_dtypes.bfloat16)
    cN = np.asarray(L0_c, np.float32)

    in_maps = []
    lr = 0.01 * 2.0
    for core in range(8):
        b, hf = core // 2, core % 2
        xo = fmT(x[b, hf * 2048:(hf + 1) * 2048])
        if hf == 0:
            xf = np.zeros((1024, 2048), dtype=np.float32)
            xxt = np.ascontiguousarray(np.stack([x[b, 2048], x[b, 0]], 1))
        else:
            xf = fmT(x[b, 0:2048])
            xxt = np.ascontiguousarray(np.stack([np.zeros(1024, np.float32), x[b, 2048]], 1))
        consts = np.zeros((128, 16), dtype=np.float32)
        for col, v in {C_ZERO: 0.0, C_LNEPS: 1e-5, C_CLIPEPS: 1e-6, C_LR2: lr * lr,
                       C_N2LR: -2.0 * lr, C_NLR: -lr, C_INV1024: 1.0 / 1024.0,
                       C_QUART: 0.25, C_ONE: 1.0, C_FHMASK: float(hf == 1),
                       C_XTRAMASK: float(hf == 0), C_TWO: 2.0}.items():
            consts[:, col] = v
        in_maps.append(dict(common, xo=xo, xxt=xxt, xf=xf, consts=consts,
                            cfm=np.ascontiguousarray(cN[b].reshape(8, 128).T)))

    nc = build_program()
    res = run_bass_kernel_spmd(nc, in_maps, core_ids=list(range(8)))
    out = np.empty((B, S_, D), dtype=np.float32)
    for core in range(8):
        b, hf = core // 2, core % 2
        out[b, hf * 2048:(hf + 1) * 2048] = res.results[core]["out"].T
    return out


# version 11
# speedup vs baseline: 1.0033x; 1.0033x over previous
import numpy as np
from contextlib import ExitStack

import concourse.bass as bass
import concourse.mybir as mybir
from concourse.bass_utils import run_bass_kernel_spmd

F32 = mybir.dt.float32
BF16 = mybir.dt.bfloat16
AF = mybir.ActivationFunctionType
ALU = mybir.AluOpType
AX = mybir.AxisListType

ENGINES = ("sync", "scalar", "vector", "tensor", "gpsimd")
DEBUG_TAPS = False
STOP_AT = None
NOPREF = False

C_ZERO, C_LNEPS, C_CLIPEPS, C_LR2, C_N2LR, C_NLR, C_INV1024, C_QUART = range(8)
C_ONE, C_FHMASK, C_XTRAMASK, C_TWO = 8, 9, 10, 11


class Sem:
    def __init__(self, handle, name):
        self.h, self.name, self.n = handle, name, 0


class Sched:
    def __init__(self):
        self.q = {e: [] for e in ENGINES}
        self.trace = {e: [] for e in ENGINES}

    def op(self, eng, fn, inc=None, dv=1):
        if inc is None:
            self.q[eng].append(fn)
            return None
        def run(e, fn=fn, inc=inc, dv=dv):
            fn(e).then_inc(inc.h, dv)
        self.q[eng].append(run)
        inc.n += dv
        self.trace[eng].append(("i", inc, dv))
        return (inc, inc.n)

    def wait(self, eng, edge):
        if edge is None:
            return
        sem, v = edge
        self.q[eng].append(lambda e, sem=sem, v=v: e.wait_ge(sem.h, v))
        self.trace[eng].append(("w", sem, v))

    def check(self):
        pos = {e: 0 for e in ENGINES}
        vals = {}
        progressed = True
        while progressed:
            progressed = False
            for e in ENGINES:
                while pos[e] < len(self.trace[e]):
                    kind, sem, v = self.trace[e][pos[e]]
                    if kind == "i":
                        vals[sem.name] = vals.get(sem.name, 0) + v
                        pos[e] += 1
                        progressed = True
                    elif vals.get(sem.name, 0) >= v:
                        pos[e] += 1
                        progressed = True
                    else:
                        break
        stuck = {e: self.trace[e][pos[e]] for e in ENGINES if pos[e] < len(self.trace[e])}
        if stuck:
            raise RuntimeError("deadlock: " + str({e: (k, s.name, v) for e, (k, s, v) in stuck.items()}))


class Banks:
    def __init__(self, n=8):
        self.pool = [[i, None] for i in range(n)]

    def get(self):
        i, e = self.pool.pop(0)
        return i, e

    def put(self, i, e):
        assert i not in [x[0] for x in self.pool], f"double put of bank {i}"
        self.pool.append([i, e])


def build_program():
    nc = bass.Bass("TRN2", target_bir_lowering=False, detect_race_conditions=False)

    dt = {}
    def din(name, shape):
        dt[name] = nc.dram_tensor(name, list(shape), F32, kind="ExternalInput").ap()
    din("xo", (1024, 2048)); din("xxt", (1024, 2)); din("xf", (1024, 2048))
    din("wbT", (1024, 1024)); din("wpT", (1024, 1024)); din("wmod", (1024, 6144))
    din("cfm", (128, 8)); din("gam", (128, 8)); din("bet", (128, 8))
    din("initBT", (1024, 16)); din("initA", (1024, 16)); din("initAT", (16, 1024))
    din("initB", (16, 1024)); din("wr", (1024, 8))
    dt["sel"] = nc.dram_tensor("sel", [8, 1024], BF16, kind="ExternalInput").ap()
    din("ones", (128, 128)); din("eye", (128, 128)); din("consts", (128, 16))
    din("wex1", (9, 1024, 512)); din("wex2", (9, 512, 1024))
    dt["onesb"] = nc.dram_tensor("onesb", [128, 128], BF16, kind="ExternalInput").ap()
    out_t = nc.dram_tensor("out", [1024, 2048], F32, kind="ExternalOutput").ap()
    dbg_t = nc.dram_tensor("dbg", [1024, 2048], F32, kind="ExternalOutput").ap() if DEBUG_TAPS else None
    dbg2_t = nc.dram_tensor("dbg2", [8, 8, 256], F32, kind="ExternalOutput").ap() if DEBUG_TAPS else None
    dbg3_t = nc.dram_tensor("dbg3", [8, 128, 4], F32, kind="ExternalOutput").ap() if DEBUG_TAPS else None
    dbg4_t = nc.dram_tensor("dbg4", [128, 96], F32, kind="ExternalOutput").ap() if DEBUG_TAPS else None
    x2sp = nc.dram_tensor("x2sp", [1024, 2048], F32,
                          kind="ExternalOutput" if DEBUG_TAPS else "Internal").ap()
    w1s = nc.dram_tensor("w1s", [9, 2, 1024, 512], BF16).ap()
    w2s = nc.dram_tensor("w2s", [9, 2, 512, 1024], BF16).ap()
    wbs = nc.dram_tensor("wbs", [2, 2, 1024, 512], BF16).ap()

    fm = lambda n: dt[n].rearrange("(kc p) t -> p kc t", p=128)

    st = ExitStack()
    with st:
        def sb(name, shape):
            return st.enter_context(nc.sbuf_tensor(name, shape, F32))

        wslab = sb("wslab", [128, 16384])
        xsl = sb("xsl", [128, 2, 8, 256])
        xcol = sb("xcol", [128, 8])
        xnp_w = sb("xnp", [128, 2, 2048])
        hp_w = sb("hp", [128, 2, 2048])
        btp_w = sb("btp", [128, 128])
        ibtp_w = sb("ibtp", [128, 128])
        hsl = sb("hsl", [128, 2048])
        vsl = sb("vsl", [128, 2048])
        vctm = sb("vctm", [128, 2, 1024])
        ztm = sb("ztm", [128, 2, 1024])
        x2sl = sb("x2sl", [128, 2, 8, 256])
        sq = sb("sq", [128, 2, 256])
        stats = sb("stats", [128, 4, 256])
        modv = sb("modv", [128, 48])
        dmod = modv
        sfm = sb("sfm", [128, 8]); cfmv = sb("cfmv", [128, 8])
        gamv = sb("gamv", [128, 8]); betv = sb("betv", [128, 8])
        conv = sb("conv", [128, 16])
        onesv = sb("onesv", [128, 128]); eyev = sb("eyev", [128, 128])
        initAv = sb("initAv", [128, 8, 16])
        selv = st.enter_context(nc.sbuf_tensor("selv", [8, 1024], BF16))
        accA = st.enter_context(nc.sbuf_tensor("accA", [16, 1024], F32))
        accB = st.enter_context(nc.sbuf_tensor("accB", [16, 1024], F32))
        iA2T = st.enter_context(nc.sbuf_tensor("iA2T", [16, 1024], F32))
        iBv = st.enter_context(nc.sbuf_tensor("iBv", [16, 1024], F32))
        a2ef = st.enter_context(nc.sbuf_tensor("a2ef", [16, 1024], F32))
        beff = st.enter_context(nc.sbuf_tensor("beff", [16, 1024], F32))
        prji = st.enter_context(nc.sbuf_tensor("prji", [16, 256], F32))
        prje = st.enter_context(nc.sbuf_tensor("prje", [16, 256], F32))
        ptmi = sb("ptmi", [128, 2, 16])
        ptmep_w = sb("ptmep", [128, 32])
        ptmef = sb("ptmef", [128, 32])
        btf_w = sb("btf", [128, 128])
        ptmep = lambda hl: ptmep_w[:, hl * 16:(hl + 1) * 16].bitcast(BF16)
        eyeb = st.enter_context(nc.sbuf_tensor("eyeb", [128, 128], BF16))
        mids = st.enter_context(nc.sbuf_tensor("mids", [16, 256], F32))
        c16 = st.enter_context(nc.sbuf_tensor("c16", [16, 8], F32))
        lgfm = st.enter_context(nc.sbuf_tensor("lgfm", [8, 256], F32))
        prfm = st.enter_context(nc.sbuf_tensor("prfm", [8, 256], BF16))
        lgtm = sb("lgtm", [128, 2, 8])
        rtr = sb("rtr", [128, 2, 10])
        m1c = sb("m1c", [128, 2]); m2c = sb("m2c", [128, 2]); nm1c = sb("nm1c", [128, 2])
        dvc = sb("dvc", [128, 2]); g1c = sb("g1c", [128, 2])
        eqc = sb("eqc", [128, 2, 8]); prtm = sb("prtm", [128, 2, 8])
        maskv = sb("maskv", [128, 2, 256])
        maskb = st.enter_context(nc.sbuf_tensor("maskb", [128, 2, 256], BF16))
        onebv = st.enter_context(nc.sbuf_tensor("onebv", [128, 128], BF16))
        wrp = st.enter_context(nc.sbuf_tensor("wrp", [128, 2, 8, 8], BF16))
        ps = [st.enter_context(nc.psum_tensor(f"ps{i}", [128, 512], F32)) for i in range(8)]

        sems = {n: Sem(st.enter_context(nc.semaphore(n)), n)
                for n in ("dW", "dX", "dG", "dP", "dS", "dT", "dE", "dF", "pe", "dv", "ac", "gp")}
        dW, dX, dG = sems["dW"], sems["dX"], sems["dG"]
        dP, dS, dT, dE, dF = sems["dP"], sems["dS"], sems["dT"], sems["dE"], sems["dF"]
        spe, sdv, sac, sgp = sems["pe"], sems["dv"], sems["ac"], sems["gp"]

        S = Sched()
        BK = Banks()

        def finish():
            S.check()
            blk = st.enter_context(nc.Block())
            for ename in ENGINES:
                section = getattr(blk, ename)
                def emit(eng, _q=tuple(S.q[ename])):
                    for item in _q:
                        item(eng)
                section(emit)
            return nc

        def dma(qeng, sem, out_ap, in_ap, waits=(), slow=False):
            for ed in waits:
                S.wait(qeng, ed)
            def fn(e, o=out_ap, i=in_ap, slow=slow):
                return e.dma_start(out=o, in_=i, allow_slow_non_contiguous=True) if slow \
                    else e.dma_start(out=o, in_=i)
            return S.op(qeng, fn, inc=sem, dv=16)

        def vop(fn, eng="vector", inc=None):
            return S.op(eng, fn, inc=(inc or {"vector": sdv, "scalar": sac,
                                             "gpsimd": sgp}[eng]))

        def vc(fn, eng="vector"):
            ed = vop(fn, eng)
            S.wait(eng, ed)
            return ed

        wpf = wslab[:, 0:8192]
        wppf = lambda hl: wslab[:, 8192 + hl * 4096:8192 + (hl + 1) * 4096].bitcast(BF16)
        wpp = lambda hl: wslab[:, 8192 + hl * 4096:8192 + (hl + 1) * 4096]             .bitcast(BF16).rearrange("p (k f) -> p k f", k=8)
        wbp = lambda hl: wslab[:, hl * 4096:(hl + 1) * 4096]             .bitcast(BF16).rearrange("p (k f) -> p k f", k=8)
        xnp = lambda sl, hl: xnp_w[:, sl, hl * 1024:(hl + 1) * 1024]             .bitcast(BF16).rearrange("p (k t) -> p k t", k=8)
        hpb = lambda sl, hl: hp_w[:, sl, hl * 1024:(hl + 1) * 1024].bitcast(BF16)
        hpv = lambda sl, hl: hp_w[:, sl, hl * 1024:(hl + 1) * 1024]             .bitcast(BF16).rearrange("p (k t) -> p k t", k=8)
        btpf = lambda hl: btp_w[:, hl * 64:(hl + 1) * 64].bitcast(BF16)
        btp = lambda hl: btp_w[:, hl * 64:(hl + 1) * 64]             .bitcast(BF16).rearrange("p (k r) -> p k r", k=8)
        ibtpf = lambda hl: ibtp_w[:, hl * 64:(hl + 1) * 64].bitcast(BF16)
        ibtp = lambda hl: ibtp_w[:, hl * 64:(hl + 1) * 64]             .bitcast(BF16).rearrange("p (k r) -> p k r", k=8)
        w1v = lambda b: wslab[:, b * 8192:b * 8192 + 4096].rearrange("p (k f) -> p k f", k=8)
        w2v = lambda b: wslab[:, b * 8192 + 4096:(b + 1) * 8192].rearrange("p (f h) -> p f h", f=4)
        w1p = lambda b, pl: wslab[:, b * 8192 + pl * 2048: b * 8192 + (pl + 1) * 2048]             .bitcast(BF16).rearrange("p (k f) -> p k f", k=8)
        w2p = lambda b, pl: wslab[:, b * 8192 + 4096 + pl * 2048: b * 8192 + 4096 + (pl + 1) * 2048]             .bitcast(BF16).rearrange("p (f h) -> p f h", f=4)
        h2p = lambda sl, pl: (vctm if pl == 0 else ztm)[:, sl, :]             .bitcast(BF16).rearrange("p (k t) -> p k t", k=8)
        hep = lambda pl: vsl[:, 1024:2048].bitcast(BF16)             .rearrange("p (f two t) -> p f two t", f=4, two=2)[:, :, pl, :]
        wstg = x2sl.ap().rearrange("p a b c -> p (a b c)")
        ppair = lambda half, pl: wslab[:, half * 4096 + pl * 2048:(half * 4096) + (pl + 1) * 2048]             .bitcast(BF16)
        wmv = lambda b: (vctm if b == 0 else ztm).ap().rearrange(
            "p a b -> p (a b)")[:, 0:2048].rearrange("p (k t) -> p k t", k=8)
        hcv = lambda sl: hsl.ap().rearrange("p (k t) -> p k t", k=8)
        vv = vsl.ap().rearrange("p (k t) -> p k t", k=8)
        xc3 = xcol.ap().rearrange("p (a b) -> p a b", b=1)
        ztm_p = lambda ct, hl: ztm[:, ct, hl * 512:(hl + 1) * 512].bitcast(BF16)
        s16 = vsl[:, 0:128].rearrange("p (k r) -> p k r", k=8)
        wrv = vsl[:, 144:208].rearrange("p (a b) -> p a b", a=8)
        initBTv = x2sl.ap().rearrange("p a b c -> p (a b c)")[:, 0:128]             .rearrange("p (k r) -> p k r", k=8)
        hem = lambda half: vsl[:, half * 1024:half * 1024 + 1024].rearrange("p (f t) -> p f t", f=4)
        mflat = maskv.ap().rearrange("p a b -> p (a b)")
        hx3 = mflat[:, 0:16].rearrange("p (k t) -> p k t", k=8)
        xnx3 = mflat[:, 16:32].rearrange("p (k t) -> p k t", k=8)
        xx3 = mflat[:, 32:48].rearrange("p (k t) -> p k t", k=8)

        e_sm = {}
        for name, dst in (("consts", conv), ("ones", onesv), ("eye", eyev),
                          ("cfm", cfmv), ("gam", gamv), ("bet", betv)):
            e_sm[name] = dma("sync", dW, dst[:], dt[name])
        e_ibt = dma("sync", dW, initBTv[:], dt["initBT"].rearrange("(kc p) r -> p kc r", p=128))
        e_ia = dma("sync", dW, initAv[:], dt["initA"].rearrange("(kc p) r -> p kc r", p=128))
        e_iat = dma("sync", dW, iA2T[:], dt["initAT"])
        e_ib = dma("sync", dW, iBv[:], dt["initB"])
        e_wr = dma("sync", dW, wrv[:], dt["wr"].rearrange("(kc p) e -> p kc e", p=128))
        e_onb = dma("sync", dW, onebv[:], dt["onesb"])
        e_sel = dma("sync", dW, selv[:], dt["sel"])
        e_wm = {}
        def load_wm(jg, waits=()):
            e_wm[jg] = dma("sync", dW, wmv(jg % 2)[:],
                           fm("wmod")[:, :, jg * 256:(jg + 1) * 256], waits=waits)
        load_wm(0)
        load_wm(1)

        x_edge = {}
        def load_x(idx, waits=()):
            src = fm("xf") if idx < 8 else fm("xo")
            c = idx % 8
            x_edge[idx] = dma("scalar", dX, xsl[:, idx % 2, :, :],
                              src[:, :, c * 256:(c + 1) * 256], waits=waits)
        load_x(0)
        load_x(1)
        e_wpt0 = dma("scalar", dP, wpf.rearrange("p (k h) -> p k h", k=8), fm("wpT"))
        e_xxt = dma("scalar", dT, xx3[:],
                    dt["xxt"].rearrange("(kc p) t -> p kc t", p=128), slow=True)

        S.wait("scalar", e_sm["cfm"])
        e_silu = vop(lambda e: e.activation(out=sfm[:], in_=cfmv[:], func=AF.Silu), "scalar")
        S.wait("vector", e_silu)
        for k in range(8):
            e_s16 = vc(lambda e, k=k: e.tensor_copy(s16[:, k, :],
                                                    bass.AP(sfm, k, [[8, 128], [0, 16]])))

        S.wait("vector", e_sm["eye"])
        e_eyeb = vc(lambda e: e.tensor_copy(eyeb[:], eyev[:]))
        S.wait("vector", e_wpt0)
        vc(lambda e: e.tensor_copy(wppf(0), wpf))
        e_wpt = vc(lambda e: e.tensor_tensor(wppf(1), wpf, wppf(0), ALU.subtract))
        S.wait("vector", e_ibt)
        ibtf = x2sl.ap().rearrange("p a b c -> p (a b c)")[:, 0:128]
        vc(lambda e: e.tensor_copy(ibtpf(0), ibtf))
        e_ibp = vc(lambda e: e.tensor_tensor(ibtpf(1), ibtf, ibtpf(0), ALU.subtract))

        ps_state = {"w1_store": None, "w2_store": None, "split_done": None,
                    "last_store": None, "last_load": None}
        PS_ITEMS = [((dt["wex1"][ei], w1s[ei], 0), (dt["wex2"][ei], w2s[ei], 1))
                    for ei in range(9)]
        PS_ITEMS.append(((dt["wbT"][:, 0:512], wbs[0], 0),
                         (dt["wbT"][:, 512:1024], wbs[1], 0)))
        PAIR_SRC = (wslab[:, 0:4096].bitcast(BF16).rearrange("p (pl k f) -> p pl k f", pl=2, k=8),
                    wslab[:, 4096:8192].bitcast(BF16).rearrange("p (pl f h) -> p pl f h", pl=2, f=4))

        def presplit_half(srcd, dstd, mode, qeng):
            stg3 = x2sl.ap().rearrange("p a b c -> p (a b c)").rearrange(
                "p (k f) -> p k f", k=(8 if mode == 0 else 4))
            e_ld = dma(qeng, dS, stg3,
                       srcd.rearrange("(kc p) f -> p kc f", p=128),
                       waits=[w for w in (ps_state["split_done"],) if w])
            ps_state["last_load"] = e_ld
            S.wait("scalar", e_ld)
            S.wait("scalar", ps_state["w1_store" if mode == 0 else "w2_store"])
            e_hi = vop(lambda e, mode=mode: e.activation(out=ppair(mode, 0), in_=wstg,
                                                         func=AF.Copy), "scalar")
            S.wait("vector", e_ld)
            S.wait("vector", e_hi)
            e_sp = vc(lambda e, mode=mode: e.tensor_tensor(ppair(mode, 1), wstg,
                                                           ppair(mode, 0), ALU.subtract))
            ps_state["split_done"] = e_sp
            e_st = dma("gpsimd", dE if mode == 0 else dF,
                       dstd.rearrange("pl (kc p) f -> p pl kc f", p=128),
                       PAIR_SRC[mode], waits=(e_sp,))
            ps_state["w1_store" if mode == 0 else "w2_store"] = e_st
            ps_state["last_store"] = e_st

        def presplit_item(item, qeng):
            for srcd, dstd, mode in item:
                presplit_half(srcd, dstd, mode, qeng)

        ps_state["split_done"] = e_ibp
        ps_state["w1_store"] = e_wpt
        ps_state["w2_store"] = e_wpt
        P0H = [half for ei in (5, 6, 7, 8) for half in PS_ITEMS[ei]]

        S.wait("tensor", e_s16)
        S.wait("tensor", e_sm["eye"])
        S.wait("tensor", e_sm["ones"])
        wm_pe = {}
        for jg in range(24):
            S.wait("tensor", e_wm[jg])
            bk, fe = BK.get()
            S.wait("tensor", fe)
            for k in range(8):
                fn = (lambda e, k=k, jg=jg, bk=bk: e.matmul(
                    ps[bk][0:16, 0:256], s16[:, k, :], wmv(jg % 2)[:, k, :],
                    start=(k == 0), stop=(k == 7)))
                e_mm = S.op("tensor", fn, inc=spe) if k == 7 else S.op("tensor", fn)
            wm_pe[jg] = e_mm
            if jg + 2 < 24:
                load_wm(jg + 2, waits=(e_mm,))
            S.wait("vector", e_mm)
            e_ev = vop(lambda e, bk=bk: e.tensor_copy(prji[:], ps[bk][0:16, 0:256]))
            BK.put(bk, e_ev)
            bk2, fe2 = BK.get()
            S.wait("tensor", fe2)
            S.wait("tensor", e_ev)
            for t2 in range(2):
                fn = (lambda e, bk2=bk2, t2=t2: e.transpose(
                    ps[bk2][:, t2 * 16:(t2 + 1) * 16], prji[:, t2 * 128:(t2 + 1) * 128],
                    eyev[0:16, 0:16]))
                e_tr = S.op("tensor", fn, inc=spe) if t2 == 1 else S.op("tensor", fn)
            S.wait("vector", e_tr)
            e_mod = vop(lambda e, bk2=bk2, jg=jg: e.tensor_copy(
                modv[:, jg * 2:(jg + 1) * 2], bass.AP(ps[bk2], 0, [[512, 128], [16, 2]])))
            BK.put(bk2, e_mod)
            if jg >= 8 and jg % 2 == 0:
                presplit_half(*P0H[(jg - 8) // 2], qeng="scalar")

        e_wmod_pe = (spe, spe.n)
        e_wbt = None

        S.wait("vector", e_mod)
        vc(lambda e: e.tensor_scalar_add(dmod[:, 8:16], modv[:, 8:16], 1.0))
        vc(lambda e: e.tensor_scalar_add(dmod[:, 32:40], modv[:, 32:40], 1.0))
        for src_c in (16, 40):
            vc(lambda e, s=src_c: e.tensor_single_scalar(rtr[:, 0, 0:8], modv[:, s:s + 8], 0.0, ALU.is_ge))
            vc(lambda e: e.tensor_scalar(out=rtr[:, 0, 0:8], in0=rtr[:, 0, 0:8], scalar1=2.0,
                                         scalar2=-1.0, op0=ALU.mult, op1=ALU.add))
            vc(lambda e, s=src_c: e.tensor_tensor(rtr[:, 1, 0:8], modv[:, s:s + 8], rtr[:, 0, 0:8], ALU.mult))
            vc(lambda e: e.tensor_scalar_max(rtr[:, 1, 0:8], rtr[:, 1, 0:8], 0.25))
            e_dmod = vc(lambda e, s=src_c: e.tensor_tensor(dmod[:, s:s + 8], rtr[:, 1, 0:8],
                                                           rtr[:, 0, 0:8], ALU.mult))
        if DEBUG_TAPS:
            dma("gpsimd", dG, dbg4_t[:, 0:48], modv[:], waits=(e_dmod,))
            dma("gpsimd", dG, dbg4_t[:, 48:96], dmod[:], waits=(e_dmod,))
        S.wait("vector", e_wr)
        vc(lambda e: e.tensor_copy(wrp[:, 0, :, :].rearrange("p a b -> p (a b)"),
                                   wrv[:].rearrange("p a b -> p (a b)")))
        e_wrp = vc(lambda e: e.tensor_tensor(wrp[:, 1, :, :].rearrange("p a b -> p (a b)"),
                                             wrv[:].rearrange("p a b -> p (a b)"),
                                             wrp[:, 0, :, :].rearrange("p a b -> p (a b)"), ALU.subtract))
        S.wait("vector", e_iat)
        e_ia2 = vc(lambda e: e.tensor_scalar_mul(iA2T[:], iA2T[:], 2.0))
        S.wait("vector", e_ib)
        vop(lambda e: e.memset(accA[:], 0.0))
        e_zacc = vop(lambda e: e.memset(accB[:], 0.0))

        state = {"sq_user": e_zacc}

        def ln_stats(src_view, W, deps=(), pair=None):
            bk1, f1 = BK.get()
            bk2, f2 = BK.get()
            for ed in deps:
                S.wait("tensor", ed)
                S.wait("scalar", ed)
                S.wait("vector", ed)
            S.wait("tensor", f1)
            S.wait("tensor", f2)
            if pair is None:
                for k in range(8):
                    fn = (lambda e, k=k, bk1=bk1: e.matmul(ps[bk1][:, 0:W], onesv[:], src_view(k),
                                                           start=(k == 0), stop=(k == 7)))
                    e_s1 = S.op("tensor", fn, inc=spe) if k == 7 else S.op("tensor", fn)
            else:
                S.wait("tensor", e_onb)
                for k in range(8):
                    for hl in range(2):
                        fn = (lambda e, k=k, hl=hl, bk1=bk1: e.matmul(
                            ps[bk1][:, 0:W], onebv[:], pair(k, hl),
                            start=(k == 0 and hl == 0), stop=(k == 7 and hl == 1)))
                        e_s1 = S.op("tensor", fn, inc=spe) if (k == 7 and hl == 1) \
                            else S.op("tensor", fn)
            S.wait("scalar", state["sq_user"])
            sq_pe = []
            for k in range(8):
                if k >= 2:
                    S.wait("scalar", sq_pe[k - 2])
                e_sq = vop(lambda e, k=k: e.activation(out=sq[:, k % 2, 0:W], in_=src_view(k),
                                                       func=AF.Square), "scalar")
                S.wait("tensor", e_sq)
                fn = (lambda e, k=k, bk2=bk2: e.matmul(ps[bk2][:, 0:W], onesv[:], sq[:, k % 2, 0:W],
                                                       start=(k == 0), stop=(k == 7)))
                ed = S.op("tensor", fn, inc=spe)
                sq_pe.append(ed)
            e_s2 = sq_pe[-1]
            S.wait("vector", e_s1)
            S.wait("vector", e_s2)
            vc(lambda e, bk1=bk1: e.tensor_scalar(out=stats[:, 0, 0:W], in0=ps[bk1][:, 0:W],
                                                  scalar1=conv[:, C_INV1024:C_INV1024 + 1],
                                                  scalar2=None, op0=ALU.mult))
            e_e1 = vc(lambda e, bk2=bk2: e.tensor_scalar(out=stats[:, 1, 0:W], in0=ps[bk2][:, 0:W],
                                                         scalar1=conv[:, C_INV1024:C_INV1024 + 1],
                                                         scalar2=None, op0=ALU.mult))
            BK.put(bk1, e_e1)
            BK.put(bk2, e_e1)
            vc(lambda e: e.tensor_tensor(stats[:, 2, 0:W], stats[:, 0, 0:W], stats[:, 0, 0:W], ALU.mult))
            e_var = vc(lambda e: e.tensor_tensor(stats[:, 1, 0:W], stats[:, 1, 0:W],
                                                 stats[:, 2, 0:W], ALU.subtract))
            S.wait("scalar", e_var)
            e_std = vc(lambda e: e.activation(out=stats[:, 2, 0:W], in_=stats[:, 1, 0:W], func=AF.Sqrt,
                                              bias=conv[:, C_LNEPS:C_LNEPS + 1], scale=1.0), "scalar")
            S.wait("vector", e_std)
            e_r = vc(lambda e: e.reciprocal(stats[:, 2, 0:W], stats[:, 2, 0:W]))
            e_nm = vc(lambda e: e.tensor_tensor(stats[:, 3, 0:W], stats[:, 0, 0:W],
                                                stats[:, 2, 0:W], ALU.mult))
            return e_nm, e_s2

        def apply_ln(dst_view, src_view, W, sc, sh, e_stats, e_pe, pre_waits=(),
                     u_buf=None, src3=None):
            S.wait("vector", e_stats)
            S.wait("vector", e_pe)
            for ed in pre_waits:
                S.wait("vector", ed)
            sc_t, sc_c = sc
            sh_t, sh_c = sh
            if src3 is not None:
                u3, uk = u_buf
                rb3 = bass.AP(stats, 2 * 256, [[1024, 128], [0, 8], [1, W]])
                nb3 = bass.AP(stats, 3 * 256, [[1024, 128], [0, 8], [1, W]])
                vc(lambda e: e.tensor_tensor(u3, src3, rb3, ALU.mult))
                vc(lambda e: e.tensor_tensor(u3, u3, nb3, ALU.subtract))
                last = None
                for k in range(8):
                    fn = (lambda e, k=k: e.tensor_scalar(
                        out=dst_view(k), in0=uk(k),
                        scalar1=sc_t[:, sc_c + k:sc_c + k + 1],
                        scalar2=sh_t[:, sh_c + k:sh_c + k + 1], op0=ALU.mult, op1=ALU.add))
                    last = vc(fn) if k == 7 else vop(fn)
                return last
            last = None
            for k in range(8):
                rb = bass.AP(stats, 2 * 256, [[1024, 128], [1, W]])
                nb = bass.AP(stats, 3 * 256, [[1024, 128], [1, W]])
                vc(lambda e, k=k, rb=rb: e.tensor_tensor(sq[:, k % 2, 0:W], src_view(k), rb, ALU.mult))
                vc(lambda e, k=k, nb=nb: e.tensor_tensor(sq[:, k % 2, 0:W], sq[:, k % 2, 0:W], nb, ALU.subtract))
                last = vc(lambda e, k=k: e.tensor_scalar(
                    out=dst_view(k), in0=sq[:, k % 2, 0:W],
                    scalar1=sc_t[:, sc_c + k:sc_c + k + 1],
                    scalar2=sh_t[:, sh_c + k:sh_c + k + 1], op0=ALU.mult, op1=ALU.add))
            state["sq_user"] = last
            return last

        if STOP_AT == "p0":
            return finish()
        S.wait("vector", e_xxt)
        S.wait("vector", e_dmod)
        xxv = lambda k: xx3[:, k, :]
        e_st, e_pe0 = ln_stats(xxv, 2, deps=(e_xxt, e_sm["consts"]))
        e_hx = apply_ln(lambda k: hx3[:, k, :], xxv, 2, (dmod, 8), (dmod, 0), e_st, e_pe0)
        hxv = lambda k: hx3[:, k, :]
        e_st2, e_pe1 = ln_stats(hxv, 2, deps=(e_hx, e_sm["gam"], e_sm["bet"]))
        e_xnx = apply_ln(lambda k: xnx3[:, k, :], hxv, 2, (gamv, 0), (betv, 0), e_st2, e_pe1)
        e_xnx = vc(lambda e: e.tensor_scalar(out=xnx3[:, :, 0:1], in0=xnx3[:, :, 0:1],
                                             scalar1=conv[:, C_XTRAMASK:C_XTRAMASK + 1],
                                             scalar2=None, op0=ALU.mult))

        if STOP_AT == "p1":
            return finish()
        h_free = {-2: e_wmod_pe, -1: e_wmod_pe}
        xn_free = {-2: None, -1: None}
        P = {}

        def part1a(idx):
            me = {}
            xvw = lambda k: xsl[:, idx % 2, k, :]
            x3 = xsl[:, idx % 2, :, :]
            e_st, e_pe_a = ln_stats(xvw, 256, deps=(x_edge[idx], e_dmod))
            e_ap = apply_ln(lambda k: hcv(idx % 2)[:, k, :], xvw, 256, (dmod, 8), (dmod, 0),
                            e_st, e_pe_a, pre_waits=(h_free[idx - 2],),
                            u_buf=(vv[:, :, :], lambda k: vv[:, k, :]), src3=x3)
            h3f = hsl[:, :]
            S.wait("scalar", e_ap)
            S.wait("scalar", h_free[idx - 2])
            e_hi = vop(lambda e, idx=idx: e.activation(out=hpb(idx % 2, 0), in_=h3f,
                                                       func=AF.Copy), "scalar")
            S.wait("vector", e_hi)
            me["h"] = vc(lambda e, idx=idx: e.tensor_tensor(hpb(idx % 2, 1), h3f,
                                                            hpb(idx % 2, 0), ALU.subtract))
            return me

        def part1b(idx, e_h):
            me = {}
            hvw = lambda k: hcv(idx % 2)[:, k, :]
            h3 = hcv(idx % 2)[:, :, :]
            e_st2, e_pe_b = ln_stats(hvw, 256, deps=(e_h, e_sm["gam"], e_sm["bet"]),
                                     pair=lambda k, hl, idx=idx: hpv(idx % 2, hl)[:, k, :])
            if idx not in (0, 8):
                S.wait("vector", e_st2)
                rb0 = bass.AP(stats, 2 * 256, [[1024, 128], [0, 8], [1, 1]])
                nb0 = bass.AP(stats, 3 * 256, [[1024, 128], [0, 8], [1, 1]])
                vc(lambda e: e.tensor_tensor(sq[:, 0, 0:8].rearrange("p (a b) -> p a b", b=1),
                                             hcv(idx % 2)[:, :, 0:1], rb0, ALU.mult))
                vc(lambda e: e.tensor_tensor(sq[:, 0, 0:8].rearrange("p (a b) -> p a b", b=1),
                                             sq[:, 0, 0:8].rearrange("p (a b) -> p a b", b=1),
                                             nb0, ALU.subtract))
                e_pk = None
                for k in range(8):
                    fn = (lambda e, k=k: e.tensor_scalar(
                        out=xcol[:, k:k + 1], in0=sq[:, 0, k:k + 1],
                        scalar1=gamv[:, k:k + 1], scalar2=betv[:, k:k + 1],
                        op0=ALU.mult, op1=ALU.add))
                    e_pk = vc(fn) if k == 7 else vop(fn)
                state["sq_user"] = e_pk
                psl = (idx - 1) % 2
                vc(lambda e, psl=psl: e.tensor_copy(xnp(psl, 0)[:, :, 255:256], xc3))
                me["patch"] = vc(lambda e, psl=psl: e.tensor_tensor(
                    xnp(psl, 1)[:, :, 255:256], xc3,
                    xnp(psl, 0)[:, :, 255:256], ALU.subtract))
            e_apx = apply_ln(lambda k: hcv(0)[:, k, :], hvw, 256, (gamv, 0), (betv, 0),
                             e_st2, e_pe_b,
                             u_buf=(vv[:, :, :], lambda k: vv[:, k, :]), src3=h3)
            S.wait("scalar", e_apx)
            S.wait("scalar", xn_free[idx - 2])
            e_hix = vop(lambda e, idx=idx: e.activation(out=xnp(idx % 2, 0)[:, :, 0:255],
                                                        in_=hcv(0)[:, :, 1:256],
                                                        func=AF.Copy), "scalar")
            S.wait("vector", e_hix)
            me["xn"] = vc(lambda e, idx=idx: e.tensor_tensor(
                xnp(idx % 2, 1)[:, :, 0:255], hcv(0)[:, :, 1:256],
                xnp(idx % 2, 0)[:, :, 0:255], ALU.subtract))
            if "patch" not in me:
                me["patch"] = me["xn"]
            return me

        def patch_last(idx):
            col = 1 if idx == 7 else 0
            S.wait("vector", e_xnx)
            vc(lambda e, col=col: e.tensor_copy(xc3, xnx3[:, :, col:col + 1]))
            vc(lambda e, idx=idx: e.tensor_copy(xnp(idx % 2, 0)[:, :, 255:256], xc3))
            return vc(lambda e, idx=idx: e.tensor_tensor(
                xnp(idx % 2, 1)[:, :, 255:256], xc3,
                xnp(idx % 2, 0)[:, :, 255:256], ALU.subtract))

        def partZ(idx, e_h):
            me = {}
            S.wait("tensor", e_h)
            S.wait("tensor", e_eyeb)
            for ct in range(2):
                for fg in range(2):
                    for hl in range(2):
                        bkt, fet = BK.get()
                        S.wait("tensor", fet)
                        for ff in range(4):
                            f = fg * 4 + ff
                            fn = (lambda e, bkt=bkt, ct=ct, f=f, ff=ff, hl=hl, idx=idx:
                                  e.matmul(ps[bkt][:, ff * 128:(ff + 1) * 128],
                                           hpv(idx % 2, hl)[:, f, ct * 128:ct * 128 + 128],
                                           eyeb[:], start=True, stop=True))
                            e_t = S.op("tensor", fn, inc=spe) if ff == 3 else S.op("tensor", fn)
                        S.wait("vector", e_t)
                        ed = vop(lambda e, bkt=bkt, ct=ct, fg=fg, hl=hl: e.tensor_copy(
                            ztm_p(ct, hl)[:, fg * 512:(fg + 1) * 512], ps[bkt][:, 0:512]))
                        BK.put(bkt, ed)
            bkp, fep = BK.get()
            S.wait("tensor", fep)
            S.wait("tensor", e_ibp)
            for k in range(8):
                for pi, (pa, pb) in enumerate(((0, 0), (0, 1), (1, 0))):
                    fn = (lambda e, k=k, bkp=bkp, pa=pa, pb=pb, pi=pi, idx=idx: e.matmul(
                        ps[bkp][0:16, 0:256], ibtp(pa)[:, k, :], hpv(idx % 2, pb)[:, k, :],
                        start=(k == 0 and pi == 0), stop=(k == 7 and pi == 2)))
                    e_pi = S.op("tensor", fn, inc=spe) if (k == 7 and pi == 2) \
                        else S.op("tensor", fn)
            me["hpe"] = e_pi
            S.wait("vector", e_pi)
            e_piv = vop(lambda e, bkp=bkp: e.tensor_copy(prji[:], ps[bkp][0:16, 0:256]))
            BK.put(bkp, e_piv)
            bkq, feq = BK.get()
            S.wait("tensor", feq)
            S.wait("tensor", e_piv)
            for ct in range(2):
                fn = (lambda e, bkq=bkq, ct=ct: e.transpose(
                    ps[bkq][:, ct * 16:(ct + 1) * 16], prji[:, ct * 128:(ct + 1) * 128],
                    eyev[0:16, 0:16]))
                e_pt = S.op("tensor", fn, inc=spe) if ct == 1 else S.op("tensor", fn)
            S.wait("vector", e_pt)
            e_ptm = vop(lambda e, bkq=bkq: e.tensor_copy(ptmi[:].rearrange("p a b -> p (a b)"),
                                                         ps[bkq][:, 0:32]))
            BK.put(bkq, e_ptm)
            me["ptmi"] = e_ptm
            return me

        def part2(idx, e_xn_ready):
            me = {}
            fh = idx < 8
            for f in range(8):
                if f % 2 == 0:
                    bkv, fev = BK.get()
                    S.wait("tensor", fev)
                    if f == 0:
                        S.wait("tensor", e_xn_ready)
                        S.wait("tensor", e_wpt)
                for k in range(8):
                    for pi, (pa, pb) in enumerate(((0, 0), (0, 1), (1, 0))):
                        last = (k == 7 and f % 2 == 1 and pi == 2)
                        fn = (lambda e, f=f, k=k, bkv=bkv, pa=pa, pb=pb, pi=pi, idx=idx,
                              last=last: e.matmul(
                            ps[bkv][:, (f % 2) * 256:(f % 2) * 256 + 256],
                            wpp(pa)[:, k, f * 128:(f + 1) * 128],
                            xnp(idx % 2, pb)[:, k, :],
                            start=(k == 0 and f % 2 == 0 and pi == 0), stop=last))
                        if last:
                            e_v = S.op("tensor", fn, inc=spe)
                        else:
                            S.op("tensor", fn)
                if f % 2 == 1:
                    S.wait("vector", e_v)
                    e_vev = vop(lambda e, bkv=bkv, f=f: e.tensor_copy(
                        vsl[:, (f - 1) * 256:(f + 1) * 256], ps[bkv][:, 0:512]))
                    BK.put(bkv, e_vev)
            me["xnpe"] = e_v
            bkp, fep = BK.get()
            S.wait("tensor", fep)
            S.wait("tensor", e_vev)
            S.wait("tensor", e_ia)
            for k in range(8):
                fn = (lambda e, k=k, bkp=bkp: e.matmul(
                    ps[bkp][0:16, 0:256], initAv[:, k, :], vv[:, k, :],
                    start=(k == 0), stop=(k == 7)))
                e_pe2 = S.op("tensor", fn, inc=spe) if k == 7 else S.op("tensor", fn)
            S.wait("vector", e_pe2)
            e_pev = vop(lambda e, bkp=bkp: e.tensor_copy(prje[:], ps[bkp][0:16, 0:256]))
            BK.put(bkp, e_pev)
            bkq, feq = BK.get()
            S.wait("tensor", feq)
            S.wait("tensor", e_pev)
            for ct in range(2):
                fn = (lambda e, bkq=bkq, ct=ct: e.transpose(
                    ps[bkq][:, ct * 16:(ct + 1) * 16], prje[:, ct * 128:(ct + 1) * 128],
                    eyev[0:16, 0:16]))
                e_pt = S.op("tensor", fn, inc=spe) if ct == 1 else S.op("tensor", fn)
            S.wait("vector", e_pt)
            e_pf = vc(lambda e, bkq=bkq: e.tensor_copy(ptmef[:], ps[bkq][:, 0:32]))
            BK.put(bkq, e_pf)
            vc(lambda e: e.tensor_copy(ptmep(0), ptmef[:]))
            e_ptme = vc(lambda e: e.tensor_tensor(ptmep(1), ptmef[:],
                                                  ptmep(0), ALU.subtract))
            for ct in range(2):
                for fg in range(2):
                    bkt, fet = BK.get()
                    S.wait("tensor", fet)
                    for ff in range(4):
                        f = fg * 4 + ff
                        fn = (lambda e, bkt=bkt, ct=ct, f=f, ff=ff: e.transpose(
                            ps[bkt][:, ff * 128:(ff + 1) * 128], vv[:, f, ct * 128:ct * 128 + 128],
                            eyev[:]))
                        e_t = S.op("tensor", fn, inc=spe) if ff == 3 else S.op("tensor", fn)
                    S.wait("vector", e_t)
                    e_vc = vop(lambda e, bkt=bkt, ct=ct, fg=fg: e.tensor_copy(
                        vctm[:, ct, fg * 512:(fg + 1) * 512], ps[bkt][:, 0:512]))
                    BK.put(bkt, e_vc)
            acc_last = None

            def acc_upd(bkd, hg, acc_t):
                if fh:
                    return vop(lambda e, bkd=bkd, hg=hg, acc_t=acc_t: e.scalar_tensor_tensor(
                        out=acc_t[:, hg * 512:(hg + 1) * 512], in0=ps[bkd][0:16, 0:512],
                        scalar=conv[0:16, C_FHMASK:C_FHMASK + 1],
                        in1=acc_t[:, hg * 512:(hg + 1) * 512], op0=ALU.mult, op1=ALU.add))
                return vop(lambda e, bkd=bkd, hg=hg, acc_t=acc_t: e.tensor_tensor(
                    acc_t[:, hg * 512:(hg + 1) * 512], acc_t[:, hg * 512:(hg + 1) * 512],
                    ps[bkd][0:16, 0:512], ALU.add))

            for hg in range(2):
                bkd, fed = BK.get()
                S.wait("tensor", fed)
                S.wait("tensor", e_vc)
                for ct in range(2):
                    fn = (lambda e, bkd=bkd, ct=ct, hg=hg: e.matmul(
                        ps[bkd][0:16, 0:512], ptmi[:, ct, :],
                        vctm[:, ct, hg * 512:(hg + 1) * 512],
                        start=(ct == 0), stop=(ct == 1)))
                    e_d = S.op("tensor", fn, inc=spe) if ct == 1 else S.op("tensor", fn)
                S.wait("vector", e_d)
                acc_last = acc_upd(bkd, hg, accA)
                BK.put(bkd, acc_last)
            for hg in range(2):
                bkd, fed = BK.get()
                S.wait("tensor", fed)
                for ct in range(2):
                    for pi, (pa, pb) in enumerate(((0, 0), (0, 1), (1, 0))):
                        fn = (lambda e, bkd=bkd, ct=ct, hg=hg, pa=pa, pb=pb, pi=pi: e.matmul(
                            ps[bkd][0:16, 0:512], ptmep(pa)[:, ct * 16:(ct + 1) * 16],
                            ztm_p(ct, pb)[:, hg * 512:(hg + 1) * 512],
                            start=(ct == 0 and pi == 0), stop=(ct == 1 and pi == 2)))
                        e_d = S.op("tensor", fn, inc=spe) if (ct == 1 and pi == 2) \
                            else S.op("tensor", fn)
                S.wait("vector", e_d)
                acc_last = acc_upd(bkd, hg, accB)
                BK.put(bkd, acc_last)
            me["acc"] = acc_last
            return me

        def part3(idx, e_h_ready, e_acc_prev, e_spill_prev):
            s = idx - 8
            me = {}
            S.wait("vector", e_acc_prev)
            S.wait("vector", e_zacc)
            vc(lambda e: e.tensor_tensor(a2ef[:], accA[:], accA[:], ALU.mult))
            vc(lambda e: e.tensor_reduce(c16[:, 0:1], a2ef[:], AX.X, ALU.add))
            vc(lambda e: e.tensor_tensor(beff[:], accB[:], accB[:], ALU.mult))
            vc(lambda e: e.tensor_reduce(c16[:, 1:2], beff[:], AX.X, ALU.add))
            e_rs = vc(lambda e: e.tensor_tensor(c16[:, 2:3], c16[:, 0:1], c16[:, 1:2], ALU.add))
            bkc, fec = BK.get()
            S.wait("tensor", fec)
            S.wait("tensor", e_rs)
            e_cs = S.op("tensor", lambda e, bkc=bkc: e.matmul(
                ps[bkc][0:16, 0:1], onesv[0:16, 0:16], c16[:, 2:3], start=True, stop=True), inc=spe)
            S.wait("scalar", e_cs)
            e_nrm = vop(lambda e, bkc=bkc: e.activation(out=c16[:, 3:4], in_=ps[bkc][0:16, 0:1],
                                                        func=AF.Sqrt,
                                                        bias=conv[0:16, C_ZERO:C_ZERO + 1],
                                                        scale=conv[0:16, C_LR2:C_LR2 + 1]), "scalar")
            BK.put(bkc, e_nrm)
            S.wait("vector", e_nrm)
            vc(lambda e: e.tensor_scalar(out=c16[:, 3:4], in0=c16[:, 3:4],
                                         scalar1=conv[0:16, C_CLIPEPS:C_CLIPEPS + 1],
                                         scalar2=None, op0=ALU.add))
            vc(lambda e: e.reciprocal(c16[:, 4:5], c16[:, 3:4]))
            vc(lambda e: e.tensor_scalar_min(c16[:, 4:5], c16[:, 4:5], 1.0))
            vc(lambda e: e.tensor_scalar(out=c16[:, 5:6], in0=c16[:, 4:5],
                                         scalar1=conv[0:16, C_N2LR:C_N2LR + 1], scalar2=None, op0=ALU.mult))
            vc(lambda e: e.tensor_scalar(out=c16[:, 6:7], in0=c16[:, 4:5],
                                         scalar1=conv[0:16, C_NLR:C_NLR + 1], scalar2=None, op0=ALU.mult))
            S.wait("vector", e_ia2)
            vc(lambda e: e.scalar_tensor_tensor(out=a2ef[:], in0=accA[:], scalar=c16[:, 5:6],
                                                in1=iA2T[:], op0=ALU.mult, op1=ALU.add))
            e_ab = vc(lambda e: e.scalar_tensor_tensor(out=beff[:], in0=accB[:], scalar=c16[:, 6:7],
                                                       in1=iBv[:], op0=ALU.mult, op1=ALU.add))
            bkb, feb = BK.get()
            S.wait("tensor", feb)
            S.wait("tensor", e_ab)
            for k in range(8):
                fn = (lambda e, bkb=bkb, k=k: e.transpose(
                    ps[bkb][:, k * 16:(k + 1) * 16], beff[:, k * 128:(k + 1) * 128], eyev[0:16, 0:16]))
                e_bt = S.op("tensor", fn, inc=spe) if k == 7 else S.op("tensor", fn)
            S.wait("vector", e_bt)
            e_bf = vc(lambda e, bkb=bkb: e.tensor_copy(btf_w[:], ps[bkb][:, 0:128]))
            BK.put(bkb, e_bf)
            vc(lambda e: e.tensor_copy(btpf(0), btf_w[:]))
            e_btp = vc(lambda e: e.tensor_tensor(btpf(1), btf_w[:],
                                                 btpf(0), ALU.subtract))
            base_banks = []
            for bg in range(4):
                bkx, fex = BK.get()
                S.wait("tensor", fex)
                base_banks.append(bkx)
                if bg == 0:
                    S.wait("tensor", e_h_ready)
                    S.wait("tensor", e_wbt)
                for half in range(2):
                    hh = bg * 2 + half
                    for k in range(8):
                        for pi, (pa, pb) in enumerate(((0, 0), (0, 1), (1, 0))):
                            S.op("tensor", (lambda e, bkx=bkx, hh=hh, half=half, k=k,
                                            pa=pa, pb=pb, pi=pi, idx=idx: e.matmul(
                                ps[bkx][:, half * 256:half * 256 + 256],
                                wbp(pa)[:, k, hh * 128:(hh + 1) * 128],
                                hpv(idx % 2, pb)[:, k, :],
                                start=(k == 0 and half == 0 and pi == 0), stop=False)))
            bkm, fem = BK.get()
            S.wait("tensor", fem)
            S.wait("tensor", e_btp)
            for k in range(8):
                for pi, (pa, pb) in enumerate(((0, 0), (0, 1), (1, 0))):
                    fn = (lambda e, bkm=bkm, k=k, pa=pa, pb=pb, pi=pi, idx=idx: e.matmul(
                        ps[bkm][0:16, 0:256], btp(pa)[:, k, :], hpv(idx % 2, pb)[:, k, :],
                        start=(k == 0 and pi == 0), stop=(k == 7 and pi == 2)))
                    e_md = S.op("tensor", fn, inc=spe) if (k == 7 and pi == 2) \
                        else S.op("tensor", fn)
            me["hpe"] = e_md
            S.wait("vector", e_md)
            e_mv = vop(lambda e, bkm=bkm: e.tensor_copy(mids[:], ps[bkm][0:16, 0:256]))
            BK.put(bkm, e_mv)
            S.wait("tensor", e_mv)
            e_l = {}
            for bg in range(4):
                for half in range(2):
                    hh = bg * 2 + half
                    fn = (lambda e, bg=bg, half=half, hh=hh: e.matmul(
                        ps[base_banks[bg]][:, half * 256:half * 256 + 256],
                        a2ef[:, hh * 128:(hh + 1) * 128], mids[:], start=False, stop=(half == 1)))
                    e_l[bg] = S.op("tensor", fn, inc=spe) if half == 1 else S.op("tensor", fn)
            x2e = []
            for bg in range(4):
                S.wait("vector", e_l[bg])
                if bg == 0:
                    S.wait("vector", e_spill_prev)
                    S.wait("vector", ps_state["last_load"] if s == 0 else None)
                for half in range(2):
                    hh = bg * 2 + half
                    ed = vop(lambda e, bg=bg, half=half, hh=hh, s=s: e.scalar_tensor_tensor(
                        out=x2sl[:, s % 2, hh, :],
                        in0=ps[base_banks[bg]][:, half * 256:half * 256 + 256],
                        scalar=dmod[:, 16 + hh:16 + hh + 1],
                        in1=xsl[:, (s + 8) % 2, hh, :], op0=ALU.mult, op1=ALU.add))
                    x2e.append(ed)
                BK.put(base_banks[bg], x2e[-1])
            me["x2"] = x2e[-1]
            me["spill"] = dma("gpsimd", dG,
                              x2sp.rearrange("(kc p) t -> p kc t", p=128)[:, :, s * 256:(s + 1) * 256],
                              x2sl[:, s % 2, :, :], waits=(me["x2"],))
            return me

        P[0] = part1a(0)
        P[0].update(part1b(0, P[0]["h"]))
        Z = {0: partZ(0, P[0]["h"])}
        h_free[0] = Z[0]["hpe"]
        PRESPLIT = {0: (9,), 1: (0,), 2: (1,), 3: (2,), 4: (3,), 5: (4,)}
        for s in range(8):
            nxt = s + 1
            P[nxt] = part1a(nxt)
            P[nxt].update(part1b(nxt, P[nxt]["h"]))
            if s < 7:
                e_xn_ready = P[nxt]["patch"]
                load_x(s + 2, waits=(P[s]["h"],))
            else:
                load_x(9, waits=(P[7]["h"],))
                e_xn_ready = patch_last(7)
            m2 = part2(s, e_xn_ready)
            P[s].update(m2)
            xn_free[s] = m2["xnpe"]
            Z[nxt] = partZ(nxt, P[nxt]["h"])
            h_free[nxt] = Z[nxt]["hpe"]
            for ei in PRESPLIT.get(s, ()):
                presplit_item(PS_ITEMS[ei], "sync" if ei in (9, 1, 3) else "scalar")
        for half in range(2):
            for pl in range(2):
                e_wbt = dma("sync", dW,
                            wbp(pl)[:, :, half * 512:(half + 1) * 512],
                            wbs[half, pl].rearrange("(kc p) f -> p kc f", p=128),
                            waits=(ps_state["w1_store"], ps_state["w2_store"]))
        if STOP_AT == "fh":
            return finish()
        prev_acc = P[7]["acc"]
        spill_hist = {}
        for s in range(8):
            idx = 8 + s
            if s < 7:
                P[idx + 1] = part1a(idx + 1)
            m3 = part3(idx, P[idx]["h"], prev_acc, spill_hist.get(idx - 2))
            P[idx].update(m3)
            h_free[idx] = m3["hpe"]
            spill_hist[idx] = m3["spill"]
            if s < 6:
                load_x(idx + 2, waits=(m3["x2"],))
            if s < 7:
                P[idx + 1].update(part1b(idx + 1, P[idx + 1]["h"]))
                e_xn_ready = P[idx + 1]["patch"]
            else:
                e_xn_ready = patch_last(15)
            m2 = part2(idx, e_xn_ready)
            P[idx].update(m2)
            xn_free[idx] = m2["xnpe"]
            prev_acc = m2["acc"]
            if s < 7:
                Z[idx + 1] = partZ(idx + 1, P[idx + 1]["h"])

        e_attn_pe = (spe, spe.n)
        if STOP_AT == "attn":
            return finish()

        dbg_backedge = {}
        x2_edge = {}
        def load_x2(g, waits):
            x2_edge[g] = dma("scalar", dX, x2sl[:, g % 2, :, :],
                             x2sp.rearrange("(kc p) t -> p kc t", p=128)[:, :, g * 256:(g + 1) * 256],
                             waits=waits)
        load_x2(0, (P[14]["spill"], P[8]["spill"]))
        load_x2(1, (P[15]["spill"], P[9]["spill"]))

        wex_edge = {}
        first_wex = [True]
        def load_wex(g, ei, buf, waits=()):
            deps = list(waits)
            if first_wex[0]:
                deps.append(e_attn_pe)
                first_wex[0] = False
            e1 = dma("sync", dW,
                     wslab[:, buf * 8192:buf * 8192 + 4096].bitcast(BF16)
                     .rearrange("p (pl k f) -> p pl k f", pl=2, k=8),
                     w1s[ei].rearrange("pl (kc p) f -> p pl kc f", p=128), waits=deps)
            wex_edge[(g, ei)] = dma("sync", dW,
                                    wslab[:, buf * 8192 + 4096:(buf + 1) * 8192].bitcast(BF16)
                                    .rearrange("p (pl f h) -> p pl f h", pl=2, f=4),
                                    w2s[ei].rearrange("pl (fc p) h -> p pl fc h", p=128))

        load_wex(0, 0, 0)
        load_wex(0, 1, 1)

        h2_free = {-2: None, -1: None}
        mask_gp = {0: None, 1: None}
        wex_pe = {0: None, 1: None}
        out_dma = []
        for g in range(8):
            xv2 = lambda k, g=g: x2sl[:, g % 2, k, :]
            e_st, e_pe_g = ln_stats(xv2, 256, deps=(x2_edge[g],))
            e_h2 = apply_ln(lambda k, g=g: hcv(g % 2)[:, k, :], xv2, 256, (dmod, 32), (dmod, 24),
                            e_st, e_pe_g,
                            pre_waits=(dbg_backedge.get("h2") if DEBUG_TAPS else None,))
            S.wait("vector", h2_free[g - 2])
            if g == 0:
                S.wait("vector", e_attn_pe)
            vc(lambda e, g=g: e.tensor_copy(h2p(g % 2, 0).rearrange("p a b -> p (a b)"),
                                            hcv(g % 2)[:, :, :].rearrange("p a b -> p (a b)")))
            e_h2s = vc(lambda e, g=g: e.tensor_tensor(
                h2p(g % 2, 1).rearrange("p a b -> p (a b)"),
                hcv(g % 2)[:, :, :].rearrange("p a b -> p (a b)"),
                h2p(g % 2, 0).rearrange("p a b -> p (a b)"), ALU.subtract))
            if DEBUG_TAPS:
                dbg_backedge["h2"] = dma(
                    "gpsimd", dG, dbg_t.rearrange("(kc p) t -> p kc t", p=128)[:, :, g * 256:(g + 1) * 256],
                    hcv(g % 2)[:, :, :], waits=(e_h2,))
            bkr, fer = BK.get()
            S.wait("tensor", fer)
            S.wait("tensor", e_h2s)
            S.wait("tensor", e_wrp)
            for k in range(8):
                for pw, ph in ((0, 0), (0, 1), (1, 0)):
                    fn = (lambda e, bkr=bkr, k=k, g=g, pw=pw, ph=ph: e.matmul(
                        ps[bkr][0:8, 0:256], wrp[:, pw, k, :], h2p(g % 2, ph)[:, k, :],
                        start=(k == 0 and pw == 0 and ph == 0), stop=(k == 7 and pw == 1)))
                    e_lg = S.op("tensor", fn, inc=spe) if (k == 7 and pw == 1) else S.op("tensor", fn)
            S.wait("vector", e_lg)
            e_lgv = vop(lambda e, bkr=bkr: e.tensor_copy(lgfm[:], ps[bkr][0:8, 0:256]))
            BK.put(bkr, e_lgv)
            bkr2, fer2 = BK.get()
            S.wait("tensor", fer2)
            S.wait("tensor", e_lgv)
            for ct in range(2):
                fn = (lambda e, bkr2=bkr2, ct=ct: e.transpose(
                    ps[bkr2][:, ct * 8:(ct + 1) * 8], lgfm[:, ct * 128:(ct + 1) * 128],
                    eyev[0:8, 0:8]))
                e_lt = S.op("tensor", fn, inc=spe) if ct == 1 else S.op("tensor", fn)
            S.wait("vector", e_lt)
            e_ltv = vop(lambda e, bkr2=bkr2: e.tensor_copy(lgtm[:].rearrange("p a b -> p (a b)"),
                                                           ps[bkr2][:, 0:16]))
            BK.put(bkr2, e_ltv)
            S.wait("vector", e_ltv)
            if DEBUG_TAPS:
                S.wait("vector", dbg_backedge.get("prfm"))
                S.wait("vector", dbg_backedge.get("m1c"))
            for ct in range(2):
                vc(lambda e, ct=ct: e.max(out=rtr[:, ct, 0:8], in_=lgtm[:, ct, :]))
            for ct in range(2):
                vc(lambda e, ct=ct: e.tensor_copy(m1c[:, ct:ct + 1], rtr[:, ct, 0:1]))
                vc(lambda e, ct=ct: e.tensor_copy(m2c[:, ct:ct + 1], rtr[:, ct, 1:2]))
            e_nm1 = vc(lambda e: e.tensor_scalar_mul(nm1c[:], m1c[:], -1.0))
            S.wait("scalar", e_nm1)
            for ct in range(2):
                e_exp = vc(lambda e, ct=ct: e.activation(
                    out=dvc[:, ct:ct + 1], in_=m2c[:, ct:ct + 1], func=AF.Exp,
                    bias=nm1c[:, ct:ct + 1], scale=1.0), "scalar")
            S.wait("vector", e_exp)
            vc(lambda e: e.tensor_scalar_add(dvc[:], dvc[:], 1.0))
            vc(lambda e: e.reciprocal(dvc[:], dvc[:]))
            vc(lambda e: e.tensor_single_scalar(g1c[:], dvc[:], 1.0, ALU.is_ge))
            for ct in range(2):
                vc(lambda e, ct=ct: e.tensor_single_scalar(eqc[:, ct, :], lgtm[:, ct, :],
                                                           m1c[:, ct:ct + 1], ALU.is_equal))
                e_prt = vc(lambda e, ct=ct: e.tensor_scalar(out=prtm[:, ct, :], in0=eqc[:, ct, :],
                                                            scalar1=g1c[:, ct:ct + 1],
                                                            scalar2=None, op0=ALU.mult))
            bkr3, fer3 = BK.get()
            S.wait("tensor", fer3)
            S.wait("tensor", e_prt)
            for ct in range(2):
                fn = (lambda e, bkr3=bkr3, ct=ct: e.transpose(
                    ps[bkr3][0:8, ct * 128:(ct + 1) * 128], prtm[:, ct, :], eyev[:]))
                e_pf = S.op("tensor", fn, inc=spe) if ct == 1 else S.op("tensor", fn)
            S.wait("vector", e_pf)
            e_pfm = vop(lambda e, bkr3=bkr3: e.tensor_copy(prfm[:], ps[bkr3][0:8, 0:256]))
            BK.put(bkr3, e_pfm)
            if DEBUG_TAPS:
                e_tap2 = dma("gpsimd", dG, dbg2_t[g], prfm[:], waits=(e_pfm,))
                dma("gpsimd", dG, dbg3_t[g][:, 0:2], m1c[:], waits=(e_pfm,), slow=True)
                e_tap3 = dma("gpsimd", dG, dbg3_t[g][:, 2:4], m2c[:], waits=(e_pfm,), slow=True)
                dbg_backedge["prfm"] = e_tap2
                dbg_backedge["m1c"] = e_tap3

            if STOP_AT == f"ffn_r{g}":
                return finish()
            ye_banks = []
            for bg in range(4):
                bky, fey = BK.get()
                S.wait("tensor", fey)
                ye_banks.append(bky)
            ye_last = {}
            e_hpe_last = None
            e_ye_prev = None
            prev_hem1 = None
            for ei in range(9):
                buf = (g * 9 + ei) % 2
                nxt = (g, ei + 1) if ei < 8 else (g + 1, 0)
                if nxt[0] <= 7 and nxt not in wex_edge and not (NOPREF and nxt[0] > 0):
                    load_wex(nxt[0], nxt[1], (g * 9 + ei + 1) % 2,
                             waits=() if wex_pe[(buf + 1) % 2] is None else (wex_pe[(buf + 1) % 2],))
                if ei < 8:
                    bkm2, fem2 = BK.get()
                    S.wait("tensor", fem2)
                    S.wait("tensor", e_pfm)
                    S.wait("tensor", e_sel)
                    e_mk = S.op("tensor", lambda e, bkm2=bkm2, ei=ei: e.matmul(
                        ps[bkm2][:, 0:256], selv[:, ei * 128:(ei + 1) * 128], prfm[:],
                        start=True, stop=True), inc=spe)
                    S.wait("scalar", e_mk)
                    S.wait("scalar", mask_gp[ei % 2])
                    e_mke = vop(lambda e, bkm2=bkm2, ei=ei: e.copy(maskb[:, ei % 2, :],
                                                                   ps[bkm2][:, 0:256]), "scalar")
                    BK.put(bkm2, e_mke)
                S.wait("tensor", wex_edge[(g, ei)])
                e_hm = None
                for fg in range(2):
                    bkh, feh = BK.get()
                    S.wait("tensor", feh)
                    for half in range(2):
                        f = fg * 2 + half
                        for k in range(8):
                            for pw, ph in ((0, 0), (0, 1), (1, 0)):
                                fn = (lambda e, bkh=bkh, f=f, half=half, k=k, buf=buf, g=g,
                                      pw=pw, ph=ph: e.matmul(
                                    ps[bkh][:, half * 256:half * 256 + 256],
                                    w1p(buf, pw)[:, k, f * 128:(f + 1) * 128],
                                    h2p(g % 2, ph)[:, k, :],
                                    start=(k == 0 and half == 0 and pw == 0 and ph == 0),
                                    stop=(k == 7 and half == 1 and pw == 1)))
                                if k == 7 and half == 1 and pw == 1:
                                    e_he = S.op("tensor", fn, inc=spe)
                                else:
                                    S.op("tensor", fn)
                    e_hpe_last = e_he
                    S.wait("scalar", e_he)
                    if fg == 0:
                        S.wait("scalar", prev_hem1)
                        S.wait("vector", e_ye_prev)
                    e_t1 = {}
                    for half in range(2):
                        f = fg * 2 + half
                        e_t1[half] = vop(lambda e, bkh=bkh, half=half, f=f: e.activation(
                            out=hem(1)[:, f, :], in_=ps[bkh][:, half * 256:half * 256 + 256],
                            func=AF.Relu), "scalar")
                    for half in range(2):
                        f = fg * 2 + half
                        if half == 0:
                            S.wait("vector", e_t1[1])
                            if ei < 8:
                                S.wait("vector", e_mke)
                        if ei < 8:
                            e_msk = vc(lambda e, f=f, ei=ei: e.tensor_tensor(
                                hem(1)[:, f, :], hem(1)[:, f, :], maskb[:, ei % 2, :], ALU.mult))
                            if half == 1:
                                mask_gp[ei % 2] = e_msk
                        e_hm = vc(lambda e, bkh=bkh, half=half, f=f: e.tensor_tensor(
                            hem(0)[:, f, :], hem(1)[:, f, :],
                            ps[bkh][:, half * 256:half * 256 + 256], ALU.mult))
                        vc(lambda e, f=f: e.tensor_copy(hep(0)[:, f, :], hem(0)[:, f, :]))
                        e_hm = vc(lambda e, f=f: e.tensor_tensor(hep(1)[:, f, :], hem(0)[:, f, :],
                                                                 hep(0)[:, f, :], ALU.subtract))
                    BK.put(bkh, e_hm)
                if STOP_AT == f"ffn_he{ei}" and g == 0:
                    return finish()
                S.wait("tensor", e_hm)
                for bg in range(4):
                    for half in range(2):
                        hh = bg * 2 + half
                        for f in range(4):
                            for pw, ph in ((0, 0), (0, 1), (1, 0)):
                                last_pass = (pw == 1)
                                last_all = (half == 1 and f == 3 and bg == 3 and last_pass)
                                fn = (lambda e, bank=ye_banks[bg], half=half, hh=hh, f=f, buf=buf,
                                      ei=ei, pw=pw, ph=ph: e.matmul(
                                    ps[bank][:, half * 256:half * 256 + 256],
                                    w2p(buf, pw)[:, f, hh * 128:(hh + 1) * 128], hep(ph)[:, f, :],
                                    start=(ei == 0 and half == 0 and f == 0 and pw == 0 and ph == 0),
                                    stop=(ei == 8 and half == 1 and f == 3 and pw == 1)))
                                if ei == 8 and f == 3 and half == 1 and last_pass:
                                    ye_last[bg] = S.op("tensor", fn, inc=spe)
                                elif last_all:
                                    wex_pe[buf] = S.op("tensor", fn, inc=spe)
                                else:
                                    S.op("tensor", fn)
                if ei == 8:
                    wex_pe[buf] = ye_last[3]
                e_ye_prev = wex_pe[buf] if ei < 8 else ye_last[3]
                prev_hem1 = e_hm
                if STOP_AT == f"ffn_e{ei}" and g == 0:
                    return finish()
                if STOP_AT is not None and STOP_AT.startswith(f"ffn_e{ei}mim") and g == 0:
                    S.wait("tensor", wex_edge[(0, 8)])
                    for fg in range(2):
                        bkh, feh = BK.get()
                        S.wait("tensor", feh)
                        for half in range(2):
                            f = fg * 2 + half
                            for k in range(8):
                                fn = (lambda e, bkh=bkh, f=f, half=half, k=k, g=g: e.matmul(
                                    ps[bkh][:, half * 256:half * 256 + 256],
                                    w1v(0)[:, k, f * 128:(f + 1) * 128], hcv(g % 2)[:, k, :],
                                    start=(k == 0 and half == 0), stop=(k == 7 and half == 1)))
                                if k == 7 and half == 1:
                                    e_heM = S.op("tensor", fn, inc=spe)
                                else:
                                    S.op("tensor", fn)
                        if STOP_AT == f"ffn_e{ei}mimA":
                            S.wait("vector", e_heM)
                            ed = vop(lambda e, bkh=bkh: e.tensor_copy(sq[:, 0, :], ps[bkh][:, 0:256]))
                            BK.put(bkh, ed)
                            continue
                        S.wait("scalar", e_heM)
                        if fg == 0:
                            S.wait("scalar", prev_hem1)
                            S.wait("vector", e_ye_prev)
                        for half in range(2):
                            f = fg * 2 + half
                            e_t1M = vop(lambda e, bkh=bkh, half=half, f=f: e.activation(
                                out=hem(1)[:, f, :], in_=ps[bkh][:, half * 256:half * 256 + 256],
                                func=AF.Relu), "scalar")
                            if STOP_AT == f"ffn_e{ei}mimB":
                                S.wait("vector", e_t1M)
                                e_hmM = vop(lambda e, bkh=bkh, half=half, f=f: e.tensor_copy(
                                    hem(0)[:, f, :], hem(1)[:, f, :]))
                            else:
                                S.wait("vector", e_t1M)
                                e_hmM = vop(lambda e, bkh=bkh, half=half, f=f: e.tensor_tensor(
                                    hem(0)[:, f, :], hem(1)[:, f, :],
                                    ps[bkh][:, half * 256:half * 256 + 256], ALU.mult))
                        BK.put(bkh, e_hmM)
                    return finish()
                if STOP_AT == f"ffn_e{ei}pad" and g == 0:
                    for fg in range(2):
                        bkh, feh = BK.get()
                        S.wait("tensor", feh)
                        for half in range(2):
                            f = fg * 2 + half
                            for k in range(8):
                                fn = (lambda e, bkh=bkh, f=f, half=half, k=k, g=g: e.matmul(
                                    ps[bkh][:, half * 256:half * 256 + 256],
                                    w1v(0)[:, k, f * 128:(f + 1) * 128], hcv(g % 2)[:, k, :],
                                    start=(k == 0 and half == 0), stop=(k == 7 and half == 1)))
                                if k == 7 and half == 1:
                                    e_dummy = S.op("tensor", fn, inc=spe)
                                else:
                                    S.op("tensor", fn)
                        S.wait("vector", e_dummy)
                        ed = vop(lambda e, bkh=bkh: e.tensor_copy(sq[:, 0, :], ps[bkh][:, 0:256]))
                        BK.put(bkh, ed)
                    return finish()
            h2_free[g] = e_hpe_last
            oe = []
            for bg in range(4):
                S.wait("vector", ye_last[bg])
                for half in range(2):
                    hh = bg * 2 + half
                    ed = vop(lambda e, bank=ye_banks[bg], half=half, hh=hh, g=g: e.scalar_tensor_tensor(
                        out=x2sl[:, g % 2, hh, :],
                        in0=ps[bank][:, half * 256:half * 256 + 256],
                        scalar=dmod[:, 40 + hh:40 + hh + 1],
                        in1=x2sl[:, g % 2, hh, :], op0=ALU.mult, op1=ALU.add))
                    oe.append(ed)
                BK.put(ye_banks[bg], oe[-1])
            e_od = dma("gpsimd", dG,
                       out_t.rearrange("(kc p) t -> p kc t", p=128)[:, :, g * 256:(g + 1) * 256],
                       x2sl[:, g % 2, :, :], waits=(oe[-1],))
            out_dma.append(e_od)
            if g + 2 <= 7:
                load_x2(g + 2, (e_od, P[8 + g + 2]["spill"]))
            if STOP_AT == f"ffn_g{g}":
                return finish()

        S.wait("gpsimd", out_dma[-1])
        S.wait("sync", out_dma[-1])

        S.check()
        blk = st.enter_context(nc.Block())
        for ename in ENGINES:
            section = getattr(blk, ename)
            def emit(eng, _q=tuple(S.q[ename])):
                for item in _q:
                    item(eng)
            section(emit)

    return nc


def kernel(L0_attn_x, L0_attn_Wb, L0_attn_init_A, L0_attn_init_B, L0_attn_ln_gamma,
           L0_attn_ln_beta, L0_attn_Wp, L0_ffn_Ws1, L0_ffn_Ws2, L0_ffn_W1, L0_ffn_W2,
           L0_ffn_Wr, L0_c, L0_wmod):
    x = np.asarray(L0_attn_x, dtype=np.float32)
    B, S_, D = x.shape
    fmT = lambda a: np.ascontiguousarray(np.asarray(a, np.float32).T)

    common = dict(
        wbT=fmT(L0_attn_Wb), wpT=fmT(L0_attn_Wp),
        initA=np.ascontiguousarray(np.asarray(L0_attn_init_A, np.float32)),
        initAT=fmT(L0_attn_init_A),
        initB=np.ascontiguousarray(np.asarray(L0_attn_init_B, np.float32)),
        initBT=fmT(L0_attn_init_B),
        gam=np.ascontiguousarray(np.asarray(L0_attn_ln_gamma, np.float32).reshape(8, 128).T),
        bet=np.ascontiguousarray(np.asarray(L0_attn_ln_beta, np.float32).reshape(8, 128).T),
        wr=np.ascontiguousarray(np.asarray(L0_ffn_Wr, np.float32)),
        wmod=np.ascontiguousarray(np.asarray(L0_wmod, np.float32)),
        wex1=np.ascontiguousarray(np.concatenate(
            [np.asarray(L0_ffn_W1, np.float32), np.asarray(L0_ffn_Ws1, np.float32)[None]], 0)),
        wex2=np.ascontiguousarray(np.concatenate(
            [np.asarray(L0_ffn_W2, np.float32), np.asarray(L0_ffn_Ws2, np.float32)[None]], 0)),
        ones=np.ones((128, 128), dtype=np.float32),
        eye=np.eye(128, dtype=np.float32),
    )
    import ml_dtypes
    sel = np.zeros((8, 1024), dtype=ml_dtypes.bfloat16)
    for e in range(8):
        sel[e, e * 128:(e + 1) * 128] = 1.0
    common["sel"] = sel
    common["onesb"] = np.ones((128, 128), dtype=ml_dtypes.bfloat16)
    cN = np.asarray(L0_c, np.float32)

    in_maps = []
    lr = 0.01 * 2.0
    for core in range(8):
        b, hf = core // 2, core % 2
        xo = fmT(x[b, hf * 2048:(hf + 1) * 2048])
        if hf == 0:
            xf = np.zeros((1024, 2048), dtype=np.float32)
            xxt = np.ascontiguousarray(np.stack([x[b, 2048], x[b, 0]], 1))
        else:
            xf = fmT(x[b, 0:2048])
            xxt = np.ascontiguousarray(np.stack([np.zeros(1024, np.float32), x[b, 2048]], 1))
        consts = np.zeros((128, 16), dtype=np.float32)
        for col, v in {C_ZERO: 0.0, C_LNEPS: 1e-5, C_CLIPEPS: 1e-6, C_LR2: lr * lr,
                       C_N2LR: -2.0 * lr, C_NLR: -lr, C_INV1024: 1.0 / 1024.0,
                       C_QUART: 0.25, C_ONE: 1.0, C_FHMASK: float(hf == 1),
                       C_XTRAMASK: float(hf == 0), C_TWO: 2.0}.items():
            consts[:, col] = v
        in_maps.append(dict(common, xo=xo, xxt=xxt, xf=xf, consts=consts,
                            cfm=np.ascontiguousarray(cN[b].reshape(8, 128).T)))

    nc = build_program()
    res = run_bass_kernel_spmd(nc, in_maps, core_ids=list(range(8)))
    out = np.empty((B, S_, D), dtype=np.float32)
    for core in range(8):
        b, hf = core // 2, core % 2
        out[b, hf * 2048:(hf + 1) * 2048] = res.results[core]["out"].T
    return out


# version 12
# speedup vs baseline: 1.0310x; 1.0276x over previous
import numpy as np
from contextlib import ExitStack

import concourse.bass as bass
import concourse.mybir as mybir
from concourse.bass_utils import run_bass_kernel_spmd

F32 = mybir.dt.float32
BF16 = mybir.dt.bfloat16
AF = mybir.ActivationFunctionType
ALU = mybir.AluOpType
AX = mybir.AxisListType

ENGINES = ("sync", "scalar", "vector", "tensor", "gpsimd")
DEBUG_TAPS = False
STOP_AT = None
NOPREF = False

C_ZERO, C_LNEPS, C_CLIPEPS, C_LR2, C_N2LR, C_NLR, C_INV1024, C_QUART = range(8)
C_ONE, C_FHMASK, C_XTRAMASK, C_TWO = 8, 9, 10, 11


class Sem:
    def __init__(self, handle, name):
        self.h, self.name, self.n = handle, name, 0


class Sched:
    def __init__(self):
        self.q = {e: [] for e in ENGINES}
        self.trace = {e: [] for e in ENGINES}

    def op(self, eng, fn, inc=None, dv=1):
        if inc is None:
            self.q[eng].append(fn)
            return None
        def run(e, fn=fn, inc=inc, dv=dv):
            fn(e).then_inc(inc.h, dv)
        self.q[eng].append(run)
        inc.n += dv
        self.trace[eng].append(("i", inc, dv))
        return (inc, inc.n)

    def wait(self, eng, edge):
        if edge is None:
            return
        sem, v = edge
        self.q[eng].append(lambda e, sem=sem, v=v: e.wait_ge(sem.h, v))
        self.trace[eng].append(("w", sem, v))

    def check(self):
        pos = {e: 0 for e in ENGINES}
        vals = {}
        progressed = True
        while progressed:
            progressed = False
            for e in ENGINES:
                while pos[e] < len(self.trace[e]):
                    kind, sem, v = self.trace[e][pos[e]]
                    if kind == "i":
                        vals[sem.name] = vals.get(sem.name, 0) + v
                        pos[e] += 1
                        progressed = True
                    elif vals.get(sem.name, 0) >= v:
                        pos[e] += 1
                        progressed = True
                    else:
                        break
        stuck = {e: self.trace[e][pos[e]] for e in ENGINES if pos[e] < len(self.trace[e])}
        if stuck:
            raise RuntimeError("deadlock: " + str({e: (k, s.name, v) for e, (k, s, v) in stuck.items()}))


class Banks:
    def __init__(self, n=8):
        self.pool = [[i, None] for i in range(n)]

    def get(self):
        i, e = self.pool.pop(0)
        return i, e

    def put(self, i, e):
        assert i not in [x[0] for x in self.pool], f"double put of bank {i}"
        self.pool.append([i, e])


def build_program():
    nc = bass.Bass("TRN2", target_bir_lowering=False, detect_race_conditions=False)

    dt = {}
    def din(name, shape):
        dt[name] = nc.dram_tensor(name, list(shape), F32, kind="ExternalInput").ap()
    din("xo", (1024, 2048)); din("xxt", (1024, 2)); din("xf", (1024, 2048))
    din("wbT", (1024, 1024)); din("wpT", (1024, 1024)); din("wmod", (1024, 6144))
    din("cfm", (128, 8)); din("gam", (128, 8)); din("bet", (128, 8))
    din("initBT", (1024, 16)); din("initA", (1024, 16)); din("initAT", (16, 1024))
    din("initB", (16, 1024)); din("wr", (1024, 8))
    dt["sel"] = nc.dram_tensor("sel", [8, 1024], BF16, kind="ExternalInput").ap()
    din("ones", (128, 128)); din("eye", (128, 128)); din("consts", (128, 16))
    din("wex1", (9, 1024, 512)); din("wex2", (9, 512, 1024))
    dt["onesb"] = nc.dram_tensor("onesb", [128, 128], BF16, kind="ExternalInput").ap()
    out_t = nc.dram_tensor("out", [1024, 2048], F32, kind="ExternalOutput").ap()
    dbg_t = nc.dram_tensor("dbg", [1024, 2048], F32, kind="ExternalOutput").ap() if DEBUG_TAPS else None
    dbg2_t = nc.dram_tensor("dbg2", [8, 8, 256], F32, kind="ExternalOutput").ap() if DEBUG_TAPS else None
    dbg3_t = nc.dram_tensor("dbg3", [8, 128, 4], F32, kind="ExternalOutput").ap() if DEBUG_TAPS else None
    dbg4_t = nc.dram_tensor("dbg4", [128, 96], F32, kind="ExternalOutput").ap() if DEBUG_TAPS else None
    x2sp = nc.dram_tensor("x2sp", [1024, 2048], F32,
                          kind="ExternalOutput" if DEBUG_TAPS else "Internal").ap()
    w1s = nc.dram_tensor("w1s", [9, 2, 1024, 512], BF16).ap()
    w2s = nc.dram_tensor("w2s", [9, 2, 512, 1024], BF16).ap()
    wbs = nc.dram_tensor("wbs", [2, 2, 1024, 512], BF16).ap()

    fm = lambda n: dt[n].rearrange("(kc p) t -> p kc t", p=128)

    st = ExitStack()
    with st:
        def sb(name, shape):
            return st.enter_context(nc.sbuf_tensor(name, shape, F32))

        wslab = sb("wslab", [128, 16384])
        xsl = sb("xsl", [128, 2, 8, 256])
        xcol = sb("xcol", [128, 8])
        xnp_w = sb("xnp", [128, 2, 2048])
        hp_w = sb("hp", [128, 2, 2048])
        btp_w = sb("btp", [128, 128])
        ibtp_w = sb("ibtp", [128, 128])
        hsl = sb("hsl", [128, 2048])
        vsl = sb("vsl", [128, 2048])
        vctm = sb("vctm", [128, 2, 1024])
        ztm = sb("ztm", [128, 2, 1024])
        x2sl = sb("x2sl", [128, 2, 8, 256])
        sq = sb("sq", [128, 2, 256])
        stats = sb("stats", [128, 4, 256])
        modv = sb("modv", [128, 48])
        dmod = modv
        sfm = sb("sfm", [128, 8]); cfmv = sb("cfmv", [128, 8])
        gamv = sb("gamv", [128, 8]); betv = sb("betv", [128, 8])
        conv = sb("conv", [128, 16])
        onesv = sb("onesv", [128, 128]); eyev = sb("eyev", [128, 128])
        initAv = sb("initAv", [128, 8, 16])
        selv = st.enter_context(nc.sbuf_tensor("selv", [8, 1024], BF16))
        accA = st.enter_context(nc.sbuf_tensor("accA", [16, 1024], F32))
        accB = st.enter_context(nc.sbuf_tensor("accB", [16, 1024], F32))
        iA2T = st.enter_context(nc.sbuf_tensor("iA2T", [16, 1024], F32))
        iBv = st.enter_context(nc.sbuf_tensor("iBv", [16, 1024], F32))
        a2ef = st.enter_context(nc.sbuf_tensor("a2ef", [16, 1024], F32))
        beff = st.enter_context(nc.sbuf_tensor("beff", [16, 1024], F32))
        prji = st.enter_context(nc.sbuf_tensor("prji", [16, 256], F32))
        prje = st.enter_context(nc.sbuf_tensor("prje", [16, 256], F32))
        ptmi = sb("ptmi", [128, 2, 16])
        ptmep_w = sb("ptmep", [128, 32])
        ptmef = sb("ptmef", [128, 32])
        btf_w = sb("btf", [128, 128])
        ptmep = lambda hl: ptmep_w[:, hl * 16:(hl + 1) * 16].bitcast(BF16)
        eyeb = st.enter_context(nc.sbuf_tensor("eyeb", [128, 128], BF16))
        mids = st.enter_context(nc.sbuf_tensor("mids", [16, 256], F32))
        c16 = st.enter_context(nc.sbuf_tensor("c16", [16, 8], F32))
        lgfm = st.enter_context(nc.sbuf_tensor("lgfm", [8, 256], F32))
        prfm = st.enter_context(nc.sbuf_tensor("prfm", [8, 256], BF16))
        lgtm = sb("lgtm", [128, 2, 8])
        rtr = sb("rtr", [128, 2, 10])
        m1c = sb("m1c", [128, 2]); m2c = sb("m2c", [128, 2]); nm1c = sb("nm1c", [128, 2])
        dvc = sb("dvc", [128, 2]); g1c = sb("g1c", [128, 2])
        eqc = sb("eqc", [128, 2, 8]); prtm = sb("prtm", [128, 2, 8])
        maskv = sb("maskv", [128, 2, 256])
        maskb = st.enter_context(nc.sbuf_tensor("maskb", [128, 2, 256], BF16))
        onebv = st.enter_context(nc.sbuf_tensor("onebv", [128, 128], BF16))
        wrp = st.enter_context(nc.sbuf_tensor("wrp", [128, 2, 8, 8], BF16))
        ps = [st.enter_context(nc.psum_tensor(f"ps{i}", [128, 512], F32)) for i in range(8)]

        sems = {n: Sem(st.enter_context(nc.semaphore(n)), n)
                for n in ("dW", "dX", "dG", "dP", "dS", "dT", "dE", "dF", "pe", "dv", "ac", "gp")}
        dW, dX, dG = sems["dW"], sems["dX"], sems["dG"]
        dP, dS, dT, dE, dF = sems["dP"], sems["dS"], sems["dT"], sems["dE"], sems["dF"]
        spe, sdv, sac, sgp = sems["pe"], sems["dv"], sems["ac"], sems["gp"]

        S = Sched()
        BK = Banks()

        def finish():
            S.check()
            blk = st.enter_context(nc.Block())
            for ename in ENGINES:
                section = getattr(blk, ename)
                def emit(eng, _q=tuple(S.q[ename])):
                    for item in _q:
                        item(eng)
                section(emit)
            return nc

        def dma(qeng, sem, out_ap, in_ap, waits=(), slow=False):
            for ed in waits:
                S.wait(qeng, ed)
            def fn(e, o=out_ap, i=in_ap, slow=slow):
                return e.dma_start(out=o, in_=i, allow_slow_non_contiguous=True) if slow \
                    else e.dma_start(out=o, in_=i)
            return S.op(qeng, fn, inc=sem, dv=16)

        def vop(fn, eng="vector", inc=None):
            return S.op(eng, fn, inc=(inc or {"vector": sdv, "scalar": sac,
                                             "gpsimd": sgp}[eng]))

        def vc(fn, eng="vector"):
            ed = vop(fn, eng)
            S.wait(eng, ed)
            return ed

        wpf = wslab[:, 0:8192]
        wppf = lambda hl: wslab[:, 8192 + hl * 4096:8192 + (hl + 1) * 4096].bitcast(BF16)
        wpp = lambda hl: wslab[:, 8192 + hl * 4096:8192 + (hl + 1) * 4096]             .bitcast(BF16).rearrange("p (k f) -> p k f", k=8)
        wbp = lambda hl: wslab[:, hl * 4096:(hl + 1) * 4096]             .bitcast(BF16).rearrange("p (k f) -> p k f", k=8)
        xnp = lambda sl, hl: xnp_w[:, sl, hl * 1024:(hl + 1) * 1024]             .bitcast(BF16).rearrange("p (k t) -> p k t", k=8)
        hpb = lambda sl, hl: hp_w[:, sl, hl * 1024:(hl + 1) * 1024].bitcast(BF16)
        hpv = lambda sl, hl: hp_w[:, sl, hl * 1024:(hl + 1) * 1024]             .bitcast(BF16).rearrange("p (k t) -> p k t", k=8)
        btpf = lambda hl: btp_w[:, hl * 64:(hl + 1) * 64].bitcast(BF16)
        btp = lambda hl: btp_w[:, hl * 64:(hl + 1) * 64]             .bitcast(BF16).rearrange("p (k r) -> p k r", k=8)
        ibtpf = lambda hl: ibtp_w[:, hl * 64:(hl + 1) * 64].bitcast(BF16)
        ibtp = lambda hl: ibtp_w[:, hl * 64:(hl + 1) * 64]             .bitcast(BF16).rearrange("p (k r) -> p k r", k=8)
        w1v = lambda b: wslab[:, b * 8192:b * 8192 + 4096].rearrange("p (k f) -> p k f", k=8)
        w2v = lambda b: wslab[:, b * 8192 + 4096:(b + 1) * 8192].rearrange("p (f h) -> p f h", f=4)
        w1p = lambda b, pl: wslab[:, b * 8192 + pl * 2048: b * 8192 + (pl + 1) * 2048]             .bitcast(BF16).rearrange("p (k f) -> p k f", k=8)
        w2p = lambda b, pl: wslab[:, b * 8192 + 4096 + pl * 2048: b * 8192 + 4096 + (pl + 1) * 2048]             .bitcast(BF16).rearrange("p (f h) -> p f h", f=4)
        h2p = lambda sl, pl: (vctm if pl == 0 else ztm)[:, sl, :]             .bitcast(BF16).rearrange("p (k t) -> p k t", k=8)
        hep = lambda pl: vsl[:, 1024:2048].bitcast(BF16)             .rearrange("p (f two t) -> p f two t", f=4, two=2)[:, :, pl, :]
        wstg = x2sl.ap().rearrange("p a b c -> p (a b c)")
        ppair = lambda half, pl: wslab[:, half * 4096 + pl * 2048:(half * 4096) + (pl + 1) * 2048]             .bitcast(BF16)
        wmv = lambda b: (vctm if b == 0 else ztm).ap().rearrange(
            "p a b -> p (a b)")[:, 0:2048].rearrange("p (k t) -> p k t", k=8)
        hcv = lambda sl: hsl.ap().rearrange("p (k t) -> p k t", k=8)
        vv = vsl.ap().rearrange("p (k t) -> p k t", k=8)
        xc3 = xcol.ap().rearrange("p (a b) -> p a b", b=1)
        ztm_p = lambda ct, hl: ztm[:, ct, hl * 512:(hl + 1) * 512].bitcast(BF16)
        s16 = vsl[:, 0:128].rearrange("p (k r) -> p k r", k=8)
        wrv = vsl[:, 144:208].rearrange("p (a b) -> p a b", a=8)
        initBTv = x2sl.ap().rearrange("p a b c -> p (a b c)")[:, 0:128]             .rearrange("p (k r) -> p k r", k=8)
        hem = lambda half: vsl[:, half * 1024:half * 1024 + 1024].rearrange("p (f t) -> p f t", f=4)
        mflat = maskv.ap().rearrange("p a b -> p (a b)")
        hx3 = mflat[:, 0:16].rearrange("p (k t) -> p k t", k=8)
        xnx3 = mflat[:, 16:32].rearrange("p (k t) -> p k t", k=8)
        xx3 = mflat[:, 32:48].rearrange("p (k t) -> p k t", k=8)

        e_sm = {}
        for name, dst in (("consts", conv), ("ones", onesv), ("eye", eyev),
                          ("cfm", cfmv), ("gam", gamv), ("bet", betv)):
            e_sm[name] = dma("sync", dW, dst[:], dt[name])
        e_ibt = dma("sync", dW, initBTv[:], dt["initBT"].rearrange("(kc p) r -> p kc r", p=128))
        e_ia = dma("sync", dW, initAv[:], dt["initA"].rearrange("(kc p) r -> p kc r", p=128))
        e_iat = dma("sync", dW, iA2T[:], dt["initAT"])
        e_ib = dma("sync", dW, iBv[:], dt["initB"])
        e_wr = dma("sync", dW, wrv[:], dt["wr"].rearrange("(kc p) e -> p kc e", p=128))
        e_onb = dma("sync", dW, onebv[:], dt["onesb"])
        e_sel = dma("sync", dW, selv[:], dt["sel"])
        e_wm = {}
        def load_wm(jg, waits=()):
            e_wm[jg] = dma("sync", dW, wmv(jg % 2)[:],
                           fm("wmod")[:, :, jg * 256:(jg + 1) * 256], waits=waits)
        load_wm(0)
        load_wm(1)

        x_edge = {}
        def load_x(idx, waits=()):
            src = fm("xf") if idx < 8 else fm("xo")
            c = idx % 8
            x_edge[idx] = dma("scalar", dX, xsl[:, idx % 2, :, :],
                              src[:, :, c * 256:(c + 1) * 256], waits=waits)
        load_x(0)
        load_x(1)
        e_wpt0 = dma("scalar", dP, wpf.rearrange("p (k h) -> p k h", k=8), fm("wpT"))
        e_xxt = dma("scalar", dT, xx3[:],
                    dt["xxt"].rearrange("(kc p) t -> p kc t", p=128), slow=True)

        S.wait("scalar", e_sm["cfm"])
        e_silu = vop(lambda e: e.activation(out=sfm[:], in_=cfmv[:], func=AF.Silu), "scalar")
        S.wait("vector", e_silu)
        for k in range(8):
            e_s16 = vc(lambda e, k=k: e.tensor_copy(s16[:, k, :],
                                                    bass.AP(sfm, k, [[8, 128], [0, 16]])))

        S.wait("vector", e_sm["eye"])
        e_eyeb = vc(lambda e: e.tensor_copy(eyeb[:], eyev[:]))
        S.wait("vector", e_wpt0)
        vc(lambda e: e.tensor_copy(wppf(0), wpf))
        e_wpt = vc(lambda e: e.tensor_tensor(wppf(1), wpf, wppf(0), ALU.subtract))
        S.wait("vector", e_ibt)
        ibtf = x2sl.ap().rearrange("p a b c -> p (a b c)")[:, 0:128]
        vc(lambda e: e.tensor_copy(ibtpf(0), ibtf))
        e_ibp = vc(lambda e: e.tensor_tensor(ibtpf(1), ibtf, ibtpf(0), ALU.subtract))

        ps_state = {"w1_store": None, "w2_store": None, "split_done": None,
                    "last_store": None, "last_load": None}
        PS_ITEMS = [((dt["wex1"][ei], w1s[ei], 0), (dt["wex2"][ei], w2s[ei], 1))
                    for ei in range(9)]
        PS_ITEMS.append(((dt["wbT"][:, 0:512], wbs[0], 0),
                         (dt["wbT"][:, 512:1024], wbs[1], 0)))
        PAIR_SRC = (wslab[:, 0:4096].bitcast(BF16).rearrange("p (pl k f) -> p pl k f", pl=2, k=8),
                    wslab[:, 4096:8192].bitcast(BF16).rearrange("p (pl f h) -> p pl f h", pl=2, f=4))

        def presplit_half(srcd, dstd, mode, qeng):
            stg3 = x2sl.ap().rearrange("p a b c -> p (a b c)").rearrange(
                "p (k f) -> p k f", k=(8 if mode == 0 else 4))
            e_ld = dma(qeng, dS, stg3,
                       srcd.rearrange("(kc p) f -> p kc f", p=128),
                       waits=[w for w in (ps_state["split_done"],) if w])
            ps_state["last_load"] = e_ld
            S.wait("scalar", e_ld)
            S.wait("scalar", ps_state["w1_store" if mode == 0 else "w2_store"])
            e_hi = vop(lambda e, mode=mode: e.activation(out=ppair(mode, 0), in_=wstg,
                                                         func=AF.Copy), "scalar")
            S.wait("vector", e_ld)
            S.wait("vector", e_hi)
            e_sp = vc(lambda e, mode=mode: e.tensor_tensor(ppair(mode, 1), wstg,
                                                           ppair(mode, 0), ALU.subtract))
            ps_state["split_done"] = e_sp
            e_st = dma("gpsimd", dE if mode == 0 else dF,
                       dstd.rearrange("pl (kc p) f -> p pl kc f", p=128),
                       PAIR_SRC[mode], waits=(e_sp,))
            ps_state["w1_store" if mode == 0 else "w2_store"] = e_st
            ps_state["last_store"] = e_st

        def presplit_item(item, qeng):
            for srcd, dstd, mode in item:
                presplit_half(srcd, dstd, mode, qeng)

        ps_state["split_done"] = e_ibp
        ps_state["w1_store"] = e_wpt
        ps_state["w2_store"] = e_wpt
        P0H = [half for ei in (5, 6, 7, 8) for half in PS_ITEMS[ei]]

        S.wait("tensor", e_s16)
        S.wait("tensor", e_sm["eye"])
        S.wait("tensor", e_sm["ones"])
        wm_pe = {}
        for jg in range(24):
            S.wait("tensor", e_wm[jg])
            bk, fe = BK.get()
            S.wait("tensor", fe)
            for k in range(8):
                fn = (lambda e, k=k, jg=jg, bk=bk: e.matmul(
                    ps[bk][0:16, 0:256], s16[:, k, :], wmv(jg % 2)[:, k, :],
                    start=(k == 0), stop=(k == 7)))
                e_mm = S.op("tensor", fn, inc=spe) if k == 7 else S.op("tensor", fn)
            wm_pe[jg] = e_mm
            if jg + 2 < 24:
                load_wm(jg + 2, waits=(e_mm,))
            S.wait("vector", e_mm)
            e_ev = vop(lambda e, bk=bk: e.tensor_copy(prji[:], ps[bk][0:16, 0:256]))
            BK.put(bk, e_ev)
            bk2, fe2 = BK.get()
            S.wait("tensor", fe2)
            S.wait("tensor", e_ev)
            for t2 in range(2):
                fn = (lambda e, bk2=bk2, t2=t2: e.transpose(
                    ps[bk2][:, t2 * 16:(t2 + 1) * 16], prji[:, t2 * 128:(t2 + 1) * 128],
                    eyev[0:16, 0:16]))
                e_tr = S.op("tensor", fn, inc=spe) if t2 == 1 else S.op("tensor", fn)
            S.wait("vector", e_tr)
            e_mod = vop(lambda e, bk2=bk2, jg=jg: e.tensor_copy(
                modv[:, jg * 2:(jg + 1) * 2], bass.AP(ps[bk2], 0, [[512, 128], [16, 2]])))
            BK.put(bk2, e_mod)
            if jg >= 8 and jg % 2 == 0:
                presplit_half(*P0H[(jg - 8) // 2], qeng="scalar")

        e_wmod_pe = (spe, spe.n)
        e_wbt = None

        S.wait("vector", e_mod)
        vc(lambda e: e.tensor_scalar_add(dmod[:, 8:16], modv[:, 8:16], 1.0))
        vc(lambda e: e.tensor_scalar_add(dmod[:, 32:40], modv[:, 32:40], 1.0))
        for src_c in (16, 40):
            vc(lambda e, s=src_c: e.tensor_single_scalar(rtr[:, 0, 0:8], modv[:, s:s + 8], 0.0, ALU.is_ge))
            vc(lambda e: e.tensor_scalar(out=rtr[:, 0, 0:8], in0=rtr[:, 0, 0:8], scalar1=2.0,
                                         scalar2=-1.0, op0=ALU.mult, op1=ALU.add))
            vc(lambda e, s=src_c: e.tensor_tensor(rtr[:, 1, 0:8], modv[:, s:s + 8], rtr[:, 0, 0:8], ALU.mult))
            vc(lambda e: e.tensor_scalar_max(rtr[:, 1, 0:8], rtr[:, 1, 0:8], 0.25))
            e_dmod = vc(lambda e, s=src_c: e.tensor_tensor(dmod[:, s:s + 8], rtr[:, 1, 0:8],
                                                           rtr[:, 0, 0:8], ALU.mult))
        if DEBUG_TAPS:
            dma("gpsimd", dG, dbg4_t[:, 0:48], modv[:], waits=(e_dmod,))
            dma("gpsimd", dG, dbg4_t[:, 48:96], dmod[:], waits=(e_dmod,))
        S.wait("vector", e_wr)
        vc(lambda e: e.tensor_copy(wrp[:, 0, :, :].rearrange("p a b -> p (a b)"),
                                   wrv[:].rearrange("p a b -> p (a b)")))
        e_wrp = vc(lambda e: e.tensor_tensor(wrp[:, 1, :, :].rearrange("p a b -> p (a b)"),
                                             wrv[:].rearrange("p a b -> p (a b)"),
                                             wrp[:, 0, :, :].rearrange("p a b -> p (a b)"), ALU.subtract))
        S.wait("vector", e_iat)
        e_ia2 = vc(lambda e: e.tensor_scalar_mul(iA2T[:], iA2T[:], 2.0))
        S.wait("vector", e_ib)
        vop(lambda e: e.memset(accA[:], 0.0))
        e_zacc = vop(lambda e: e.memset(accB[:], 0.0))

        state = {"sq_user": e_zacc}

        def ln_stats(src_view, W, deps=(), pair=None):
            bk1, f1 = BK.get()
            bk2, f2 = BK.get()
            for ed in deps:
                S.wait("tensor", ed)
                S.wait("scalar", ed)
                S.wait("vector", ed)
            S.wait("tensor", f1)
            S.wait("tensor", f2)
            if pair is None:
                for k in range(8):
                    fn = (lambda e, k=k, bk1=bk1: e.matmul(ps[bk1][:, 0:W], onesv[:], src_view(k),
                                                           start=(k == 0), stop=(k == 7)))
                    e_s1 = S.op("tensor", fn, inc=spe) if k == 7 else S.op("tensor", fn)
            else:
                S.wait("tensor", e_onb)
                for k in range(8):
                    for hl in range(2):
                        fn = (lambda e, k=k, hl=hl, bk1=bk1: e.matmul(
                            ps[bk1][:, 0:W], onebv[:], pair(k, hl),
                            start=(k == 0 and hl == 0), stop=(k == 7 and hl == 1)))
                        e_s1 = S.op("tensor", fn, inc=spe) if (k == 7 and hl == 1) \
                            else S.op("tensor", fn)
            S.wait("scalar", state["sq_user"])
            sq_pe = []
            for k in range(8):
                if k >= 2:
                    S.wait("scalar", sq_pe[k - 2])
                e_sq = vop(lambda e, k=k: e.activation(out=sq[:, k % 2, 0:W], in_=src_view(k),
                                                       func=AF.Square), "scalar")
                S.wait("tensor", e_sq)
                fn = (lambda e, k=k, bk2=bk2: e.matmul(ps[bk2][:, 0:W], onesv[:], sq[:, k % 2, 0:W],
                                                       start=(k == 0), stop=(k == 7)))
                ed = S.op("tensor", fn, inc=spe)
                sq_pe.append(ed)
            e_s2 = sq_pe[-1]
            S.wait("vector", e_s1)
            S.wait("vector", e_s2)
            vc(lambda e, bk1=bk1: e.tensor_scalar(out=stats[:, 0, 0:W], in0=ps[bk1][:, 0:W],
                                                  scalar1=conv[:, C_INV1024:C_INV1024 + 1],
                                                  scalar2=None, op0=ALU.mult))
            e_e1 = vc(lambda e, bk2=bk2: e.tensor_scalar(out=stats[:, 1, 0:W], in0=ps[bk2][:, 0:W],
                                                         scalar1=conv[:, C_INV1024:C_INV1024 + 1],
                                                         scalar2=None, op0=ALU.mult))
            BK.put(bk1, e_e1)
            BK.put(bk2, e_e1)
            vc(lambda e: e.tensor_tensor(stats[:, 2, 0:W], stats[:, 0, 0:W], stats[:, 0, 0:W], ALU.mult))
            e_var = vc(lambda e: e.tensor_tensor(stats[:, 1, 0:W], stats[:, 1, 0:W],
                                                 stats[:, 2, 0:W], ALU.subtract))
            S.wait("scalar", e_var)
            e_std = vc(lambda e: e.activation(out=stats[:, 2, 0:W], in_=stats[:, 1, 0:W], func=AF.Sqrt,
                                              bias=conv[:, C_LNEPS:C_LNEPS + 1], scale=1.0), "scalar")
            S.wait("vector", e_std)
            e_r = vc(lambda e: e.reciprocal(stats[:, 2, 0:W], stats[:, 2, 0:W]))
            e_nm = vc(lambda e: e.tensor_tensor(stats[:, 3, 0:W], stats[:, 0, 0:W],
                                                stats[:, 2, 0:W], ALU.mult))
            return e_nm, e_s2

        def apply_ln(dst_view, src_view, W, sc, sh, e_stats, e_pe, pre_waits=(),
                     u_buf=None, src3=None):
            S.wait("vector", e_stats)
            S.wait("vector", e_pe)
            for ed in pre_waits:
                S.wait("vector", ed)
            sc_t, sc_c = sc
            sh_t, sh_c = sh
            if src3 is not None:
                u3, uk = u_buf
                rb3 = bass.AP(stats, 2 * 256, [[1024, 128], [0, 8], [1, W]])
                nb3 = bass.AP(stats, 3 * 256, [[1024, 128], [0, 8], [1, W]])
                vc(lambda e: e.tensor_tensor(u3, src3, rb3, ALU.mult))
                vc(lambda e: e.tensor_tensor(u3, u3, nb3, ALU.subtract))
                last = None
                for k in range(8):
                    fn = (lambda e, k=k: e.tensor_scalar(
                        out=dst_view(k), in0=uk(k),
                        scalar1=sc_t[:, sc_c + k:sc_c + k + 1],
                        scalar2=sh_t[:, sh_c + k:sh_c + k + 1], op0=ALU.mult, op1=ALU.add))
                    last = vc(fn) if k == 7 else vop(fn)
                return last
            last = None
            for k in range(8):
                rb = bass.AP(stats, 2 * 256, [[1024, 128], [1, W]])
                nb = bass.AP(stats, 3 * 256, [[1024, 128], [1, W]])
                vc(lambda e, k=k, rb=rb: e.tensor_tensor(sq[:, k % 2, 0:W], src_view(k), rb, ALU.mult))
                vc(lambda e, k=k, nb=nb: e.tensor_tensor(sq[:, k % 2, 0:W], sq[:, k % 2, 0:W], nb, ALU.subtract))
                last = vc(lambda e, k=k: e.tensor_scalar(
                    out=dst_view(k), in0=sq[:, k % 2, 0:W],
                    scalar1=sc_t[:, sc_c + k:sc_c + k + 1],
                    scalar2=sh_t[:, sh_c + k:sh_c + k + 1], op0=ALU.mult, op1=ALU.add))
            state["sq_user"] = last
            return last

        if STOP_AT == "p0":
            return finish()
        S.wait("vector", e_xxt)
        S.wait("vector", e_dmod)
        xxv = lambda k: xx3[:, k, :]
        e_st, e_pe0 = ln_stats(xxv, 2, deps=(e_xxt, e_sm["consts"]))
        e_hx = apply_ln(lambda k: hx3[:, k, :], xxv, 2, (dmod, 8), (dmod, 0), e_st, e_pe0)
        hxv = lambda k: hx3[:, k, :]
        e_st2, e_pe1 = ln_stats(hxv, 2, deps=(e_hx, e_sm["gam"], e_sm["bet"]))
        e_xnx = apply_ln(lambda k: xnx3[:, k, :], hxv, 2, (gamv, 0), (betv, 0), e_st2, e_pe1)
        e_xnx = vc(lambda e: e.tensor_scalar(out=xnx3[:, :, 0:1], in0=xnx3[:, :, 0:1],
                                             scalar1=conv[:, C_XTRAMASK:C_XTRAMASK + 1],
                                             scalar2=None, op0=ALU.mult))

        if STOP_AT == "p1":
            return finish()
        h_free = {-2: e_wmod_pe, -1: e_wmod_pe}
        xn_free = {-2: None, -1: None}
        P = {}

        def part1a(idx):
            me = {}
            xvw = lambda k: xsl[:, idx % 2, k, :]
            x3 = xsl[:, idx % 2, :, :]
            e_st, e_pe_a = ln_stats(xvw, 256, deps=(x_edge[idx], e_dmod))
            e_ap = apply_ln(lambda k: hcv(idx % 2)[:, k, :], xvw, 256, (dmod, 8), (dmod, 0),
                            e_st, e_pe_a, pre_waits=(h_free[idx - 2],),
                            u_buf=(vv[:, :, :], lambda k: vv[:, k, :]), src3=x3)
            h3f = hsl[:, :]
            S.wait("scalar", e_ap)
            S.wait("scalar", h_free[idx - 2])
            e_hi = vop(lambda e, idx=idx: e.activation(out=hpb(idx % 2, 0), in_=h3f,
                                                       func=AF.Copy), "scalar")
            S.wait("vector", e_hi)
            me["h"] = vc(lambda e, idx=idx: e.tensor_tensor(hpb(idx % 2, 1), h3f,
                                                            hpb(idx % 2, 0), ALU.subtract))
            return me

        def part1b(idx, e_h):
            me = {}
            hvw = lambda k: hcv(idx % 2)[:, k, :]
            h3 = hcv(idx % 2)[:, :, :]
            e_st2, e_pe_b = ln_stats(hvw, 256, deps=(e_h, e_sm["gam"], e_sm["bet"]),
                                     pair=lambda k, hl, idx=idx: hpv(idx % 2, hl)[:, k, :])
            if idx not in (0, 8):
                S.wait("vector", e_st2)
                rb0 = bass.AP(stats, 2 * 256, [[1024, 128], [0, 8], [1, 1]])
                nb0 = bass.AP(stats, 3 * 256, [[1024, 128], [0, 8], [1, 1]])
                vc(lambda e: e.tensor_tensor(sq[:, 0, 0:8].rearrange("p (a b) -> p a b", b=1),
                                             hcv(idx % 2)[:, :, 0:1], rb0, ALU.mult))
                vc(lambda e: e.tensor_tensor(sq[:, 0, 0:8].rearrange("p (a b) -> p a b", b=1),
                                             sq[:, 0, 0:8].rearrange("p (a b) -> p a b", b=1),
                                             nb0, ALU.subtract))
                e_pk = None
                for k in range(8):
                    fn = (lambda e, k=k: e.tensor_scalar(
                        out=xcol[:, k:k + 1], in0=sq[:, 0, k:k + 1],
                        scalar1=gamv[:, k:k + 1], scalar2=betv[:, k:k + 1],
                        op0=ALU.mult, op1=ALU.add))
                    e_pk = vc(fn) if k == 7 else vop(fn)
                state["sq_user"] = e_pk
                psl = (idx - 1) % 2
                vc(lambda e, psl=psl: e.tensor_copy(xnp(psl, 0)[:, :, 255:256], xc3))
                me["patch"] = vc(lambda e, psl=psl: e.tensor_tensor(
                    xnp(psl, 1)[:, :, 255:256], xc3,
                    xnp(psl, 0)[:, :, 255:256], ALU.subtract))
            e_apx = apply_ln(lambda k: hcv(0)[:, k, :], hvw, 256, (gamv, 0), (betv, 0),
                             e_st2, e_pe_b,
                             u_buf=(vv[:, :, :], lambda k: vv[:, k, :]), src3=h3)
            S.wait("scalar", e_apx)
            S.wait("scalar", xn_free[idx - 2])
            e_hix = vop(lambda e, idx=idx: e.activation(out=xnp(idx % 2, 0)[:, :, 0:255],
                                                        in_=hcv(0)[:, :, 1:256],
                                                        func=AF.Copy), "scalar")
            S.wait("vector", e_hix)
            me["xn"] = vc(lambda e, idx=idx: e.tensor_tensor(
                xnp(idx % 2, 1)[:, :, 0:255], hcv(0)[:, :, 1:256],
                xnp(idx % 2, 0)[:, :, 0:255], ALU.subtract))
            if "patch" not in me:
                me["patch"] = me["xn"]
            return me

        def patch_last(idx):
            col = 1 if idx == 7 else 0
            S.wait("vector", e_xnx)
            vc(lambda e, col=col: e.tensor_copy(xc3, xnx3[:, :, col:col + 1]))
            vc(lambda e, idx=idx: e.tensor_copy(xnp(idx % 2, 0)[:, :, 255:256], xc3))
            return vc(lambda e, idx=idx: e.tensor_tensor(
                xnp(idx % 2, 1)[:, :, 255:256], xc3,
                xnp(idx % 2, 0)[:, :, 255:256], ALU.subtract))

        def partZ(idx, e_h):
            me = {}
            S.wait("tensor", e_h)
            S.wait("tensor", e_eyeb)
            for ct in range(2):
                for fg in range(2):
                    for hl in range(2):
                        bkt, fet = BK.get()
                        S.wait("tensor", fet)
                        for ff in range(4):
                            f = fg * 4 + ff
                            fn = (lambda e, bkt=bkt, ct=ct, f=f, ff=ff, hl=hl, idx=idx:
                                  e.matmul(ps[bkt][:, ff * 128:(ff + 1) * 128],
                                           hpv(idx % 2, hl)[:, f, ct * 128:ct * 128 + 128],
                                           eyeb[:], start=True, stop=True))
                            e_t = S.op("tensor", fn, inc=spe) if ff == 3 else S.op("tensor", fn)
                        S.wait("vector", e_t)
                        ed = vop(lambda e, bkt=bkt, ct=ct, fg=fg, hl=hl: e.tensor_copy(
                            ztm_p(ct, hl)[:, fg * 512:(fg + 1) * 512], ps[bkt][:, 0:512]))
                        BK.put(bkt, ed)
            bkp, fep = BK.get()
            S.wait("tensor", fep)
            S.wait("tensor", e_ibp)
            for k in range(8):
                for pi, (pa, pb) in enumerate(((0, 0), (0, 1), (1, 0))):
                    fn = (lambda e, k=k, bkp=bkp, pa=pa, pb=pb, pi=pi, idx=idx: e.matmul(
                        ps[bkp][0:16, 0:256], ibtp(pa)[:, k, :], hpv(idx % 2, pb)[:, k, :],
                        start=(k == 0 and pi == 0), stop=(k == 7 and pi == 2)))
                    e_pi = S.op("tensor", fn, inc=spe) if (k == 7 and pi == 2) \
                        else S.op("tensor", fn)
            me["hpe"] = e_pi
            S.wait("vector", e_pi)
            e_piv = vop(lambda e, bkp=bkp: e.tensor_copy(prji[:], ps[bkp][0:16, 0:256]))
            BK.put(bkp, e_piv)
            bkq, feq = BK.get()
            S.wait("tensor", feq)
            S.wait("tensor", e_piv)
            for ct in range(2):
                fn = (lambda e, bkq=bkq, ct=ct: e.transpose(
                    ps[bkq][:, ct * 16:(ct + 1) * 16], prji[:, ct * 128:(ct + 1) * 128],
                    eyev[0:16, 0:16]))
                e_pt = S.op("tensor", fn, inc=spe) if ct == 1 else S.op("tensor", fn)
            S.wait("vector", e_pt)
            e_ptm = vop(lambda e, bkq=bkq: e.tensor_copy(ptmi[:].rearrange("p a b -> p (a b)"),
                                                         ps[bkq][:, 0:32]))
            BK.put(bkq, e_ptm)
            me["ptmi"] = e_ptm
            return me

        def part2(idx, e_xn_ready):
            me = {}
            fh = idx < 8
            for f in range(8):
                if f % 2 == 0:
                    bkv, fev = BK.get()
                    S.wait("tensor", fev)
                    if f == 0:
                        S.wait("tensor", e_xn_ready)
                        S.wait("tensor", e_wpt)
                for k in range(8):
                    for pi, (pa, pb) in enumerate(((0, 0), (0, 1), (1, 0))):
                        last = (k == 7 and f % 2 == 1 and pi == 2)
                        fn = (lambda e, f=f, k=k, bkv=bkv, pa=pa, pb=pb, pi=pi, idx=idx,
                              last=last: e.matmul(
                            ps[bkv][:, (f % 2) * 256:(f % 2) * 256 + 256],
                            wpp(pa)[:, k, f * 128:(f + 1) * 128],
                            xnp(idx % 2, pb)[:, k, :],
                            start=(k == 0 and f % 2 == 0 and pi == 0), stop=last))
                        if last:
                            e_v = S.op("tensor", fn, inc=spe)
                        else:
                            S.op("tensor", fn)
                if f % 2 == 1:
                    S.wait("vector", e_v)
                    e_vev = vop(lambda e, bkv=bkv, f=f: e.tensor_copy(
                        vsl[:, (f - 1) * 256:(f + 1) * 256], ps[bkv][:, 0:512]))
                    BK.put(bkv, e_vev)
            me["xnpe"] = e_v
            bkp, fep = BK.get()
            S.wait("tensor", fep)
            S.wait("tensor", e_vev)
            S.wait("tensor", e_ia)
            for k in range(8):
                fn = (lambda e, k=k, bkp=bkp: e.matmul(
                    ps[bkp][0:16, 0:256], initAv[:, k, :], vv[:, k, :],
                    start=(k == 0), stop=(k == 7)))
                e_pe2 = S.op("tensor", fn, inc=spe) if k == 7 else S.op("tensor", fn)
            S.wait("vector", e_pe2)
            e_pev = vop(lambda e, bkp=bkp: e.tensor_copy(prje[:], ps[bkp][0:16, 0:256]))
            BK.put(bkp, e_pev)
            bkq, feq = BK.get()
            S.wait("tensor", feq)
            S.wait("tensor", e_pev)
            for ct in range(2):
                fn = (lambda e, bkq=bkq, ct=ct: e.transpose(
                    ps[bkq][:, ct * 16:(ct + 1) * 16], prje[:, ct * 128:(ct + 1) * 128],
                    eyev[0:16, 0:16]))
                e_pt = S.op("tensor", fn, inc=spe) if ct == 1 else S.op("tensor", fn)
            S.wait("vector", e_pt)
            e_pf = vc(lambda e, bkq=bkq: e.tensor_copy(ptmef[:], ps[bkq][:, 0:32]))
            BK.put(bkq, e_pf)
            vc(lambda e: e.tensor_copy(ptmep(0), ptmef[:]))
            e_ptme = vc(lambda e: e.tensor_tensor(ptmep(1), ptmef[:],
                                                  ptmep(0), ALU.subtract))
            for ct in range(2):
                for fg in range(2):
                    bkt, fet = BK.get()
                    S.wait("tensor", fet)
                    for ff in range(4):
                        f = fg * 4 + ff
                        fn = (lambda e, bkt=bkt, ct=ct, f=f, ff=ff: e.transpose(
                            ps[bkt][:, ff * 128:(ff + 1) * 128], vv[:, f, ct * 128:ct * 128 + 128],
                            eyev[:]))
                        e_t = S.op("tensor", fn, inc=spe) if ff == 3 else S.op("tensor", fn)
                    S.wait("vector", e_t)
                    e_vc = vop(lambda e, bkt=bkt, ct=ct, fg=fg: e.tensor_copy(
                        vctm[:, ct, fg * 512:(fg + 1) * 512], ps[bkt][:, 0:512]))
                    BK.put(bkt, e_vc)
            acc_last = None

            def acc_upd(bkd, hg, acc_t):
                if fh:
                    return vop(lambda e, bkd=bkd, hg=hg, acc_t=acc_t: e.scalar_tensor_tensor(
                        out=acc_t[:, hg * 512:(hg + 1) * 512], in0=ps[bkd][0:16, 0:512],
                        scalar=conv[0:16, C_FHMASK:C_FHMASK + 1],
                        in1=acc_t[:, hg * 512:(hg + 1) * 512], op0=ALU.mult, op1=ALU.add))
                return vop(lambda e, bkd=bkd, hg=hg, acc_t=acc_t: e.tensor_tensor(
                    acc_t[:, hg * 512:(hg + 1) * 512], acc_t[:, hg * 512:(hg + 1) * 512],
                    ps[bkd][0:16, 0:512], ALU.add))

            for hg in range(2):
                bkd, fed = BK.get()
                S.wait("tensor", fed)
                S.wait("tensor", e_vc)
                for ct in range(2):
                    fn = (lambda e, bkd=bkd, ct=ct, hg=hg: e.matmul(
                        ps[bkd][0:16, 0:512], ptmi[:, ct, :],
                        vctm[:, ct, hg * 512:(hg + 1) * 512],
                        start=(ct == 0), stop=(ct == 1)))
                    e_d = S.op("tensor", fn, inc=spe) if ct == 1 else S.op("tensor", fn)
                S.wait("vector", e_d)
                acc_last = acc_upd(bkd, hg, accA)
                BK.put(bkd, acc_last)
            for hg in range(2):
                bkd, fed = BK.get()
                S.wait("tensor", fed)
                for ct in range(2):
                    for pi, (pa, pb) in enumerate(((0, 0), (0, 1), (1, 0))):
                        fn = (lambda e, bkd=bkd, ct=ct, hg=hg, pa=pa, pb=pb, pi=pi: e.matmul(
                            ps[bkd][0:16, 0:512], ptmep(pa)[:, ct * 16:(ct + 1) * 16],
                            ztm_p(ct, pb)[:, hg * 512:(hg + 1) * 512],
                            start=(ct == 0 and pi == 0), stop=(ct == 1 and pi == 2)))
                        e_d = S.op("tensor", fn, inc=spe) if (ct == 1 and pi == 2) \
                            else S.op("tensor", fn)
                S.wait("vector", e_d)
                acc_last = acc_upd(bkd, hg, accB)
                BK.put(bkd, acc_last)
            me["acc"] = acc_last
            return me

        def part3(idx, e_h_ready, e_acc_prev, e_spill_prev):
            s = idx - 8
            me = {}
            S.wait("vector", e_acc_prev)
            S.wait("vector", e_zacc)
            vc(lambda e: e.tensor_tensor(a2ef[:], accA[:], accA[:], ALU.mult))
            vc(lambda e: e.tensor_reduce(c16[:, 0:1], a2ef[:], AX.X, ALU.add))
            vc(lambda e: e.tensor_tensor(beff[:], accB[:], accB[:], ALU.mult))
            vc(lambda e: e.tensor_reduce(c16[:, 1:2], beff[:], AX.X, ALU.add))
            e_rs = vc(lambda e: e.tensor_tensor(c16[:, 2:3], c16[:, 0:1], c16[:, 1:2], ALU.add))
            bkc, fec = BK.get()
            S.wait("tensor", fec)
            S.wait("tensor", e_rs)
            e_cs = S.op("tensor", lambda e, bkc=bkc: e.matmul(
                ps[bkc][0:16, 0:1], onesv[0:16, 0:16], c16[:, 2:3], start=True, stop=True), inc=spe)
            S.wait("scalar", e_cs)
            e_nrm = vop(lambda e, bkc=bkc: e.activation(out=c16[:, 3:4], in_=ps[bkc][0:16, 0:1],
                                                        func=AF.Sqrt,
                                                        bias=conv[0:16, C_ZERO:C_ZERO + 1],
                                                        scale=conv[0:16, C_LR2:C_LR2 + 1]), "scalar")
            BK.put(bkc, e_nrm)
            S.wait("vector", e_nrm)
            vc(lambda e: e.tensor_scalar(out=c16[:, 3:4], in0=c16[:, 3:4],
                                         scalar1=conv[0:16, C_CLIPEPS:C_CLIPEPS + 1],
                                         scalar2=None, op0=ALU.add))
            vc(lambda e: e.reciprocal(c16[:, 4:5], c16[:, 3:4]))
            vc(lambda e: e.tensor_scalar_min(c16[:, 4:5], c16[:, 4:5], 1.0))
            vc(lambda e: e.tensor_scalar(out=c16[:, 5:6], in0=c16[:, 4:5],
                                         scalar1=conv[0:16, C_N2LR:C_N2LR + 1], scalar2=None, op0=ALU.mult))
            vc(lambda e: e.tensor_scalar(out=c16[:, 6:7], in0=c16[:, 4:5],
                                         scalar1=conv[0:16, C_NLR:C_NLR + 1], scalar2=None, op0=ALU.mult))
            S.wait("vector", e_ia2)
            vc(lambda e: e.scalar_tensor_tensor(out=a2ef[:], in0=accA[:], scalar=c16[:, 5:6],
                                                in1=iA2T[:], op0=ALU.mult, op1=ALU.add))
            e_ab = vc(lambda e: e.scalar_tensor_tensor(out=beff[:], in0=accB[:], scalar=c16[:, 6:7],
                                                       in1=iBv[:], op0=ALU.mult, op1=ALU.add))
            bkb, feb = BK.get()
            S.wait("tensor", feb)
            S.wait("tensor", e_ab)
            for k in range(8):
                fn = (lambda e, bkb=bkb, k=k: e.transpose(
                    ps[bkb][:, k * 16:(k + 1) * 16], beff[:, k * 128:(k + 1) * 128], eyev[0:16, 0:16]))
                e_bt = S.op("tensor", fn, inc=spe) if k == 7 else S.op("tensor", fn)
            S.wait("vector", e_bt)
            e_bf = vc(lambda e, bkb=bkb: e.tensor_copy(btf_w[:], ps[bkb][:, 0:128]))
            BK.put(bkb, e_bf)
            vc(lambda e: e.tensor_copy(btpf(0), btf_w[:]))
            e_btp = vc(lambda e: e.tensor_tensor(btpf(1), btf_w[:],
                                                 btpf(0), ALU.subtract))
            base_banks = []
            for bg in range(4):
                bkx, fex = BK.get()
                S.wait("tensor", fex)
                base_banks.append(bkx)
                if bg == 0:
                    S.wait("tensor", e_h_ready)
                    S.wait("tensor", e_wbt)
                for half in range(2):
                    hh = bg * 2 + half
                    for k in range(8):
                        for pi, (pa, pb) in enumerate(((0, 0), (0, 1), (1, 0))):
                            S.op("tensor", (lambda e, bkx=bkx, hh=hh, half=half, k=k,
                                            pa=pa, pb=pb, pi=pi, idx=idx: e.matmul(
                                ps[bkx][:, half * 256:half * 256 + 256],
                                wbp(pa)[:, k, hh * 128:(hh + 1) * 128],
                                hpv(idx % 2, pb)[:, k, :],
                                start=(k == 0 and half == 0 and pi == 0), stop=False)))
            bkm, fem = BK.get()
            S.wait("tensor", fem)
            S.wait("tensor", e_btp)
            for k in range(8):
                for pi, (pa, pb) in enumerate(((0, 0), (0, 1), (1, 0))):
                    fn = (lambda e, bkm=bkm, k=k, pa=pa, pb=pb, pi=pi, idx=idx: e.matmul(
                        ps[bkm][0:16, 0:256], btp(pa)[:, k, :], hpv(idx % 2, pb)[:, k, :],
                        start=(k == 0 and pi == 0), stop=(k == 7 and pi == 2)))
                    e_md = S.op("tensor", fn, inc=spe) if (k == 7 and pi == 2) \
                        else S.op("tensor", fn)
            me["hpe"] = e_md
            S.wait("vector", e_md)
            e_mv = vop(lambda e, bkm=bkm: e.tensor_copy(mids[:], ps[bkm][0:16, 0:256]))
            BK.put(bkm, e_mv)
            S.wait("tensor", e_mv)
            e_l = {}
            for bg in range(4):
                for half in range(2):
                    hh = bg * 2 + half
                    fn = (lambda e, bg=bg, half=half, hh=hh: e.matmul(
                        ps[base_banks[bg]][:, half * 256:half * 256 + 256],
                        a2ef[:, hh * 128:(hh + 1) * 128], mids[:], start=False, stop=(half == 1)))
                    e_l[bg] = S.op("tensor", fn, inc=spe) if half == 1 else S.op("tensor", fn)
            x2e = []
            for bg in range(4):
                S.wait("vector", e_l[bg])
                if bg == 0:
                    S.wait("vector", e_spill_prev)
                    S.wait("vector", ps_state["last_load"] if s == 0 else None)
                for half in range(2):
                    hh = bg * 2 + half
                    ed = vop(lambda e, bg=bg, half=half, hh=hh, s=s: e.scalar_tensor_tensor(
                        out=x2sl[:, s % 2, hh, :],
                        in0=ps[base_banks[bg]][:, half * 256:half * 256 + 256],
                        scalar=dmod[:, 16 + hh:16 + hh + 1],
                        in1=xsl[:, (s + 8) % 2, hh, :], op0=ALU.mult, op1=ALU.add))
                    x2e.append(ed)
                BK.put(base_banks[bg], x2e[-1])
            me["x2"] = x2e[-1]
            me["spill"] = dma("gpsimd", dG,
                              x2sp.rearrange("(kc p) t -> p kc t", p=128)[:, :, s * 256:(s + 1) * 256],
                              x2sl[:, s % 2, :, :], waits=(me["x2"],))
            return me

        P[0] = part1a(0)
        P[0].update(part1b(0, P[0]["h"]))
        Z = {0: partZ(0, P[0]["h"])}
        h_free[0] = Z[0]["hpe"]
        PRESPLIT = {0: (9,), 1: (0,), 2: (1,), 3: (2,), 4: (3,), 5: (4,)}
        for s in range(8):
            nxt = s + 1
            P[nxt] = part1a(nxt)
            P[nxt].update(part1b(nxt, P[nxt]["h"]))
            if s < 7:
                e_xn_ready = P[nxt]["patch"]
                load_x(s + 2, waits=(P[s]["h"],))
            else:
                load_x(9, waits=(P[7]["h"],))
                e_xn_ready = patch_last(7)
            m2 = part2(s, e_xn_ready)
            P[s].update(m2)
            xn_free[s] = m2["xnpe"]
            Z[nxt] = partZ(nxt, P[nxt]["h"])
            h_free[nxt] = Z[nxt]["hpe"]
            for ei in PRESPLIT.get(s, ()):
                presplit_item(PS_ITEMS[ei], "sync" if ei in (9, 1, 3) else "scalar")
        for half in range(2):
            for pl in range(2):
                e_wbt = dma("sync", dW,
                            wbp(pl)[:, :, half * 512:(half + 1) * 512],
                            wbs[half, pl].rearrange("(kc p) f -> p kc f", p=128),
                            waits=(ps_state["w1_store"], ps_state["w2_store"]))
        if STOP_AT == "fh":
            return finish()
        prev_acc = P[7]["acc"]
        spill_hist = {}
        for s in range(8):
            idx = 8 + s
            if s < 7:
                P[idx + 1] = part1a(idx + 1)
            m3 = part3(idx, P[idx]["h"], prev_acc, spill_hist.get(idx - 2))
            P[idx].update(m3)
            h_free[idx] = m3["hpe"]
            spill_hist[idx] = m3["spill"]
            if s < 6:
                load_x(idx + 2, waits=(m3["x2"],))
            if s < 7:
                P[idx + 1].update(part1b(idx + 1, P[idx + 1]["h"]))
                e_xn_ready = P[idx + 1]["patch"]
            else:
                e_xn_ready = patch_last(15)
            m2 = part2(idx, e_xn_ready)
            P[idx].update(m2)
            xn_free[idx] = m2["xnpe"]
            prev_acc = m2["acc"]
            if s < 7:
                Z[idx + 1] = partZ(idx + 1, P[idx + 1]["h"])

        e_attn_pe = (spe, spe.n)
        if STOP_AT == "attn":
            return finish()

        dbg_backedge = {}
        x2_edge = {}
        def load_x2(g, waits):
            x2_edge[g] = dma("scalar", dX, x2sl[:, g % 2, :, :],
                             x2sp.rearrange("(kc p) t -> p kc t", p=128)[:, :, g * 256:(g + 1) * 256],
                             waits=waits)
        load_x2(0, (P[14]["spill"], P[8]["spill"]))
        load_x2(1, (P[15]["spill"], P[9]["spill"]))

        wex_edge = {}
        first_wex = [True]
        def load_wex(g, ei, buf, waits=()):
            deps = list(waits)
            if first_wex[0]:
                deps.append(e_attn_pe)
                first_wex[0] = False
            e1 = dma("sync", dW,
                     wslab[:, buf * 8192:buf * 8192 + 4096].bitcast(BF16)
                     .rearrange("p (pl k f) -> p pl k f", pl=2, k=8),
                     w1s[ei].rearrange("pl (kc p) f -> p pl kc f", p=128), waits=deps)
            wex_edge[(g, ei)] = dma("sync", dW,
                                    wslab[:, buf * 8192 + 4096:(buf + 1) * 8192].bitcast(BF16)
                                    .rearrange("p (pl f h) -> p pl f h", pl=2, f=4),
                                    w2s[ei].rearrange("pl (fc p) h -> p pl fc h", p=128))

        load_wex(0, 0, 0)
        load_wex(0, 1, 1)

        h2_free = {-2: None, -1: None}
        mask_gp = {0: None, 1: None}
        wex_pe = {0: None, 1: None}
        out_dma = []
        ln_done = {}

        def emit_ln(g):
            xv2 = lambda k, g=g: x2sl[:, g % 2, k, :]
            e_st, e_pe_g = ln_stats(xv2, 256, deps=(x2_edge[g],))
            e_h2 = apply_ln(lambda k, g=g: hcv(g % 2)[:, k, :], xv2, 256, (dmod, 32), (dmod, 24),
                            e_st, e_pe_g,
                            pre_waits=(dbg_backedge.get("h2") if DEBUG_TAPS else None,))
            S.wait("vector", h2_free[g - 2])
            if g == 0:
                S.wait("vector", e_attn_pe)
            vc(lambda e, g=g: e.tensor_copy(h2p(g % 2, 0).rearrange("p a b -> p (a b)"),
                                            hcv(g % 2)[:, :, :].rearrange("p a b -> p (a b)")))
            ln_done[g] = vc(lambda e, g=g: e.tensor_tensor(
                h2p(g % 2, 1).rearrange("p a b -> p (a b)"),
                hcv(g % 2)[:, :, :].rearrange("p a b -> p (a b)"),
                h2p(g % 2, 0).rearrange("p a b -> p (a b)"), ALU.subtract))
            if DEBUG_TAPS:
                dbg_backedge["h2"] = dma(
                    "gpsimd", dG, dbg_t.rearrange("(kc p) t -> p kc t", p=128)[:, :, g * 256:(g + 1) * 256],
                    hcv(g % 2)[:, :, :], waits=(e_h2,))

        for g in range(8):
            if g not in ln_done:
                emit_ln(g)
            e_h2s = ln_done[g]
            bkr, fer = BK.get()
            S.wait("tensor", fer)
            S.wait("tensor", e_h2s)
            S.wait("tensor", e_wrp)
            for k in range(8):
                for pw, ph in ((0, 0), (0, 1), (1, 0)):
                    fn = (lambda e, bkr=bkr, k=k, g=g, pw=pw, ph=ph: e.matmul(
                        ps[bkr][0:8, 0:256], wrp[:, pw, k, :], h2p(g % 2, ph)[:, k, :],
                        start=(k == 0 and pw == 0 and ph == 0), stop=(k == 7 and pw == 1)))
                    e_lg = S.op("tensor", fn, inc=spe) if (k == 7 and pw == 1) else S.op("tensor", fn)
            S.wait("vector", e_lg)
            e_lgv = vop(lambda e, bkr=bkr: e.tensor_copy(lgfm[:], ps[bkr][0:8, 0:256]))
            BK.put(bkr, e_lgv)
            bkr2, fer2 = BK.get()
            S.wait("tensor", fer2)
            S.wait("tensor", e_lgv)
            for ct in range(2):
                fn = (lambda e, bkr2=bkr2, ct=ct: e.transpose(
                    ps[bkr2][:, ct * 8:(ct + 1) * 8], lgfm[:, ct * 128:(ct + 1) * 128],
                    eyev[0:8, 0:8]))
                e_lt = S.op("tensor", fn, inc=spe) if ct == 1 else S.op("tensor", fn)
            S.wait("vector", e_lt)
            e_ltv = vop(lambda e, bkr2=bkr2: e.tensor_copy(lgtm[:].rearrange("p a b -> p (a b)"),
                                                           ps[bkr2][:, 0:16]))
            BK.put(bkr2, e_ltv)
            S.wait("vector", e_ltv)
            if DEBUG_TAPS:
                S.wait("vector", dbg_backedge.get("prfm"))
                S.wait("vector", dbg_backedge.get("m1c"))
            for ct in range(2):
                vc(lambda e, ct=ct: e.max(out=rtr[:, ct, 0:8], in_=lgtm[:, ct, :]))
            for ct in range(2):
                vc(lambda e, ct=ct: e.tensor_copy(m1c[:, ct:ct + 1], rtr[:, ct, 0:1]))
                vc(lambda e, ct=ct: e.tensor_copy(m2c[:, ct:ct + 1], rtr[:, ct, 1:2]))
            e_nm1 = vc(lambda e: e.tensor_scalar_mul(nm1c[:], m1c[:], -1.0))
            S.wait("scalar", e_nm1)
            for ct in range(2):
                e_exp = vc(lambda e, ct=ct: e.activation(
                    out=dvc[:, ct:ct + 1], in_=m2c[:, ct:ct + 1], func=AF.Exp,
                    bias=nm1c[:, ct:ct + 1], scale=1.0), "scalar")
            S.wait("vector", e_exp)
            vc(lambda e: e.tensor_scalar_add(dvc[:], dvc[:], 1.0))
            vc(lambda e: e.reciprocal(dvc[:], dvc[:]))
            vc(lambda e: e.tensor_single_scalar(g1c[:], dvc[:], 1.0, ALU.is_ge))
            for ct in range(2):
                vc(lambda e, ct=ct: e.tensor_single_scalar(eqc[:, ct, :], lgtm[:, ct, :],
                                                           m1c[:, ct:ct + 1], ALU.is_equal))
                e_prt = vc(lambda e, ct=ct: e.tensor_scalar(out=prtm[:, ct, :], in0=eqc[:, ct, :],
                                                            scalar1=g1c[:, ct:ct + 1],
                                                            scalar2=None, op0=ALU.mult))
            bkr3, fer3 = BK.get()
            S.wait("tensor", fer3)
            S.wait("tensor", e_prt)
            for ct in range(2):
                fn = (lambda e, bkr3=bkr3, ct=ct: e.transpose(
                    ps[bkr3][0:8, ct * 128:(ct + 1) * 128], prtm[:, ct, :], eyev[:]))
                e_pf = S.op("tensor", fn, inc=spe) if ct == 1 else S.op("tensor", fn)
            S.wait("vector", e_pf)
            e_pfm = vop(lambda e, bkr3=bkr3: e.tensor_copy(prfm[:], ps[bkr3][0:8, 0:256]))
            BK.put(bkr3, e_pfm)
            if DEBUG_TAPS:
                e_tap2 = dma("gpsimd", dG, dbg2_t[g], prfm[:], waits=(e_pfm,))
                dma("gpsimd", dG, dbg3_t[g][:, 0:2], m1c[:], waits=(e_pfm,), slow=True)
                e_tap3 = dma("gpsimd", dG, dbg3_t[g][:, 2:4], m2c[:], waits=(e_pfm,), slow=True)
                dbg_backedge["prfm"] = e_tap2
                dbg_backedge["m1c"] = e_tap3

            if STOP_AT == f"ffn_r{g}":
                return finish()
            ye_banks = []
            for bg in range(4):
                bky, fey = BK.get()
                S.wait("tensor", fey)
                ye_banks.append(bky)
            ye_last = {}
            e_hpe_last = None
            e_ye_prev = None
            prev_hem1 = None
            for ei in range(9):
                if ei == 6 and g < 7 and (g + 1) not in ln_done:
                    emit_ln(g + 1)
                buf = (g * 9 + ei) % 2
                nxt = (g, ei + 1) if ei < 8 else (g + 1, 0)
                if nxt[0] <= 7 and nxt not in wex_edge and not (NOPREF and nxt[0] > 0):
                    load_wex(nxt[0], nxt[1], (g * 9 + ei + 1) % 2,
                             waits=() if wex_pe[(buf + 1) % 2] is None else (wex_pe[(buf + 1) % 2],))
                if ei < 8:
                    bkm2, fem2 = BK.get()
                    S.wait("tensor", fem2)
                    S.wait("tensor", e_pfm)
                    S.wait("tensor", e_sel)
                    e_mk = S.op("tensor", lambda e, bkm2=bkm2, ei=ei: e.matmul(
                        ps[bkm2][:, 0:256], selv[:, ei * 128:(ei + 1) * 128], prfm[:],
                        start=True, stop=True), inc=spe)
                    S.wait("scalar", e_mk)
                    S.wait("scalar", mask_gp[ei % 2])
                    e_mke = vop(lambda e, bkm2=bkm2, ei=ei: e.copy(maskb[:, ei % 2, :],
                                                                   ps[bkm2][:, 0:256]), "scalar")
                    BK.put(bkm2, e_mke)
                S.wait("tensor", wex_edge[(g, ei)])
                e_hm = None
                for fg in range(2):
                    bkh, feh = BK.get()
                    S.wait("tensor", feh)
                    for half in range(2):
                        f = fg * 2 + half
                        for k in range(8):
                            for pw, ph in ((0, 0), (0, 1), (1, 0)):
                                fn = (lambda e, bkh=bkh, f=f, half=half, k=k, buf=buf, g=g,
                                      pw=pw, ph=ph: e.matmul(
                                    ps[bkh][:, half * 256:half * 256 + 256],
                                    w1p(buf, pw)[:, k, f * 128:(f + 1) * 128],
                                    h2p(g % 2, ph)[:, k, :],
                                    start=(k == 0 and half == 0 and pw == 0 and ph == 0),
                                    stop=(k == 7 and half == 1 and pw == 1)))
                                if k == 7 and half == 1 and pw == 1:
                                    e_he = S.op("tensor", fn, inc=spe)
                                else:
                                    S.op("tensor", fn)
                    e_hpe_last = e_he
                    S.wait("scalar", e_he)
                    if fg == 0:
                        S.wait("scalar", prev_hem1)
                        S.wait("vector", e_ye_prev)
                    e_t1 = {}
                    for half in range(2):
                        f = fg * 2 + half
                        e_t1[half] = vop(lambda e, bkh=bkh, half=half, f=f: e.activation(
                            out=hem(1)[:, f, :], in_=ps[bkh][:, half * 256:half * 256 + 256],
                            func=AF.Relu), "scalar")
                    for half in range(2):
                        f = fg * 2 + half
                        if half == 0:
                            S.wait("vector", e_t1[1])
                            if ei < 8:
                                S.wait("vector", e_mke)
                        if ei < 8:
                            e_msk = vc(lambda e, f=f, ei=ei: e.tensor_tensor(
                                hem(1)[:, f, :], hem(1)[:, f, :], maskb[:, ei % 2, :], ALU.mult))
                            if half == 1:
                                mask_gp[ei % 2] = e_msk
                        e_hm = vc(lambda e, bkh=bkh, half=half, f=f: e.tensor_tensor(
                            hem(0)[:, f, :], hem(1)[:, f, :],
                            ps[bkh][:, half * 256:half * 256 + 256], ALU.mult))
                        vc(lambda e, f=f: e.tensor_copy(hep(0)[:, f, :], hem(0)[:, f, :]))
                        e_hm = vc(lambda e, f=f: e.tensor_tensor(hep(1)[:, f, :], hem(0)[:, f, :],
                                                                 hep(0)[:, f, :], ALU.subtract))
                    BK.put(bkh, e_hm)
                if STOP_AT == f"ffn_he{ei}" and g == 0:
                    return finish()
                S.wait("tensor", e_hm)
                for bg in range(4):
                    for half in range(2):
                        hh = bg * 2 + half
                        for f in range(4):
                            for pw, ph in ((0, 0), (0, 1), (1, 0)):
                                last_pass = (pw == 1)
                                last_all = (half == 1 and f == 3 and bg == 3 and last_pass)
                                fn = (lambda e, bank=ye_banks[bg], half=half, hh=hh, f=f, buf=buf,
                                      ei=ei, pw=pw, ph=ph: e.matmul(
                                    ps[bank][:, half * 256:half * 256 + 256],
                                    w2p(buf, pw)[:, f, hh * 128:(hh + 1) * 128], hep(ph)[:, f, :],
                                    start=(ei == 0 and half == 0 and f == 0 and pw == 0 and ph == 0),
                                    stop=(ei == 8 and half == 1 and f == 3 and pw == 1)))
                                if ei == 8 and f == 3 and half == 1 and last_pass:
                                    ye_last[bg] = S.op("tensor", fn, inc=spe)
                                elif last_all:
                                    wex_pe[buf] = S.op("tensor", fn, inc=spe)
                                else:
                                    S.op("tensor", fn)
                if ei == 8:
                    wex_pe[buf] = ye_last[3]
                e_ye_prev = wex_pe[buf] if ei < 8 else ye_last[3]
                prev_hem1 = e_hm
                if STOP_AT == f"ffn_e{ei}" and g == 0:
                    return finish()
                if STOP_AT is not None and STOP_AT.startswith(f"ffn_e{ei}mim") and g == 0:
                    S.wait("tensor", wex_edge[(0, 8)])
                    for fg in range(2):
                        bkh, feh = BK.get()
                        S.wait("tensor", feh)
                        for half in range(2):
                            f = fg * 2 + half
                            for k in range(8):
                                fn = (lambda e, bkh=bkh, f=f, half=half, k=k, g=g: e.matmul(
                                    ps[bkh][:, half * 256:half * 256 + 256],
                                    w1v(0)[:, k, f * 128:(f + 1) * 128], hcv(g % 2)[:, k, :],
                                    start=(k == 0 and half == 0), stop=(k == 7 and half == 1)))
                                if k == 7 and half == 1:
                                    e_heM = S.op("tensor", fn, inc=spe)
                                else:
                                    S.op("tensor", fn)
                        if STOP_AT == f"ffn_e{ei}mimA":
                            S.wait("vector", e_heM)
                            ed = vop(lambda e, bkh=bkh: e.tensor_copy(sq[:, 0, :], ps[bkh][:, 0:256]))
                            BK.put(bkh, ed)
                            continue
                        S.wait("scalar", e_heM)
                        if fg == 0:
                            S.wait("scalar", prev_hem1)
                            S.wait("vector", e_ye_prev)
                        for half in range(2):
                            f = fg * 2 + half
                            e_t1M = vop(lambda e, bkh=bkh, half=half, f=f: e.activation(
                                out=hem(1)[:, f, :], in_=ps[bkh][:, half * 256:half * 256 + 256],
                                func=AF.Relu), "scalar")
                            if STOP_AT == f"ffn_e{ei}mimB":
                                S.wait("vector", e_t1M)
                                e_hmM = vop(lambda e, bkh=bkh, half=half, f=f: e.tensor_copy(
                                    hem(0)[:, f, :], hem(1)[:, f, :]))
                            else:
                                S.wait("vector", e_t1M)
                                e_hmM = vop(lambda e, bkh=bkh, half=half, f=f: e.tensor_tensor(
                                    hem(0)[:, f, :], hem(1)[:, f, :],
                                    ps[bkh][:, half * 256:half * 256 + 256], ALU.mult))
                        BK.put(bkh, e_hmM)
                    return finish()
                if STOP_AT == f"ffn_e{ei}pad" and g == 0:
                    for fg in range(2):
                        bkh, feh = BK.get()
                        S.wait("tensor", feh)
                        for half in range(2):
                            f = fg * 2 + half
                            for k in range(8):
                                fn = (lambda e, bkh=bkh, f=f, half=half, k=k, g=g: e.matmul(
                                    ps[bkh][:, half * 256:half * 256 + 256],
                                    w1v(0)[:, k, f * 128:(f + 1) * 128], hcv(g % 2)[:, k, :],
                                    start=(k == 0 and half == 0), stop=(k == 7 and half == 1)))
                                if k == 7 and half == 1:
                                    e_dummy = S.op("tensor", fn, inc=spe)
                                else:
                                    S.op("tensor", fn)
                        S.wait("vector", e_dummy)
                        ed = vop(lambda e, bkh=bkh: e.tensor_copy(sq[:, 0, :], ps[bkh][:, 0:256]))
                        BK.put(bkh, ed)
                    return finish()
            h2_free[g] = e_hpe_last
            oe = []
            for bg in range(4):
                S.wait("vector", ye_last[bg])
                for half in range(2):
                    hh = bg * 2 + half
                    ed = vop(lambda e, bank=ye_banks[bg], half=half, hh=hh, g=g: e.scalar_tensor_tensor(
                        out=x2sl[:, g % 2, hh, :],
                        in0=ps[bank][:, half * 256:half * 256 + 256],
                        scalar=dmod[:, 40 + hh:40 + hh + 1],
                        in1=x2sl[:, g % 2, hh, :], op0=ALU.mult, op1=ALU.add))
                    oe.append(ed)
                BK.put(ye_banks[bg], oe[-1])
            e_od = dma("gpsimd", dG,
                       out_t.rearrange("(kc p) t -> p kc t", p=128)[:, :, g * 256:(g + 1) * 256],
                       x2sl[:, g % 2, :, :], waits=(oe[-1],))
            out_dma.append(e_od)
            if g + 2 <= 7:
                load_x2(g + 2, (e_od, P[8 + g + 2]["spill"]))
            if STOP_AT == f"ffn_g{g}":
                return finish()

        S.wait("gpsimd", out_dma[-1])
        S.wait("sync", out_dma[-1])

        S.check()
        blk = st.enter_context(nc.Block())
        for ename in ENGINES:
            section = getattr(blk, ename)
            def emit(eng, _q=tuple(S.q[ename])):
                for item in _q:
                    item(eng)
            section(emit)

    return nc


def kernel(L0_attn_x, L0_attn_Wb, L0_attn_init_A, L0_attn_init_B, L0_attn_ln_gamma,
           L0_attn_ln_beta, L0_attn_Wp, L0_ffn_Ws1, L0_ffn_Ws2, L0_ffn_W1, L0_ffn_W2,
           L0_ffn_Wr, L0_c, L0_wmod):
    x = np.asarray(L0_attn_x, dtype=np.float32)
    B, S_, D = x.shape
    fmT = lambda a: np.ascontiguousarray(np.asarray(a, np.float32).T)

    common = dict(
        wbT=fmT(L0_attn_Wb), wpT=fmT(L0_attn_Wp),
        initA=np.ascontiguousarray(np.asarray(L0_attn_init_A, np.float32)),
        initAT=fmT(L0_attn_init_A),
        initB=np.ascontiguousarray(np.asarray(L0_attn_init_B, np.float32)),
        initBT=fmT(L0_attn_init_B),
        gam=np.ascontiguousarray(np.asarray(L0_attn_ln_gamma, np.float32).reshape(8, 128).T),
        bet=np.ascontiguousarray(np.asarray(L0_attn_ln_beta, np.float32).reshape(8, 128).T),
        wr=np.ascontiguousarray(np.asarray(L0_ffn_Wr, np.float32)),
        wmod=np.ascontiguousarray(np.asarray(L0_wmod, np.float32)),
        wex1=np.ascontiguousarray(np.concatenate(
            [np.asarray(L0_ffn_W1, np.float32), np.asarray(L0_ffn_Ws1, np.float32)[None]], 0)),
        wex2=np.ascontiguousarray(np.concatenate(
            [np.asarray(L0_ffn_W2, np.float32), np.asarray(L0_ffn_Ws2, np.float32)[None]], 0)),
        ones=np.ones((128, 128), dtype=np.float32),
        eye=np.eye(128, dtype=np.float32),
    )
    import ml_dtypes
    sel = np.zeros((8, 1024), dtype=ml_dtypes.bfloat16)
    for e in range(8):
        sel[e, e * 128:(e + 1) * 128] = 1.0
    common["sel"] = sel
    common["onesb"] = np.ones((128, 128), dtype=ml_dtypes.bfloat16)
    cN = np.asarray(L0_c, np.float32)

    in_maps = []
    lr = 0.01 * 2.0
    for core in range(8):
        b, hf = core // 2, core % 2
        xo = fmT(x[b, hf * 2048:(hf + 1) * 2048])
        if hf == 0:
            xf = np.zeros((1024, 2048), dtype=np.float32)
            xxt = np.ascontiguousarray(np.stack([x[b, 2048], x[b, 0]], 1))
        else:
            xf = fmT(x[b, 0:2048])
            xxt = np.ascontiguousarray(np.stack([np.zeros(1024, np.float32), x[b, 2048]], 1))
        consts = np.zeros((128, 16), dtype=np.float32)
        for col, v in {C_ZERO: 0.0, C_LNEPS: 1e-5, C_CLIPEPS: 1e-6, C_LR2: lr * lr,
                       C_N2LR: -2.0 * lr, C_NLR: -lr, C_INV1024: 1.0 / 1024.0,
                       C_QUART: 0.25, C_ONE: 1.0, C_FHMASK: float(hf == 1),
                       C_XTRAMASK: float(hf == 0), C_TWO: 2.0}.items():
            consts[:, col] = v
        in_maps.append(dict(common, xo=xo, xxt=xxt, xf=xf, consts=consts,
                            cfm=np.ascontiguousarray(cN[b].reshape(8, 128).T)))

    nc = build_program()
    res = run_bass_kernel_spmd(nc, in_maps, core_ids=list(range(8)))
    out = np.empty((B, S_, D), dtype=np.float32)
    for core in range(8):
        b, hf = core // 2, core % 2
        out[b, hf * 2048:(hf + 1) * 2048] = res.results[core]["out"].T
    return out
